# version 03
# baseline (speedup 1.0000x reference)
import numpy as np
import ml_dtypes
from contextlib import ExitStack

import concourse.bass as bass
import concourse.mybir as mybir
from concourse.bass_utils import run_bass_kernel_spmd

AF = mybir.ActivationFunctionType
ALU = mybir.AluOpType
AXX = mybir.AxisListType.X
F32 = mybir.dt.float32
BF16 = mybir.dt.bfloat16

D = 1024
DL = 256
NH = 16
HD = 64
E = 8
FF = 2048
CH = 512
HG = 4
EPS_LN = 1e-5
ENGINES = ("sync", "tensor", "scalar", "vector", "gpsimd")


class Ctr:
    def __init__(self, h):
        self.h = h
        self.n = 0


class Tok:
    __slots__ = ("ctr", "val", "k")

    def __init__(self, ctr, k, val):
        self.ctr = ctr
        self.k = k
        self.val = val


class Sched:

    def __init__(self, nc):
        self.nc = nc
        self.items = []

    def op(self, eng, fn, deps=(), inc=None, inck=1):
        deps = tuple(d for d in deps if d is not None)
        tok = None
        if inc is not None:
            inc.n += inck
            tok = Tok(inc, inck, inc.n)
        self.items.append((eng, fn, deps, tok))
        return tok

    def drain(self, ctr, eng="gpsimd"):
        v = ctr.n
        if v > 0:
            self.op(eng, lambda e, h=ctr.h, v=v: e.wait_ge(h, v))

    def run(self, name=None):
        with self.nc.Block(name) as blk:
            for eng in ENGINES:
                items = [it for it in self.items if it[0] == eng]
                if not items:
                    continue

                def body(e, items=items):
                    last = {}
                    for (_, fn, deps, tok) in items:
                        for d in deps:
                            if last.get(id(d.ctr), -1) < d.val:
                                e.wait_ge(d.ctr.h, d.val)
                                last[id(d.ctr)] = d.val
                        inst = fn(e)
                        if tok is not None:
                            inst.then_inc(tok.ctr.h, tok.k)
                getattr(blk, eng)(body)
        self.items = []


class Env:
    pass


def _stream_pieces(env, pieces, stg, dma_ctr=None, cast_eng="vector"):
    nc = env.nc
    sch = Sched(nc)
    toks = []
    cast_ctr = env.s_dv if cast_eng == "vector" else env.s_pl
    for i, (dst, c0, ncol, src) in enumerate(pieces):
        deps = [toks[i - 2]] if i >= 2 else []
        dt_ = sch.op("sync", lambda sy, i=i, ncol=ncol, src=src:
                     sy.dma_start(out=stg[i % 2][:, 0:ncol], in_=src),
                     deps=deps, inc=env.s_stg[i % 2], inck=16)
        ct = sch.op(cast_eng, lambda e, i=i, dst=dst, c0=c0, ncol=ncol:
                    e.tensor_copy(out=dst[:, c0:c0 + ncol],
                                  in_=stg[i % 2][:, 0:ncol]),
                    deps=[dt_], inc=cast_ctr)
        toks.append(ct)
    sch.run("stream")
    return toks


def _proj_phase(env, name, groups, nk, lhs_fn, rhs_fn, evac_fn, psum_n=CH):
    nc = env.nc
    toks = {}
    with nc.psum_tensor(f"{name}p0", [128, psum_n], F32) as p0, \
         nc.psum_tensor(f"{name}p1", [128, psum_n], F32) as p1:
        ps = [p0, p1]
        sch = Sched(nc)
        for gi, g in enumerate(groups):
            deps = [toks[groups[gi - 2]]] if gi >= 2 else []

            def mk_mm(g=g, gi=gi):
                def f(te):
                    for k in range(nk):
                        inst = te.matmul(ps[gi % 2][:], lhs_fn(g, k),
                                         rhs_fn(g, k),
                                         start=(k == 0), stop=(k == nk - 1))
                    return inst
                return f
            mt = sch.op("tensor", mk_mm(), deps=deps, inc=env.s_pe)

            def mk_ev(g=g, gi=gi):
                def f(ve):
                    return evac_fn(ve, g, ps[gi % 2])
                return f
            toks[g] = sch.op("vector", mk_ev(), deps=[mt], inc=env.s_dv)
        sch.run(name)
    return toks


def _p0(env, wdkv_bf, wuk_bf, wuv_bf, wuq_bf, xT_bf):
    nc = env.nc
    T = env.T
    with nc.sbuf_tensor("wstg0", [128, D], F32) as wstg0, \
         nc.sbuf_tensor("wstg1", [128, D], F32) as wstg1, \
         nc.sbuf_tensor("xstg0", [128, 1024], F32) as xstg0, \
         nc.sbuf_tensor("xstg1", [128, 1024], F32) as xstg1:
        wstg = [wstg0, wstg1]
        xstg = [xstg0, xstg1]
        sch = Sched(nc)
        ctok = None
        for dst, src in ((env.ident, env.c_ident), (env.sel16, env.c_sel16),
                         (env.sel8h, env.c_sel8h), (env.sel8e, env.c_sel8e)):
            ctok = sch.op("sync", lambda sy, dst=dst, src=src:
                          sy.dma_start(out=dst[:], in_=src),
                          inc=env.s_cst, inck=16)
        env.consts_tok = ctok
        sch.op("vector", lambda ve: ve.memset(env.ones_f1[:], 1.0))
        sch.op("vector", lambda ve: ve.memset(env.ones_fc[:], 1.0))
        sch.op("vector", lambda ve: ve.memset(env.eps_t[:], EPS_LN))
        sch.op("vector", lambda ve: ve.memset(env.qmax16[:], 0.0))
        wpieces = []
        for kd in range(8):
            wpieces.append((wdkv_bf, kd * DL, DL,
                            env.wdkv[kd * 128:(kd + 1) * 128, :]))
        for kl in range(2):
            wpieces.append((wuk_bf, kl * D, D,
                            env.wuk[kl * 128:(kl + 1) * 128, :]))
        for kl in range(2):
            wpieces.append((wuv_bf, kl * D, D,
                            env.wuv[kl * 128:(kl + 1) * 128, :]))
        for kd in range(8):
            wpieces.append((wuq_bf, kd * D, D,
                            env.wuq[kd * 128:(kd + 1) * 128, :]))
        wtoks = []
        for i, (dst, c0, ncol, src) in enumerate(wpieces):
            deps = [wtoks[i - 2]] if i >= 2 else []
            dt_ = sch.op("sync", lambda sy, i=i, ncol=ncol, src=src:
                         sy.dma_start(out=wstg[i % 2][:, 0:ncol], in_=src),
                         deps=deps, inc=env.s_stg[i % 2], inck=16)
            ct = sch.op("vector", lambda ve, i=i, dst=dst, c0=c0, ncol=ncol:
                        ve.tensor_copy(out=dst[:, c0:c0 + ncol],
                                       in_=wstg[i % 2][:, 0:ncol]),
                        deps=[dt_], inc=env.s_dv)
            wtoks.append(ct)
        xtoks = []
        xpieces = [(kd, cp) for kd in range(8) for cp in range(T // 1024)]
        for i, (kd, cp) in enumerate(xpieces):
            deps = [xtoks[i - 2]] if i >= 2 else []
            dt_ = sch.op("sync", lambda sy, i=i, kd=kd, cp=cp:
                         sy.dma_start(out=xstg[i % 2][:],
                                      in_=env.xT[kd * 128:(kd + 1) * 128,
                                                 cp * 1024:(cp + 1) * 1024]),
                         deps=deps, inc=env.s_xstg[i % 2], inck=16)
            ct = sch.op("gpsimd", lambda gp, i=i, kd=kd, cp=cp:
                        gp.tensor_copy(
                            out=xT_bf[:, kd * T + cp * 1024:
                                      kd * T + (cp + 1) * 1024],
                            in_=xstg[i % 2][:]),
                        deps=[dt_], inc=env.s_pl)
            xtoks.append(ct)
        sch.run("p0")


def _qnorms(env):
    nc = env.nc
    NCH = env.NCH
    with nc.sbuf_tensor("qsq0", [128, CH], BF16) as qsq0, \
         nc.sbuf_tensor("qsq1", [128, CH], BF16) as qsq1, \
         nc.psum_tensor("psn16", [16, CH], F32) as psn16:
        qsq = [qsq0, qsq1]
        sch = Sched(nc)
        sq_toks = []
        prev_red = None
        prev_mx = None
        i = 0
        for chq in range(NCH):
            grp_tok = None
            for kd in range(8):
                deps = [sq_toks[i - 2][1]] if i >= 2 else []
                st_ = sch.op("scalar", lambda sc, i=i, kd=kd, chq=chq:
                             sc.activation(
                                 out=qsq[i % 2][:],
                                 in_=env.qT[kd][:, chq * CH:(chq + 1) * CH],
                                 func=AF.Square),
                             deps=deps, inc=env.s_ac)
                mdeps = [st_, env.consts_tok]
                if kd == 0 and prev_red is not None:
                    mdeps.append(prev_red)
                mt = sch.op("tensor", lambda te, i=i, kd=kd:
                            te.matmul(psn16[:],
                                      env.sel16[:, kd * 16:(kd + 1) * 16],
                                      qsq[i % 2][:],
                                      start=(kd == 0), stop=(kd == 7)),
                            deps=mdeps, inc=env.s_pe)
                sq_toks.append((st_, mt))
                grp_tok = mt
                i += 1
            rd = sch.op("vector", lambda ve: ve.reduce_max(
                out=env.nrm_tmp[:], in_=psn16[:], axis=AXX),
                deps=[grp_tok] + ([prev_mx] if prev_mx else []), inc=env.s_dv)
            mx = sch.op("vector", lambda ve: ve.tensor_max(
                out=env.qmax16[:], in0=env.qmax16[:], in1=env.nrm_tmp[:]),
                deps=[rd], inc=env.s_dv)
            prev_red = rd
            prev_mx = mx
        sch.run("p2n")


def _attn_group(env, g, wuk_bf, wuv_bf):
    nc = env.nc
    T, NCH, NKT = env.T, env.NCH, env.NKT
    VW = HG * 65
    with nc.sbuf_tensor(f"kTg{g}", [128, 2 * T], BF16) as kTg, \
         nc.sbuf_tensor(f"vaug{g}", [128, NKT * VW], BF16) as vaug, \
         nc.sbuf_tensor(f"bias4_{g}", [128, 4], F32) as bias4, \
         nc.sbuf_tensor(f"kmax4_{g}", [4, 1], F32) as kmax4, \
         nc.sbuf_tensor(f"ktmp4_{g}", [4, 1], F32) as ktmp4, \
         nc.sbuf_tensor(f"brow_{g}", [1, 4], F32) as brow, \
         nc.sbuf_tensor(f"probT_{g}", [128, 4 * CH], BF16) as probT, \
         nc.sbuf_tensor(f"probM_{g}", [128, 2 * CH], BF16) as probM, \
         nc.sbuf_tensor(f"recipS_{g}", [1, 2 * CH], F32) as recipS, \
         nc.sbuf_tensor(f"rbS_{g}", [64, 2 * CH], F32) as rbS, \
         nc.sbuf_tensor(f"ksq0_{g}", [128, CH], BF16) as ksq0, \
         nc.sbuf_tensor(f"ksq1_{g}", [128, CH], BF16) as ksq1, \
         nc.sbuf_tensor(f"ctxs_{g}", [64, 2 * CH], BF16) as ctxs:

        groups_k = [(p, ch) for p in range(2) for ch in range(NCH)]
        ktoks = _proj_phase(
            env, f"kT{g}", groups_k, 2,
            lambda gk, k: wuk_bf[:, k * D + g * 256 + gk[0] * 128:
                                 k * D + g * 256 + (gk[0] + 1) * 128],
            lambda gk, k: env.c_kvT[k][:, gk[1] * CH:(gk[1] + 1) * CH],
            lambda ve, gk, ps: ve.tensor_copy(
                out=kTg[:, gk[0] * T + gk[1] * CH:
                        gk[0] * T + (gk[1] + 1) * CH],
                in_=ps[:]),
        )

        sch = Sched(nc)

        def ones_cols(gp):
            inst = None
            for kt in range(NKT):
                for hh in range(HG):
                    inst = gp.memset(vaug[:, kt * VW + hh * 65 + 64:
                                          kt * VW + hh * 65 + 65], 1.0)
            return inst
        vones_tok = sch.op("gpsimd", ones_cols, inc=env.s_pl)
        sch.run(f"vones{g}")

        def v_evac(ve, kt, ps):
            inst = None
            for hh in range(HG):
                inst = ve.tensor_copy(
                    out=vaug[:, kt * VW + hh * 65: kt * VW + hh * 65 + 64],
                    in_=ps[:, hh * 64:(hh + 1) * 64])
            return inst
        vtoks = _proj_phase(
            env, f"v{g}", list(range(NKT)), 2,
            lambda kt, k: env.c_kvT[k][:, kt * 128:(kt + 1) * 128],
            lambda kt, k: wuv_bf[:, k * D + g * 256: k * D + (g + 1) * 256],
            v_evac, psum_n=HG * 64,
        )

        bias_tok = None
        with nc.psum_tensor(f"psn4_{g}", [4, CH], F32) as psn4, \
             nc.psum_tensor(f"psb4_{g}", [128, 4], F32) as psb4:
            ksq = [ksq0, ksq1]
            sch = Sched(nc)
            km0 = sch.op("vector", lambda ve: ve.memset(kmax4[:], 0.0),
                         inc=env.s_dv)
            i = 0
            prev_red = None
            prev_mx = km0
            sq_list = []
            for chq in range(NCH):
                grp_tok = None
                for p in range(2):
                    deps = [sq_list[i - 2][1]] if i >= 2 else []
                    st_ = sch.op(
                        "scalar", lambda sc, i=i, p=p, chq=chq:
                        sc.activation(
                            out=ksq[i % 2][:],
                            in_=kTg[:, p * T + chq * CH:
                                    p * T + (chq + 1) * CH],
                            func=AF.Square),
                        deps=deps + [ktoks[(p, chq)]], inc=env.s_ac)
                    mdeps = [st_, env.consts_tok]
                    if p == 0 and prev_red is not None:
                        mdeps.append(prev_red)
                    mt = sch.op("tensor", lambda te, i=i, p=p:
                                te.matmul(psn4[:],
                                          env.sel8h[:, p * 4:(p + 1) * 4],
                                          ksq[i % 2][:],
                                          start=(p == 0), stop=(p == 1)),
                                deps=mdeps, inc=env.s_pe)
                    sq_list.append((st_, mt))
                    grp_tok = mt
                    i += 1
                rd = sch.op("vector", lambda ve: ve.reduce_max(
                    out=ktmp4[:], in_=psn4[:], axis=AXX),
                    deps=[grp_tok, prev_mx], inc=env.s_dv)
                mx = sch.op("vector", lambda ve: ve.tensor_max(
                    out=kmax4[:], in0=kmax4[:], in1=ktmp4[:]),
                    deps=[rd], inc=env.s_dv)
                prev_red = rd
                prev_mx = mx
            qg = sch.op("sync", lambda sy: sy.dma_start(
                out=env.qmaxG[:], in_=env.qmax16[g * 4:(g + 1) * 4, :],
                allow_slow_non_contiguous=True), inc=env.s_msc, inck=16)
            t1 = sch.op("vector", lambda ve: ve.tensor_mul(
                out=ktmp4[:], in0=kmax4[:], in1=env.qmaxG[:]),
                deps=[prev_mx, prev_red, qg], inc=env.s_dv)
            t2 = sch.op("scalar", lambda sc: sc.activation(
                out=ktmp4[:], in_=ktmp4[:], func=AF.Sqrt),
                deps=[t1], inc=env.s_ac)
            t3 = sch.op("vector", lambda ve: ve.tensor_scalar_mul(
                out=ktmp4[:], in0=ktmp4[:], scalar1=-0.125),
                deps=[t2], inc=env.s_dv)
            t4 = sch.op("sync", lambda sy: sy.dma_start(
                out=brow[:], in_=ktmp4[:], allow_slow_non_contiguous=True),
                deps=[t3], inc=env.s_msc, inck=16)
            t5 = sch.op("tensor", lambda te: te.matmul(
                psb4[:], env.ones_f1[0:1, :], brow[:], start=True, stop=True),
                deps=[t4], inc=env.s_pe)
            bias_tok = sch.op("vector", lambda ve: ve.tensor_copy(
                out=bias4[:], in_=psb4[:]), deps=[t5], inc=env.s_dv)
            sch.run(f"knrm{g}")

            _attn_loop(env, g, kTg, vaug, bias4, probT, probM, recipS, rbS,
                       ctxs, ktoks, vtoks, vones_tok, bias_tok)


def _attn_loop(env, g, kTg, vaug, bias4, probT, probM, recipS, rbS,
               ctxs, ktoks, vtoks, vones_tok, bias_tok):
    nc = env.nc
    T, NCH, NKT = env.T, env.NCH, env.NKT
    VW = HG * 65
    jobs = [(hh, qc) for hh in range(HG) for qc in range(NCH)]
    with nc.psum_tensor(f"ps_s0_{g}", [128, CH], F32) as ps_s0, \
         nc.psum_tensor(f"ps_s1_{g}", [128, CH], F32) as ps_s1, \
         nc.psum_tensor(f"ps_c0_{g}", [65, CH], F32) as ps_c0, \
         nc.psum_tensor(f"ps_c1_{g}", [65, CH], F32) as ps_c1, \
         nc.psum_tensor(f"ps_r0_{g}", [64, CH], F32) as ps_r0, \
         nc.psum_tensor(f"ps_r1_{g}", [64, CH], F32) as ps_r1:
        ps_s = [ps_s0, ps_s1]
        ps_c = [ps_c0, ps_c1]
        ps_r = [ps_r0, ps_r1]
        sch = Sched(nc)
        RING = 4
        slot_free = [None] * RING
        mslot_free = [None, None]
        exp_toks = {}
        mask_toks = {}
        mask_slot = {}
        pv_toks = {}
        job_dve = {}
        job_bc = {}
        gt = 0
        d = 0
        for j, (hh, qc) in enumerate(jobs):
            nkt = min(NKT, 4 * (qc + 1))
            h_glob = g * HG + hh
            pt = hh // 2
            rb = 64 * (hh % 2)
            qtile = h_glob // 2
            qb = 64 * (h_glob % 2)
            base_gt = gt
            for i in range(nkt + 1):
                if i < nkt:
                    t_this = base_gt + i
                    deps = [exp_toks.get(t_this - 2), ktoks[(pt, i // 4)]]
                    sct = sch.op(
                        "tensor",
                        lambda te, pt=pt, rb=rb, i=i, qc=qc, qtile=qtile,
                        qb=qb, t_this=t_this:
                        te.matmul(
                            ps_s[t_this % 2][:],
                            kTg[rb:rb + 64,
                                pt * T + i * 128: pt * T + (i + 1) * 128],
                            env.qT[qtile][qb:qb + 64, qc * CH:(qc + 1) * CH],
                            start=True, stop=True),
                        deps=deps, inc=env.s_pe)
                    diag = i >= nkt - 4
                    edeps = [sct, bias_tok, slot_free[t_this % RING]]
                    et = sch.op(
                        "scalar", lambda sc, t_this=t_this, hh=hh:
                        sc.activation(
                            out=probT[:, (t_this % RING) * CH:
                                      (t_this % RING + 1) * CH],
                            in_=ps_s[t_this % 2][:], func=AF.Exp,
                            scale=0.125, bias=bias4[:, hh:hh + 1]),
                        deps=edeps, inc=env.s_ac)
                    exp_toks[t_this] = et
                    if diag:
                        mdeps = [et, mslot_free[d % 2]]
                        mt = sch.op(
                            "gpsimd",
                            lambda gp, t_this=t_this, d=d, qc=qc, i=i:
                            gp.affine_select(
                                out=probM[:, (d % 2) * CH:(d % 2 + 1) * CH],
                                in_=probT[:, (t_this % RING) * CH:
                                          (t_this % RING + 1) * CH],
                                pattern=[[1, CH]], compare_op=ALU.is_ge,
                                fill=0.0, base=qc * CH - i * 128,
                                channel_multiplier=-1),
                            deps=mdeps, inc=env.s_pl)
                        mask_toks[t_this] = mt
                        mask_slot[t_this] = d % 2
                        slot_free[t_this % RING] = mt
                        d += 1
                if i >= 1:
                    ii = i - 1
                    t_pv = base_gt + ii
                    diag = ii >= nkt - 4
                    deps = [vtoks[ii], vones_tok]
                    if diag:
                        deps.append(mask_toks[t_pv])
                        rhs_slot = ('m', mask_slot[t_pv])
                    else:
                        deps.append(exp_toks[t_pv])
                        rhs_slot = ('t', t_pv % RING)
                    if ii == 0 and j >= 2:
                        deps.append(job_dve[j - 2][2])

                    def mk_pv(ii=ii, j=j, hh=hh, nkt=nkt, rhs_slot=rhs_slot):
                        def f(te):
                            if rhs_slot[0] == 'm':
                                rhs = probM[:, rhs_slot[1] * CH:
                                            (rhs_slot[1] + 1) * CH]
                            else:
                                rhs = probT[:, rhs_slot[1] * CH:
                                            (rhs_slot[1] + 1) * CH]
                            return te.matmul(
                                ps_c[j % 2][:],
                                vaug[:, ii * VW + hh * 65:
                                     ii * VW + hh * 65 + 65],
                                rhs, start=(ii == 0), stop=(ii == nkt - 1))
                        return f
                    pt_ = sch.op("tensor", mk_pv(), deps=deps, inc=env.s_pe)
                    pv_toks[t_pv] = pt_
                    if diag:
                        mslot_free[mask_slot[t_pv]] = pt_
                    else:
                        slot_free[t_pv % RING] = pt_
            gt = base_gt + nkt
            last_pv = pv_toks[gt - 1]
            rdeps = [last_pv]
            if j >= 2:
                rdeps.append(job_bc[j - 2])
            r1 = sch.op("vector", lambda ve, j=j:
                        ve.reciprocal(
                            out=recipS[0:1, (j % 2) * CH:(j % 2 + 1) * CH],
                            in_=ps_c[j % 2][64:65, :]),
                        deps=rdeps, inc=env.s_dv)
            bdeps = [r1]
            if j >= 2:
                bdeps.append(job_dve[j - 2][1])
            b1 = sch.op("tensor", lambda te, j=j:
                        te.matmul(ps_r[j % 2][:], env.ones_f1[0:1, 0:64],
                                  recipS[0:1, (j % 2) * CH:(j % 2 + 1) * CH],
                                  start=True, stop=True),
                        deps=bdeps, inc=env.s_pe)
            job_bc[j] = b1
            e1 = sch.op("vector", lambda ve, j=j:
                        ve.tensor_copy(
                            out=rbS[:, (j % 2) * CH:(j % 2 + 1) * CH],
                            in_=ps_r[j % 2][:]),
                        deps=[b1], inc=env.s_dv)
            cdeps = [e1]
            if j >= 2:
                cdeps.append(job_dve[j - 2][3])
            c1 = sch.op("vector",
                        lambda ve, j=j:
                        ve.tensor_mul(
                            out=ctxs[:, (j % 2) * CH:(j % 2 + 1) * CH],
                            in0=ps_c[j % 2][0:64, :],
                            in1=rbS[:, (j % 2) * CH:(j % 2 + 1) * CH]),
                        deps=cdeps, inc=env.s_dv)
            cd = sch.op("gpsimd",
                        lambda gp, j=j, h_glob=h_glob, qc=qc:
                        gp.dma_start(
                            out=env.ctxh[h_glob * 64:(h_glob + 1) * 64,
                                         qc * CH:(qc + 1) * CH],
                            in_=ctxs[:, (j % 2) * CH:(j % 2 + 1) * CH]),
                        deps=[c1], inc=env.s_cd[j % 2], inck=16)
            job_dve[j] = (r1, e1, c1, cd)
        sch.drain(env.s_cd[0])
        sch.drain(env.s_cd[1])
        sch.run(f"attn{g}")


def _p6(env, wo_bf, rw_bf):
    nc = env.nc
    T, NCH = env.T, env.NCH
    with nc.sbuf_tensor("ctxr", [128, 16 * CH], BF16) as ctxr, \
         nc.sbuf_tensor("xst", [128, 16 * CH], F32) as xst, \
         nc.sbuf_tensor("resid", [128, 8 * CH], F32) as resid, \
         nc.sbuf_tensor("rsq", [128, 2 * CH], F32) as rsq, \
         nc.sbuf_tensor("x1bs", [128, 8 * CH], BF16) as x1bs, \
         nc.sbuf_tensor("ab_sb", [128, 2 * CH], F32) as ab_sb, \
         nc.sbuf_tensor("rows", [1, 6 * CH], F32) as rows, \
         nc.sbuf_tensor("cwtile", [128, E], F32) as cwtile, \
         nc.sbuf_tensor("cwmask", [128, E], F32) as cwmask, \
         nc.sbuf_tensor("cwrow", [128, 5], F32) as cwrow, \
         nc.psum_tensor("ps_o0", [128, CH], F32) as ps_o0, \
         nc.psum_tensor("ps_o1", [128, CH], F32) as ps_o1, \
         nc.psum_tensor("ps_sum", [1, CH], F32) as ps_sum, \
         nc.psum_tensor("ps_ssq", [1, CH], F32) as ps_ssq, \
         nc.psum_tensor("ps_A", [128, CH], F32) as ps_A, \
         nc.psum_tensor("ps_B", [128, CH], F32) as ps_B, \
         nc.psum_tensor("ps_lg", [128, E], F32) as ps_lg, \
         nc.psum_tensor("ps_ct", [8, 128], F32) as ps_ct:
        ps_o = [ps_o0, ps_o1]
        sch = Sched(nc)
        prev = {}
        for chq in range(NCH):
            st = {}
            ctoks = []
            for kd in range(8):
                deps = []
                if chq >= 2:
                    deps.append(prev[chq - 2]['wo'][7])
                ctoks.append(sch.op(
                    "sync", lambda sy, kd=kd, chq=chq:
                    sy.dma_start(
                        out=ctxr[:, ((chq % 2) * 8 + kd) * CH:
                                 ((chq % 2) * 8 + kd + 1) * CH],
                        in_=env.ctxh[kd * 128:(kd + 1) * 128,
                                     chq * CH:(chq + 1) * CH]),
                    deps=deps, inc=env.s_cx[chq % 2], inck=16))
            xtoks = []
            for m in range(8):
                deps = []
                if chq >= 2:
                    deps.append(prev[chq - 2]['ra'][m])
                xtoks.append(sch.op(
                    "sync", lambda sy, m=m, chq=chq:
                    sy.dma_start(
                        out=xst[:, ((chq % 2) * 8 + m) * CH:
                                ((chq % 2) * 8 + m + 1) * CH],
                        in_=env.xT[m * 128:(m + 1) * 128,
                                   chq * CH:(chq + 1) * CH]),
                    deps=deps, inc=env.s_xs[chq % 2], inck=16))
            ra = [None] * 8
            sq = [None] * 8
            sums = [None] * 8
            ssqs = [None] * 8
            wo_toks = [None] * 8
            for m in range(8):
                deps = [ctoks[-1]]
                if chq * 8 + m >= 2:
                    pm = chq * 8 + m - 2
                    deps.append(ra[pm % 8] if pm // 8 == chq
                                else prev[pm // 8]['ra'][pm % 8])
                wo_toks[m] = sch.op(
                    "tensor", lambda te, m=m, chq=chq:
                    [te.matmul(ps_o[m % 2][:],
                               wo_bf[:, kd * D + m * 128:
                                     kd * D + (m + 1) * 128],
                               ctxr[:, ((chq % 2) * 8 + kd) * CH:
                                    ((chq % 2) * 8 + kd + 1) * CH],
                               start=(kd == 0), stop=(kd == 7))
                     for kd in range(8)][-1],
                    deps=deps, inc=env.s_pe)
                rdeps = [wo_toks[m], xtoks[-1]]
                if chq >= 1:
                    rdeps.append(prev[chq - 1]['x1bdma'][7])
                    rdeps.append(prev[chq - 1]['sum'][m])
                    rdeps.append(prev[chq - 1]['lg_last'])
                ra[m] = sch.op(
                    "vector", lambda ve, m=m, chq=chq:
                    ve.tensor_add(
                        out=resid[:, m * CH:(m + 1) * CH],
                        in0=ps_o[m % 2][:],
                        in1=xst[:, ((chq % 2) * 8 + m) * CH:
                                ((chq % 2) * 8 + m + 1) * CH]),
                    deps=rdeps, inc=env.s_dv)
                sdeps = [ra[m]]
                if m >= 2:
                    sdeps.append(ssqs[m - 2])
                elif chq >= 1:
                    sdeps.append(prev[chq - 1]['ssq'][6 + m])
                sq[m] = sch.op(
                    "scalar", lambda sc, m=m:
                    sc.activation(out=rsq[:, (m % 2) * CH:(m % 2 + 1) * CH],
                                  in_=resid[:, m * CH:(m + 1) * CH],
                                  func=AF.Square),
                    deps=sdeps, inc=env.s_ac)
                sums[m] = sch.op(
                    "tensor", lambda te, m=m:
                    te.matmul(ps_sum[:], env.ones_fc[:],
                              resid[:, m * CH:(m + 1) * CH],
                              start=(m == 0), stop=(m == 7)),
                    deps=[ra[m]] + ([prev[chq - 1]['rows0']]
                                    if (m == 0 and chq >= 1) else []),
                    inc=env.s_pe)
                ssqs[m] = sch.op(
                    "tensor", lambda te, m=m:
                    te.matmul(ps_ssq[:], env.ones_fc[:],
                              rsq[:, (m % 2) * CH:(m % 2 + 1) * CH],
                              start=(m == 0), stop=(m == 7)),
                    deps=[sq[m]] + ([prev[chq - 1]['rows1']]
                                    if (m == 0 and chq >= 1) else []),
                    inc=env.s_pe)
            c1 = sch.op("vector", lambda ve: ve.tensor_scalar_mul(
                out=rows[0:1, 0:CH], in0=ps_sum[:], scalar1=1.0 / D),
                deps=[sums[7]], inc=env.s_dv)
            c2 = sch.op("vector", lambda ve: ve.tensor_mul(
                out=rows[0:1, CH:2 * CH], in0=rows[0:1, 0:CH],
                in1=rows[0:1, 0:CH]), deps=[c1], inc=env.s_dv)
            c3 = sch.op("vector", lambda ve: ve.tensor_scalar_mul(
                out=rows[0:1, 4 * CH:5 * CH], in0=ps_ssq[:],
                scalar1=1.0 / D), deps=[ssqs[7]], inc=env.s_dv)
            c4 = sch.op("vector", lambda ve: ve.tensor_sub(
                out=rows[0:1, CH:2 * CH], in0=rows[0:1, 4 * CH:5 * CH],
                in1=rows[0:1, CH:2 * CH]), deps=[c2, c3], inc=env.s_dv)
            c5 = sch.op("scalar", lambda sc: sc.activation(
                out=rows[0:1, 5 * CH:6 * CH], in_=rows[0:1, CH:2 * CH],
                func=AF.Sqrt, bias=env.eps_t[:]), deps=[c4], inc=env.s_ac)
            c6 = sch.op("vector", lambda ve: ve.reciprocal(
                out=rows[0:1, 2 * CH:3 * CH], in_=rows[0:1, 5 * CH:6 * CH]),
                deps=[c5], inc=env.s_dv)
            c7 = sch.op("vector", lambda ve: ve.tensor_mul(
                out=rows[0:1, 3 * CH:4 * CH], in0=rows[0:1, 0:CH],
                in1=rows[0:1, 2 * CH:3 * CH]), deps=[c6], inc=env.s_dv)
            c8 = sch.op("vector", lambda ve: ve.tensor_scalar_mul(
                out=rows[0:1, 3 * CH:4 * CH], in0=rows[0:1, 3 * CH:4 * CH],
                scalar1=-1.0), deps=[c7], inc=env.s_dv)
            st['rows0'] = c1
            st['rows1'] = c3
            bdeps_extra = [prev[chq - 1]['abev']] if chq >= 1 else []
            bA = sch.op("tensor", lambda te: te.matmul(
                ps_A[:], env.ones_f1[0:1, :], rows[0:1, 2 * CH:3 * CH],
                start=True, stop=True), deps=[c6] + bdeps_extra, inc=env.s_pe)
            bB = sch.op("tensor", lambda te: te.matmul(
                ps_B[:], env.ones_f1[0:1, :], rows[0:1, 3 * CH:4 * CH],
                start=True, stop=True), deps=[c8] + bdeps_extra, inc=env.s_pe)
            evdeps = [bA, bB]
            if chq >= 1:
                evdeps.append(prev[chq - 1]['norm_last'])
            abev = sch.op("vector", lambda ve: [
                ve.tensor_copy(out=ab_sb[:, 0:CH], in_=ps_A[:]),
                ve.tensor_copy(out=ab_sb[:, CH:2 * CH], in_=ps_B[:])][-1],
                deps=evdeps, inc=env.s_dv)
            st['abev'] = abev
            norm_toks = [None] * 8
            cast2 = [None] * 8
            for m in range(8):
                n1 = sch.op("vector", lambda ve, m=m: ve.tensor_mul(
                    out=rsq[:, (m % 2) * CH:(m % 2 + 1) * CH],
                    in0=resid[:, m * CH:(m + 1) * CH], in1=ab_sb[:, 0:CH]),
                    deps=[abev, sq[m], sums[m], ssqs[m]], inc=env.s_dv)
                n2 = sch.op("vector", lambda ve, m=m: ve.tensor_add(
                    out=resid[:, m * CH:(m + 1) * CH],
                    in0=rsq[:, (m % 2) * CH:(m % 2 + 1) * CH],
                    in1=ab_sb[:, CH:2 * CH]), deps=[n1], inc=env.s_dv)
                n3 = sch.op("vector", lambda ve, m=m: ve.tensor_copy(
                    out=x1bs[:, m * CH:(m + 1) * CH],
                    in_=resid[:, m * CH:(m + 1) * CH]), deps=[n2],
                    inc=env.s_dv)
                norm_toks[m] = n2
                cast2[m] = n3
            st['norm_last'] = cast2[7]
            st['ra'] = ra
            st['sum'] = sums
            st['ssq'] = ssqs
            x1fdma = [None] * 8
            x1bdma = [None] * 8
            for m in range(8):
                x1fdma[m] = sch.op(
                    "gpsimd", lambda gp, m=m, chq=chq:
                    gp.dma_start(out=env.x1f[m * 128:(m + 1) * 128,
                                             chq * CH:(chq + 1) * CH],
                                 in_=resid[:, m * CH:(m + 1) * CH]),
                    deps=[norm_toks[m]], inc=env.s_xd[chq % 2], inck=16)
                x1bdma[m] = sch.op(
                    "gpsimd", lambda gp, m=m, chq=chq:
                    gp.dma_start(out=env.x1b[m * 128:(m + 1) * 128,
                                             chq * CH:(chq + 1) * CH],
                                 in_=x1bs[:, m * CH:(m + 1) * CH]),
                    deps=[cast2[m]], inc=env.s_xd[chq % 2], inck=16)
            st['x1fdma'] = x1fdma
            st['x1bdma'] = x1bdma
            st['wo'] = wo_toks
            lg_tok = None
            for tt in range(4):
                ldeps = [cast2[kd] for kd in range(8)]
                if tt >= 1:
                    ldeps += [st[f'exp{tt-1}'], st[f'm1red{tt-1}']]
                elif chq >= 1:
                    ldeps += [prev[chq - 1]['exp3'], prev[chq - 1]['m1red3']]
                lg_tok = sch.op(
                    "tensor", lambda te, tt=tt:
                    [te.matmul(ps_lg[:],
                               x1bs[:, kd * CH + tt * 128:
                                    kd * CH + (tt + 1) * 128],
                               rw_bf[:, kd * E:(kd + 1) * E],
                               start=(kd == 0), stop=(kd == 7))
                     for kd in range(8)][-1],
                    deps=ldeps, inc=env.s_pe)
                m1 = sch.op("vector", lambda ve: ve.reduce_max(
                    out=cwrow[:, 0:1], in_=ps_lg[:], axis=AXX),
                    deps=[lg_tok], inc=env.s_dv)
                m1n = sch.op("vector", lambda ve: ve.tensor_scalar_mul(
                    out=cwrow[:, 0:1], in0=cwrow[:, 0:1], scalar1=-1.0),
                    deps=[m1], inc=env.s_dv)
                st[f'm1red{tt}'] = m1n
                exdeps = [m1n]
                if tt >= 1:
                    exdeps.append(st[f'cwev{tt-1}'])
                elif chq >= 1:
                    exdeps.append(prev[chq - 1]['cwev3'])
                ex = sch.op("scalar", lambda sc: sc.activation(
                    out=cwtile[:], in_=ps_lg[:], func=AF.Exp,
                    bias=cwrow[:, 0:1]), deps=exdeps, inc=env.s_ac)
                st[f'exp{tt}'] = ex
                k1 = sch.op("vector", lambda ve: ve.reduce_max(
                    out=cwrow[:, 1:2], in_=cwtile[:], axis=AXX),
                    deps=[ex], inc=env.s_dv)
                k2 = sch.op("vector", lambda ve: ve.tensor_scalar(
                    out=cwmask[:], in0=cwtile[:], scalar1=cwrow[:, 1:2],
                    scalar2=-4.0, op0=ALU.is_ge, op1=ALU.mult),
                    deps=[k1], inc=env.s_dv)
                k3 = sch.op("vector", lambda ve: ve.tensor_add(
                    out=cwmask[:], in0=cwmask[:], in1=cwtile[:]),
                    deps=[k2], inc=env.s_dv)
                k4 = sch.op("vector", lambda ve: ve.reduce_max(
                    out=cwrow[:, 2:3], in_=cwmask[:], axis=AXX),
                    deps=[k3], inc=env.s_dv)
                k5 = sch.op("vector", lambda ve: ve.tensor_tensor(
                    out=cwrow[:, 3:4], in0=cwrow[:, 1:2], in1=cwrow[:, 2:3],
                    op=ALU.add), deps=[k4], inc=env.s_dv)
                k6 = sch.op("vector", lambda ve: ve.reciprocal(
                    out=cwrow[:, 4:5], in_=cwrow[:, 3:4]),
                    deps=[k5], inc=env.s_dv)
                k7 = sch.op("vector", lambda ve: ve.tensor_scalar(
                    out=cwmask[:], in0=cwtile[:], scalar1=cwrow[:, 2:3],
                    scalar2=None, op0=ALU.is_ge), deps=[k6, k4], inc=env.s_dv)
                k8 = sch.op("vector", lambda ve: ve.tensor_mul(
                    out=cwtile[:], in0=cwtile[:], in1=cwmask[:]),
                    deps=[k7], inc=env.s_dv)
                k9 = sch.op("vector", lambda ve: ve.tensor_scalar_mul(
                    out=cwtile[:], in0=cwtile[:], scalar1=cwrow[:, 4:5]),
                    deps=[k8], inc=env.s_dv)
                tdeps = [k9, env.consts_tok]
                if tt >= 1:
                    tdeps.append(st[f'cwev{tt-1}'])
                elif chq >= 1:
                    tdeps.append(prev[chq - 1]['cwev3'])
                tr = sch.op("tensor", lambda te: te.matmul(
                    ps_ct[:], cwtile[:], env.ident[:],
                    start=True, stop=True), deps=tdeps, inc=env.s_pe)
                cwev = sch.op(
                    "vector", lambda ve, chq=chq, tt=tt:
                    ve.tensor_copy(out=env.cwT[:, chq * CH + tt * 128:
                                               chq * CH + (tt + 1) * 128],
                                   in_=ps_ct[:]), deps=[tr], inc=env.s_dv)
                st[f'cwev{tt}'] = cwev
            st['lg_last'] = lg_tok
            prev[chq] = st
        sch.drain(env.s_xd[0])
        sch.drain(env.s_xd[1])
        sch.run("p6")


def _moe_pass(env, th):
    nc = env.nc
    T, THL, NTCH = env.T, env.THL, env.NTCH
    WB = env.wsets
    final_acc = {}
    with nc.sbuf_tensor(f"acc_{th}", [128, 8 * THL], F32) as accT, \
         nc.sbuf_tensor(f"x1h_{th}", [128, 8 * THL], BF16) as x1h:
        with nc.sbuf_tensor(f"w1b_{th}", [128, WB * 8 * 1024], BF16) as w1b, \
             nc.sbuf_tensor(f"w2b_{th}", [128, WB * 8 * 1024], BF16) as w2b, \
             nc.sbuf_tensor(f"hsb_{th}", [128, 8 * THL], BF16) as hsb, \
             nc.sbuf_tensor(f"stgw0_{th}", [128, 1024], F32) as stgw0, \
             nc.sbuf_tensor(f"stgw1_{th}", [128, 1024], F32) as stgw1, \
             nc.sbuf_tensor(f"cwb_{th}", [128, NTCH * CH], BF16) as cwb, \
             nc.psum_tensor(f"ph0_{th}", [128, CH], F32) as ph0, \
             nc.psum_tensor(f"ph1_{th}", [128, CH], F32) as ph1, \
             nc.psum_tensor(f"py0_{th}", [128, CH], F32) as py0, \
             nc.psum_tensor(f"py1_{th}", [128, CH], F32) as py1, \
             nc.psum_tensor(f"pcw_{th}", [128, CH], F32) as pcw:
            stgw = [stgw0, stgw1]
            ps_h = [ph0, ph1]
            ps_y = [py0, py1]
            sch = Sched(nc)
            x1toks = []
            for m in range(8):
                x1toks.append(sch.op(
                    "sync", lambda sy, m=m, th=th:
                    sy.dma_start(out=x1h[:, m * THL:(m + 1) * THL],
                                 in_=env.x1b[m * 128:(m + 1) * 128,
                                             th * THL:(th + 1) * THL]),
                    inc=env.s_h1, inck=16))
            x1last = x1toks[-1]
            acc_clear = []
            for m in range(8):
                acc_clear.append(sch.op(
                    "gpsimd", lambda gp, m=m:
                    gp.memset(accT[:, m * THL:(m + 1) * THL], 0.0),
                    inc=env.s_pl))

            cast_tok = {}
            step_hgrp = {}
            step_silu = {}
            step_scale = {}
            step_ygrp = {}
            step_acc = {}
            cwb_ev = {}
            dma_i = 0
            cast_list = []
            for s in range(16):
                e, fh = s // 2, s % 2
                for widx in range(16):
                    isw1 = widx < 8
                    kk = widx % 8
                    if isw1:
                        src = env.w1[e, kk * 128:(kk + 1) * 128,
                                     fh * 1024:(fh + 1) * 1024]
                    else:
                        src = env.w2[e, fh * 1024 + kk * 128:
                                     fh * 1024 + (kk + 1) * 128, :]
                    deps = [cast_list[dma_i - 2]] if dma_i >= 2 else []
                    dt_ = sch.op("sync", lambda sy, src=src, dma_i=dma_i:
                                 sy.dma_start(out=stgw[dma_i % 2][:], in_=src),
                                 deps=deps, inc=env.s_stg[dma_i % 2], inck=16)
                    cdeps = [dt_]
                    if s >= WB:
                        if isw1:
                            cdeps.append(step_hgrp[(s - WB, 7, NTCH - 1)])
                        else:
                            cdeps.append(step_ygrp[(s - WB, 7, NTCH - 1)])
                    dst = w1b if isw1 else w2b
                    ws = (s % WB) * 8
                    ct = sch.op(
                        "vector", lambda ve, dst=dst, kk=kk, dma_i=dma_i, ws=ws:
                        ve.tensor_copy(
                            out=dst[:, (ws + kk) * 1024:(ws + kk + 1) * 1024],
                            in_=stgw[dma_i % 2][:]),
                        deps=cdeps, inc=env.s_dv)
                    cast_tok[(s, 'w1' if isw1 else 'w2', kk)] = ct
                    cast_list.append(ct)
                    dma_i += 1
                if fh == 0:
                    for tch in range(NTCH):
                        deps = [env.consts_tok]
                        if tch >= 1:
                            deps.append(cwb_ev[(e, tch - 1)])
                        if e >= 1:
                            deps += [step_scale[(2 * e - 1, fm, tch)]
                                     for fm in range(8)]
                            deps.append(cwb_ev[(e - 1, NTCH - 1)])
                        bc = sch.op(
                            "tensor", lambda te, e=e, tch=tch, th=th:
                            te.matmul(pcw[:],
                                      env.sel8e[:, e * 128:(e + 1) * 128],
                                      env.cwT[0:8, th * THL + tch * CH:
                                              th * THL + (tch + 1) * CH],
                                      start=True, stop=True),
                            deps=deps, inc=env.s_pe)
                        cwb_ev[(e, tch)] = sch.op(
                            "vector", lambda ve, tch=tch:
                            ve.tensor_copy(out=cwb[:, tch * CH:(tch + 1) * CH],
                                           in_=pcw[:]),
                            deps=[bc], inc=env.s_dv)
                w1last = cast_tok[(s, 'w1', 7)]
                hidx = 0
                for fm in range(8):
                    for tch in range(NTCH):
                        deps = [w1last, x1last]
                        if hidx >= 2:
                            deps.append(step_silu[(s, (hidx - 2) // NTCH,
                                                  (hidx - 2) % NTCH)])
                        elif s >= 1:
                            php = 8 * NTCH - 2 + hidx
                            deps.append(step_silu[(s - 1, php // NTCH,
                                                  php % NTCH)])
                        ht = sch.op(
                            "tensor", lambda te, s=s, fm=fm, tch=tch,
                            hidx=hidx, WB=WB:
                            [te.matmul(
                                ps_h[hidx % 2][:],
                                env_w1b_slice(w1b, s, kd, fm, WB),
                                x1h[:, kd * THL + tch * CH:
                                    kd * THL + (tch + 1) * CH],
                                start=(kd == 0), stop=(kd == 7))
                             for kd in range(8)][-1],
                            deps=deps, inc=env.s_pe)
                        step_hgrp[(s, fm, tch)] = ht
                        sdeps = [ht]
                        if s >= 1:
                            sdeps.append(step_scale[(s - 1, fm, tch)])
                            sdeps.append(step_ygrp[(s - 1, 7, tch)])
                        stk = sch.op(
                            "scalar", lambda sc, fm=fm, tch=tch, hidx=hidx:
                            sc.activation(
                                out=hsb[:, fm * THL + tch * CH:
                                        fm * THL + (tch + 1) * CH],
                                in_=ps_h[hidx % 2][:], func=AF.Silu),
                            deps=sdeps, inc=env.s_ac)
                        step_silu[(s, fm, tch)] = stk
                        ptk = sch.op(
                            "gpsimd", lambda gp, fm=fm, tch=tch:
                            gp.tensor_mul(
                                out=hsb[:, fm * THL + tch * CH:
                                        fm * THL + (tch + 1) * CH],
                                in0=hsb[:, fm * THL + tch * CH:
                                        fm * THL + (tch + 1) * CH],
                                in1=cwb[:, tch * CH:(tch + 1) * CH]),
                            deps=[stk, cwb_ev[(e, tch)]], inc=env.s_pl)
                        step_scale[(s, fm, tch)] = ptk
                        hidx += 1
                w2last = cast_tok[(s, 'w2', 7)]
                yidx = 0
                for tch in range(NTCH):
                    scale_deps = [step_scale[(s, fm, tch)] for fm in range(8)]
                    for ym in range(8):
                        deps = [w2last] + scale_deps
                        if yidx >= 2:
                            deps.append(step_acc[(s, (yidx - 2) % 8,
                                                 (yidx - 2) // 8)])
                        elif s >= 1:
                            pyp = 8 * NTCH - 2 + yidx
                            deps.append(step_acc[(s - 1, pyp % 8, pyp // 8)])
                        yt = sch.op(
                            "tensor", lambda te, s=s, ym=ym, tch=tch,
                            yidx=yidx, WB=WB:
                            [te.matmul(
                                ps_y[yidx % 2][:],
                                env_w1b_slice(w2b, s, kf, ym, WB),
                                hsb[:, kf * THL + tch * CH:
                                    kf * THL + (tch + 1) * CH],
                                start=(kf == 0), stop=(kf == 7))
                             for kf in range(8)][-1],
                            deps=deps, inc=env.s_pe)
                        step_ygrp[(s, ym, tch)] = yt
                        adeps = [yt]
                        if s >= 1:
                            adeps.append(step_acc[(s - 1, ym, tch)])
                        else:
                            adeps.append(acc_clear[ym])
                        at = sch.op(
                            "vector", lambda ve, ym=ym, tch=tch, yidx=yidx:
                            ve.tensor_add(
                                out=accT[:, ym * THL + tch * CH:
                                         ym * THL + (tch + 1) * CH],
                                in0=accT[:, ym * THL + tch * CH:
                                         ym * THL + (tch + 1) * CH],
                                in1=ps_y[yidx % 2][:]),
                            deps=adeps, inc=env.s_dv)
                        step_acc[(s, ym, tch)] = at
                        yidx += 1
            for ym in range(8):
                for tch in range(NTCH):
                    final_acc[(ym, tch)] = step_acc[(15, ym, tch)]
            if getattr(env, 'dbg_moe', None) is not None:
                for m in range(8):
                    sch.op("gpsimd", lambda gp, m=m, th=th: gp.dma_start(
                        out=env.dbg_moe[m * 128:(m + 1) * 128,
                                        th * THL:(th + 1) * THL],
                        in_=accT[:, m * THL:(m + 1) * THL]),
                        deps=[step_acc[(15, m, tch)] for tch in range(NTCH)],
                        inc=env.s_dbg, inck=16)
            sch.drain(env.s_dbg)
            sch.run(f"moe{th}")

        _ln2(env, th, accT, final_acc)


def env_w1b_slice(wb, s, kd, fm, WB=1):
    base = ((s % WB) * 8 + kd) * 1024
    return wb[:, base + fm * 128: base + (fm + 1) * 128]


def _ln2(env, th, accT, final_acc):
    nc = env.nc
    THL, NTCH = env.THL, env.NTCH
    with nc.sbuf_tensor(f"xst2_{th}", [128, 16 * CH], F32) as xst2, \
         nc.sbuf_tensor(f"res2_{th}", [128, 8 * CH], F32) as res2, \
         nc.sbuf_tensor(f"rsq2_{th}", [128, 2 * CH], F32) as rsq2, \
         nc.sbuf_tensor(f"outf_{th}", [128, 8 * CH], F32) as outf, \
         nc.sbuf_tensor(f"ab2_{th}", [128, 2 * CH], F32) as ab2, \
         nc.sbuf_tensor(f"rows2_{th}", [1, 6 * CH], F32) as rows2, \
         nc.psum_tensor(f"ps2s_{th}", [1, CH], F32) as ps2s, \
         nc.psum_tensor(f"ps2q_{th}", [1, CH], F32) as ps2q, \
         nc.psum_tensor(f"ps2A_{th}", [128, CH], F32) as ps2A, \
         nc.psum_tensor(f"ps2B_{th}", [128, CH], F32) as ps2B:
        sch = Sched(nc)
        prev2 = {}
        for c in range(NTCH):
            st = {}
            chg = th * NTCH + c
            xtoks = []
            for m in range(8):
                deps = [final_acc[(m, c)]]
                if c >= 2:
                    deps.append(prev2[c - 2]['ra'][m])
                xtoks.append(sch.op(
                    "sync", lambda sy, m=m, c=c, chg=chg:
                    sy.dma_start(
                        out=xst2[:, ((c % 2) * 8 + m) * CH:
                                 ((c % 2) * 8 + m + 1) * CH],
                        in_=env.x1f[m * 128:(m + 1) * 128,
                                    chg * CH:(chg + 1) * CH]),
                    deps=deps, inc=env.s_xs[c % 2], inck=16))
            ra = [None] * 8
            sq = [None] * 8
            sums = [None] * 8
            ssqs = [None] * 8
            for m in range(8):
                rdeps = [xtoks[-1], final_acc[(m, c)]]
                if c >= 1:
                    rdeps.append(prev2[c - 1]['sum'][m])
                    rdeps.append(prev2[c - 1]['norm'][m])
                    if 'dbgres' in prev2[c - 1]:
                        rdeps.append(prev2[c - 1]['dbgres'][m])
                ra[m] = sch.op(
                    "vector", lambda ve, m=m, c=c:
                    ve.tensor_add(
                        out=res2[:, m * CH:(m + 1) * CH],
                        in0=accT[:, m * THL + c * CH: m * THL + (c + 1) * CH],
                        in1=xst2[:, ((c % 2) * 8 + m) * CH:
                                 ((c % 2) * 8 + m + 1) * CH]),
                    deps=rdeps, inc=env.s_dv)
                sdeps = [ra[m]]
                if m >= 2:
                    sdeps.append(ssqs[m - 2])
                elif c >= 1:
                    sdeps.append(prev2[c - 1]['ssq'][6 + m])
                sq[m] = sch.op(
                    "scalar", lambda sc, m=m:
                    sc.activation(out=rsq2[:, (m % 2) * CH:(m % 2 + 1) * CH],
                                  in_=res2[:, m * CH:(m + 1) * CH],
                                  func=AF.Square),
                    deps=sdeps, inc=env.s_ac)
                sums[m] = sch.op(
                    "tensor", lambda te, m=m:
                    te.matmul(ps2s[:], env.ones_fc[:],
                              res2[:, m * CH:(m + 1) * CH],
                              start=(m == 0), stop=(m == 7)),
                    deps=[ra[m]] + ([prev2[c - 1]['rows0']]
                                    if (m == 0 and c >= 1) else []),
                    inc=env.s_pe)
                ssqs[m] = sch.op(
                    "tensor", lambda te, m=m:
                    te.matmul(ps2q[:], env.ones_fc[:],
                              rsq2[:, (m % 2) * CH:(m % 2 + 1) * CH],
                              start=(m == 0), stop=(m == 7)),
                    deps=[sq[m]] + ([prev2[c - 1]['rows1']]
                                    if (m == 0 and c >= 1) else []),
                    inc=env.s_pe)
            c1 = sch.op("vector", lambda ve: ve.tensor_scalar_mul(
                out=rows2[0:1, 0:CH], in0=ps2s[:], scalar1=1.0 / D),
                deps=[sums[7]], inc=env.s_dv)
            c2 = sch.op("vector", lambda ve: ve.tensor_mul(
                out=rows2[0:1, CH:2 * CH], in0=rows2[0:1, 0:CH],
                in1=rows2[0:1, 0:CH]), deps=[c1], inc=env.s_dv)
            c3 = sch.op("vector", lambda ve: ve.tensor_scalar_mul(
                out=rows2[0:1, 4 * CH:5 * CH], in0=ps2q[:], scalar1=1.0 / D),
                deps=[ssqs[7]], inc=env.s_dv)
            c4 = sch.op("vector", lambda ve: ve.tensor_sub(
                out=rows2[0:1, CH:2 * CH], in0=rows2[0:1, 4 * CH:5 * CH],
                in1=rows2[0:1, CH:2 * CH]), deps=[c2, c3], inc=env.s_dv)
            c5 = sch.op("scalar", lambda sc: sc.activation(
                out=rows2[0:1, 5 * CH:6 * CH], in_=rows2[0:1, CH:2 * CH],
                func=AF.Sqrt, bias=env.eps_t[:]), deps=[c4], inc=env.s_ac)
            c6 = sch.op("vector", lambda ve: ve.reciprocal(
                out=rows2[0:1, 2 * CH:3 * CH], in_=rows2[0:1, 5 * CH:6 * CH]),
                deps=[c5], inc=env.s_dv)
            c7 = sch.op("vector", lambda ve: ve.tensor_mul(
                out=rows2[0:1, 3 * CH:4 * CH], in0=rows2[0:1, 0:CH],
                in1=rows2[0:1, 2 * CH:3 * CH]), deps=[c6], inc=env.s_dv)
            c8 = sch.op("vector", lambda ve: ve.tensor_scalar_mul(
                out=rows2[0:1, 3 * CH:4 * CH], in0=rows2[0:1, 3 * CH:4 * CH],
                scalar1=-1.0), deps=[c7], inc=env.s_dv)
            st['rows0'] = c1
            st['rows1'] = c3
            bdeps_extra = [prev2[c - 1]['abev']] if c >= 1 else []
            bA = sch.op("tensor", lambda te: te.matmul(
                ps2A[:], env.ones_f1[0:1, :], rows2[0:1, 2 * CH:3 * CH],
                start=True, stop=True), deps=[c6] + bdeps_extra, inc=env.s_pe)
            bB = sch.op("tensor", lambda te: te.matmul(
                ps2B[:], env.ones_f1[0:1, :], rows2[0:1, 3 * CH:4 * CH],
                start=True, stop=True), deps=[c8] + bdeps_extra, inc=env.s_pe)
            evdeps = [bA, bB]
            if c >= 1:
                evdeps.append(prev2[c - 1]['norm'][7])
                if 'dbgab' in prev2[c - 1]:
                    evdeps.append(prev2[c - 1]['dbgab'])
            abev = sch.op("vector", lambda ve: [
                ve.tensor_copy(out=ab2[:, 0:CH], in_=ps2A[:]),
                ve.tensor_copy(out=ab2[:, CH:2 * CH], in_=ps2B[:])][-1],
                deps=evdeps, inc=env.s_dv)
            st['abev'] = abev
            if getattr(env, 'dbg_ab', None) is not None:
                st['dbgab'] = sch.op(
                    "gpsimd", lambda gp, th=th, c=c: gp.dma_start(
                        out=env.dbg_ab[:, (th * env.NTCH + c) * 2 * CH:
                                       (th * env.NTCH + c + 1) * 2 * CH],
                        in_=ab2[:]), deps=[abev], inc=env.s_dbg, inck=16)
                st['dbgres'] = [None] * 8
                for m in range(8):
                    st['dbgres'][m] = sch.op(
                        "gpsimd", lambda gp, m=m, chg=chg: gp.dma_start(
                            out=env.dbg_res2[m * 128:(m + 1) * 128,
                                             chg * CH:(chg + 1) * CH],
                            in_=res2[:, m * CH:(m + 1) * CH]),
                        deps=[ra[m], abev], inc=env.s_dbg, inck=16)
            norm = [None] * 8
            odma = [None] * 8
            for m in range(8):
                n1 = sch.op("vector", lambda ve, m=m: ve.tensor_mul(
                    out=rsq2[:, (m % 2) * CH:(m % 2 + 1) * CH],
                    in0=res2[:, m * CH:(m + 1) * CH], in1=ab2[:, 0:CH]),
                    deps=[abev, sq[m], sums[m], ssqs[m]], inc=env.s_dv)
                ndeps = [n1]
                if c >= 1:
                    ndeps.append(prev2[c - 1]['odma'][7])
                n2 = sch.op("vector", lambda ve, m=m: ve.tensor_add(
                    out=outf[:, m * CH:(m + 1) * CH],
                    in0=rsq2[:, (m % 2) * CH:(m % 2 + 1) * CH],
                    in1=ab2[:, CH:2 * CH]), deps=ndeps, inc=env.s_dv)
                norm[m] = n2
                odma[m] = sch.op(
                    "gpsimd", lambda gp, m=m, chg=chg:
                    gp.dma_start(out=env.outT[m * 128:(m + 1) * 128,
                                              chg * CH:(chg + 1) * CH],
                                 in_=outf[:, m * CH:(m + 1) * CH]),
                    deps=[n2], inc=env.s_od[c % 2], inck=16)
            st['norm'] = norm
            st['odma'] = odma
            st['ra'] = ra
            st['sum'] = sums
            st['ssq'] = ssqs
            prev2[c] = st
        sch.drain(env.s_od[0])
        sch.drain(env.s_od[1])
        sch.drain(env.s_xs[0])
        sch.drain(env.s_xs[1])
        sch.run(f"ln2_{th}")


def build(T=4096, debug=False, wsets=1):
    assert T % 1024 == 0
    env = Env()
    env.T = T
    env.wsets = wsets
    env.NCH = T // CH
    env.NKT = T // 128
    env.THL = T // 2
    env.NTCH = env.THL // CH

    nc = bass.Bass("TRN2", target_bir_lowering=False)
    env.nc = nc

    env.xT = nc.dram_tensor("xT", [D, T], F32, kind="ExternalInput").ap()
    env.wdkv = nc.dram_tensor("wdkv", [D, DL], F32, kind="ExternalInput").ap()
    env.wuk = nc.dram_tensor("wuk", [DL, D], F32, kind="ExternalInput").ap()
    env.wuv = nc.dram_tensor("wuv", [DL, D], F32, kind="ExternalInput").ap()
    env.wuq = nc.dram_tensor("wuq", [D, D], F32, kind="ExternalInput").ap()
    env.wo = nc.dram_tensor("wo", [D, D], F32, kind="ExternalInput").ap()
    env.rw = nc.dram_tensor("rw", [D, E], F32, kind="ExternalInput").ap()
    env.w1 = nc.dram_tensor("w1", [E, D, FF], F32, kind="ExternalInput").ap()
    env.w2 = nc.dram_tensor("w2", [E, FF, D], F32, kind="ExternalInput").ap()
    env.c_ident = nc.dram_tensor("c_ident", [128, 128], F32,
                                 kind="ExternalInput").ap()
    env.c_sel16 = nc.dram_tensor("c_sel16", [128, 128], BF16,
                                 kind="ExternalInput").ap()
    env.c_sel8h = nc.dram_tensor("c_sel8h", [128, 8], BF16,
                                 kind="ExternalInput").ap()
    env.c_sel8e = nc.dram_tensor("c_sel8e", [8, 1024], BF16,
                                 kind="ExternalInput").ap()

    env.outT = nc.dram_tensor("outT", [D, T], F32, kind="ExternalOutput").ap()
    if debug:
        env.x1f = nc.dram_tensor("x1f", [D, T], F32, kind="ExternalOutput").ap()
        env.x1b = nc.dram_tensor("x1b", [D, T], BF16, kind="ExternalOutput").ap()
    else:
        env.x1f = nc.dram_tensor("x1f", [D, T], F32).ap()
        env.x1b = nc.dram_tensor("x1b", [D, T], BF16).ap()
    if debug:
        env.dbg_ckv = nc.dram_tensor("dbg_ckv", [DL, T], BF16,
                                     kind="ExternalOutput").ap()
        env.dbg_q = nc.dram_tensor("dbg_q", [D, T], BF16,
                                   kind="ExternalOutput").ap()
        env.dbg_cw = nc.dram_tensor("dbg_cw", [8, T], BF16,
                                    kind="ExternalOutput").ap()
        env.dbg_moe = nc.dram_tensor("dbg_moe", [D, T], F32,
                                     kind="ExternalOutput").ap()
        env.dbg_res2 = nc.dram_tensor("dbg_res2", [D, T], F32,
                                      kind="ExternalOutput").ap()
        env.dbg_ab = nc.dram_tensor("dbg_ab", [128, 16 * CH], F32,
                                    kind="ExternalOutput").ap()

    ctx = ExitStack()
    EC = ctx.enter_context

    env.s_pe = Ctr(EC(nc.semaphore("s_pe")))
    env.s_ac = Ctr(EC(nc.semaphore("s_ac")))
    env.s_dv = Ctr(EC(nc.semaphore("s_dv")))
    env.s_pl = Ctr(EC(nc.semaphore("s_pl")))
    env.s_cst = Ctr(EC(nc.semaphore("s_cst")))
    env.s_stg = [Ctr(EC(nc.semaphore(f"s_stg{i}"))) for i in range(2)]
    env.s_xstg = [Ctr(EC(nc.semaphore(f"s_xstg{i}"))) for i in range(2)]
    env.s_cx = [Ctr(EC(nc.semaphore(f"s_cx{i}"))) for i in range(2)]
    env.s_xs = [Ctr(EC(nc.semaphore(f"s_xs{i}"))) for i in range(2)]
    env.s_xd = [Ctr(EC(nc.semaphore(f"s_xd{i}"))) for i in range(2)]
    env.s_h1 = Ctr(EC(nc.semaphore("s_h1")))
    env.s_od = [Ctr(EC(nc.semaphore(f"s_od{i}"))) for i in range(2)]
    env.s_cd = [Ctr(EC(nc.semaphore(f"s_cd{i}"))) for i in range(2)]
    env.s_msc = Ctr(EC(nc.semaphore("s_msc")))
    env.s_dbg = Ctr(EC(nc.semaphore("s_dbg")))
    env.all_dma_ctrs = ([env.s_cst, env.s_h1, env.s_msc, env.s_dbg]
                        + env.s_stg + env.s_xstg + env.s_cx + env.s_xs
                        + env.s_xd + env.s_od + env.s_cd)

    env.ones_f1 = EC(nc.sbuf_tensor("ones_f1", [1, 128], F32))
    env.ones_fc = EC(nc.sbuf_tensor("ones_fc", [128, 1], F32))
    env.eps_t = EC(nc.sbuf_tensor("eps_t", [1, 1], F32))
    env.ident = EC(nc.sbuf_tensor("ident", [128, 128], F32))
    env.sel16 = EC(nc.sbuf_tensor("sel16", [128, 128], BF16))
    env.sel8h = EC(nc.sbuf_tensor("sel8h", [128, 8], BF16))
    env.sel8e = EC(nc.sbuf_tensor("sel8e", [8, 1024], BF16))
    env.qmax16 = EC(nc.sbuf_tensor("qmax16", [16, 1], F32))
    env.qmaxG = EC(nc.sbuf_tensor("qmaxG", [4, 1], F32))
    env.nrm_tmp = EC(nc.sbuf_tensor("nrm_tmp", [16, 1], F32))
    env.cwT = EC(nc.sbuf_tensor("cwT", [8, T], BF16))
    env.ctxh = nc.dram_tensor("ctxh", [D, T], BF16,
                              kind="ExternalOutput" if debug else "Internal").ap()

    es_w = ExitStack()
    wuk_bf = es_w.enter_context(nc.sbuf_tensor("wuk_bf", [128, 2 * D], BF16))
    wuv_bf = es_w.enter_context(nc.sbuf_tensor("wuv_bf", [128, 2 * D], BF16))
    es_ckq = ExitStack()
    env.c_kvT = [es_ckq.enter_context(nc.sbuf_tensor(f"ckv{i}", [128, T], BF16))
                 for i in range(2)]
    env.qT = [es_ckq.enter_context(nc.sbuf_tensor(f"qT{i}", [128, T], BF16))
              for i in range(8)]

    es_x = ExitStack()
    wdkv_bf = es_x.enter_context(nc.sbuf_tensor("wdkv_bf", [128, 8 * DL], BF16))
    wuq_bf = es_x.enter_context(nc.sbuf_tensor("wuq_bf", [128, 8 * D], BF16))
    xT_bf = es_x.enter_context(nc.sbuf_tensor("xT_bf", [128, 8 * env.T], BF16))

    _p0(env, wdkv_bf, wuk_bf, wuv_bf, wuq_bf, xT_bf)

    groups_ckv = [(m, ch) for m in range(2) for ch in range(env.NCH)]
    _proj_phase(
        env, "ckv", groups_ckv, 8,
        lambda g, k: wdkv_bf[:, k * DL + g[0] * 128: k * DL + (g[0] + 1) * 128],
        lambda g, k: xT_bf[:, k * T + g[1] * CH: k * T + (g[1] + 1) * CH],
        lambda ve, g, ps: ve.tensor_copy(
            out=env.c_kvT[g[0]][:, g[1] * CH:(g[1] + 1) * CH], in_=ps[:]),
    )
    groups_q = [(m, ch) for m in range(8) for ch in range(env.NCH)]
    _proj_phase(
        env, "q", groups_q, 8,
        lambda g, k: wuq_bf[:, k * D + g[0] * 128: k * D + (g[0] + 1) * 128],
        lambda g, k: xT_bf[:, k * T + g[1] * CH: k * T + (g[1] + 1) * CH],
        lambda ve, g, ps: ve.tensor_copy(
            out=env.qT[g[0]][:, g[1] * CH:(g[1] + 1) * CH], in_=ps[:]),
    )
    _qnorms(env)

    if debug:
        sch = Sched(nc)
        for m in range(2):
            sch.op("gpsimd", lambda gp, m=m: gp.dma_start(
                out=env.dbg_ckv[m * 128:(m + 1) * 128, :],
                in_=env.c_kvT[m][:]), inc=env.s_dbg, inck=16)
        for m in range(8):
            sch.op("gpsimd", lambda gp, m=m: gp.dma_start(
                out=env.dbg_q[m * 128:(m + 1) * 128, :], in_=env.qT[m][:]),
                inc=env.s_dbg, inck=16)
        sch.drain(env.s_dbg)
        sch.run("dbgq")

    es_x.close()

    for g in range(NH // HG):
        _attn_group(env, g, wuk_bf, wuv_bf)

    es_ckq.close()
    es_w.close()

    es_p6 = ExitStack()
    wo_bf = es_p6.enter_context(nc.sbuf_tensor("wo_bf", [128, 8 * D], BF16))
    rw_bf = es_p6.enter_context(nc.sbuf_tensor("rw_bf", [128, 8 * E], BF16))
    with nc.sbuf_tensor("wstg0b", [128, D], F32) as wstg0b, \
         nc.sbuf_tensor("wstg1b", [128, D], F32) as wstg1b:
        wp2 = []
        for kd in range(8):
            wp2.append((wo_bf, kd * D, D, env.wo[kd * 128:(kd + 1) * 128, :]))
        for kd in range(8):
            wp2.append((rw_bf, kd * E, E, env.rw[kd * 128:(kd + 1) * 128, :]))
        _stream_pieces(env, wp2, [wstg0b, wstg1b])

    _p6(env, wo_bf, rw_bf)
    es_p6.close()

    for th in range(2):
        _moe_pass(env, th)

    if debug:
        sch = Sched(nc)
        sch.op("gpsimd", lambda gp: gp.dma_start(out=env.dbg_cw[:],
                                                 in_=env.cwT[:]),
               inc=env.s_dbg, inck=16)
        sch.drain(env.s_dbg)
        sch.run("dbgcw")

    with nc.Block("fin") as blk:
        def fin(gp):
            for ctr in env.all_dma_ctrs:
                if ctr.n > 0:
                    gp.wait_ge(ctr.h, ctr.n)
        blk.gpsimd(fin)

    ctx.close()
    return nc


def _consts():
    ident = np.eye(128, dtype=np.float32)
    sel16 = np.zeros((128, 128), dtype=ml_dtypes.bfloat16)
    for kd in range(8):
        sel16[0:64, 16 * kd + 2 * kd] = 1
        sel16[64:128, 16 * kd + 2 * kd + 1] = 1
    sel8h = np.zeros((128, 8), dtype=ml_dtypes.bfloat16)
    for p in range(2):
        sel8h[0:64, 4 * p + 2 * p] = 1
        sel8h[64:128, 4 * p + 2 * p + 1] = 1
    sel8e = np.zeros((8, 1024), dtype=ml_dtypes.bfloat16)
    for e in range(8):
        sel8e[e, e * 128:(e + 1) * 128] = 1
    return {"c_ident": ident, "c_sel16": sel16, "c_sel8h": sel8h,
            "c_sel8e": sel8e}


def kernel(L0_attn_x, L0_attn_W_DKV, L0_attn_W_UK, L0_attn_W_UV, L0_attn_W_UQ,
           L0_attn_W_O, L0_ffn_router_w, L0_ffn_w1, L0_ffn_w2):
    x = np.ascontiguousarray(np.asarray(L0_attn_x, dtype=np.float32))
    assert x.shape == (8, 4096, 1024)
    shared = {
        "wdkv": np.ascontiguousarray(np.asarray(L0_attn_W_DKV, np.float32)),
        "wuk": np.ascontiguousarray(np.asarray(L0_attn_W_UK, np.float32)),
        "wuv": np.ascontiguousarray(np.asarray(L0_attn_W_UV, np.float32)),
        "wuq": np.ascontiguousarray(np.asarray(L0_attn_W_UQ, np.float32)),
        "wo": np.ascontiguousarray(np.asarray(L0_attn_W_O, np.float32)),
        "rw": np.ascontiguousarray(np.asarray(L0_ffn_router_w, np.float32)),
        "w1": np.ascontiguousarray(np.asarray(L0_ffn_w1, np.float32)),
        "w2": np.ascontiguousarray(np.asarray(L0_ffn_w2, np.float32)),
    }
    shared.update(_consts())
    nc = build(T=4096, debug=False)
    in_maps = []
    for b in range(8):
        m = dict(shared)
        m["xT"] = np.ascontiguousarray(x[b].T)
        in_maps.append(m)
    res = run_bass_kernel_spmd(nc, in_maps, core_ids=list(range(8)))
    out = np.stack([np.asarray(r["outT"], dtype=np.float32).T
                    for r in res.results], axis=0)
    return out


# version 04
# speedup vs baseline: 1.0421x; 1.0421x over previous
import numpy as np
import ml_dtypes
from contextlib import ExitStack

import concourse.bass as bass
import concourse.mybir as mybir
from concourse.bass_utils import run_bass_kernel_spmd

AF = mybir.ActivationFunctionType
ALU = mybir.AluOpType
AXX = mybir.AxisListType.X
F32 = mybir.dt.float32
BF16 = mybir.dt.bfloat16

D = 1024
DL = 256
NH = 16
HD = 64
E = 8
FF = 2048
CH = 512
HG = 4
EPS_LN = 1e-5
ENGINES = ("sync", "tensor", "scalar", "vector", "gpsimd")


class Ctr:
    def __init__(self, h):
        self.h = h
        self.n = 0


class Tok:
    __slots__ = ("ctr", "val", "k")

    def __init__(self, ctr, k, val):
        self.ctr = ctr
        self.k = k
        self.val = val


class Sched:

    def __init__(self, nc):
        self.nc = nc
        self.items = []

    def op(self, eng, fn, deps=(), inc=None, inck=1):
        deps = tuple(d for d in deps if d is not None)
        tok = None
        if inc is not None:
            inc.n += inck
            tok = Tok(inc, inck, inc.n)
        self.items.append((eng, fn, deps, tok))
        return tok

    def drain(self, ctr, eng="gpsimd"):
        v = ctr.n
        if v > 0:
            self.op(eng, lambda e, h=ctr.h, v=v: e.wait_ge(h, v))

    def run(self, name=None):
        with self.nc.Block(name) as blk:
            for eng in ENGINES:
                items = [it for it in self.items if it[0] == eng]
                if not items:
                    continue

                def body(e, items=items):
                    last = {}
                    for (_, fn, deps, tok) in items:
                        for d in deps:
                            if last.get(id(d.ctr), -1) < d.val:
                                e.wait_ge(d.ctr.h, d.val)
                                last[id(d.ctr)] = d.val
                        inst = fn(e)
                        if tok is not None:
                            inst.then_inc(tok.ctr.h, tok.k)
                getattr(blk, eng)(body)
        self.items = []


class Env:
    pass


def _stream_pieces(env, pieces, stg, dma_ctr=None, cast_eng="vector"):
    nc = env.nc
    sch = Sched(nc)
    toks = []
    cast_ctr = env.s_dv if cast_eng == "vector" else env.s_pl
    for i, (dst, c0, ncol, src) in enumerate(pieces):
        deps = [toks[i - 2]] if i >= 2 else []
        dt_ = sch.op("sync", lambda sy, i=i, ncol=ncol, src=src:
                     sy.dma_start(out=stg[i % 2][:, 0:ncol], in_=src),
                     deps=deps, inc=env.s_stg[i % 2], inck=16)
        ct = sch.op(cast_eng, lambda e, i=i, dst=dst, c0=c0, ncol=ncol:
                    e.tensor_copy(out=dst[:, c0:c0 + ncol],
                                  in_=stg[i % 2][:, 0:ncol]),
                    deps=[dt_], inc=cast_ctr)
        toks.append(ct)
    sch.run("stream")
    return toks


def _proj_phase(env, name, groups, nk, lhs_fn, rhs_fn, evac_fn, psum_n=CH):
    nc = env.nc
    toks = {}
    with nc.psum_tensor(f"{name}p0", [128, psum_n], F32) as p0, \
         nc.psum_tensor(f"{name}p1", [128, psum_n], F32) as p1:
        ps = [p0, p1]
        sch = Sched(nc)
        for gi, g in enumerate(groups):
            deps = [toks[groups[gi - 2]]] if gi >= 2 else []

            def mk_mm(g=g, gi=gi):
                def f(te):
                    for k in range(nk):
                        inst = te.matmul(ps[gi % 2][:], lhs_fn(g, k),
                                         rhs_fn(g, k),
                                         start=(k == 0), stop=(k == nk - 1))
                    return inst
                return f
            mt = sch.op("tensor", mk_mm(), deps=deps, inc=env.s_pe)

            def mk_ev(g=g, gi=gi):
                def f(ve):
                    return evac_fn(ve, g, ps[gi % 2])
                return f
            toks[g] = sch.op("vector", mk_ev(), deps=[mt], inc=env.s_dv)
        sch.run(name)
    return toks


def _p0(env, wdkv_bf, wuk_bf, wuv_bf, wuq_bf, xT_bf):
    nc = env.nc
    T = env.T
    with nc.sbuf_tensor("wstg0", [128, D], F32) as wstg0, \
         nc.sbuf_tensor("wstg1", [128, D], F32) as wstg1, \
         nc.sbuf_tensor("xstg0", [128, 1024], F32) as xstg0, \
         nc.sbuf_tensor("xstg1", [128, 1024], F32) as xstg1:
        wstg = [wstg0, wstg1]
        xstg = [xstg0, xstg1]
        sch = Sched(nc)
        ctok = None
        for dst, src in ((env.ident, env.c_ident), (env.sel16, env.c_sel16),
                         (env.sel8h, env.c_sel8h), (env.sel8e, env.c_sel8e)):
            ctok = sch.op("sync", lambda sy, dst=dst, src=src:
                          sy.dma_start(out=dst[:], in_=src),
                          inc=env.s_cst, inck=16)
        env.consts_tok = ctok
        sch.op("vector", lambda ve: ve.memset(env.ones_f1[:], 1.0))
        sch.op("vector", lambda ve: ve.memset(env.ones_fc[:], 1.0))
        sch.op("vector", lambda ve: ve.memset(env.eps_t[:], EPS_LN))
        sch.op("vector", lambda ve: ve.memset(env.qmax16[:], 0.0))
        wpieces = []
        for kd in range(8):
            wpieces.append((wdkv_bf, kd * DL, DL,
                            env.wdkv[kd * 128:(kd + 1) * 128, :]))
        for kl in range(2):
            wpieces.append((wuk_bf, kl * D, D,
                            env.wuk[kl * 128:(kl + 1) * 128, :]))
        for kl in range(2):
            wpieces.append((wuv_bf, kl * D, D,
                            env.wuv[kl * 128:(kl + 1) * 128, :]))
        for kd in range(8):
            wpieces.append((wuq_bf, kd * D, D,
                            env.wuq[kd * 128:(kd + 1) * 128, :]))
        wtoks = []
        for i, (dst, c0, ncol, src) in enumerate(wpieces):
            deps = [wtoks[i - 2]] if i >= 2 else []
            dt_ = sch.op("sync", lambda sy, i=i, ncol=ncol, src=src:
                         sy.dma_start(out=wstg[i % 2][:, 0:ncol], in_=src),
                         deps=deps, inc=env.s_stg[i % 2], inck=16)
            ct = sch.op("vector", lambda ve, i=i, dst=dst, c0=c0, ncol=ncol:
                        ve.tensor_copy(out=dst[:, c0:c0 + ncol],
                                       in_=wstg[i % 2][:, 0:ncol]),
                        deps=[dt_], inc=env.s_dv)
            wtoks.append(ct)
        xtoks = []
        xpieces = [(kd, cp) for kd in range(8) for cp in range(T // 1024)]
        for i, (kd, cp) in enumerate(xpieces):
            deps = [xtoks[i - 2]] if i >= 2 else []
            dt_ = sch.op("sync", lambda sy, i=i, kd=kd, cp=cp:
                         sy.dma_start(out=xstg[i % 2][:],
                                      in_=env.xT[kd * 128:(kd + 1) * 128,
                                                 cp * 1024:(cp + 1) * 1024]),
                         deps=deps, inc=env.s_xstg[i % 2], inck=16)
            ct = sch.op("gpsimd", lambda gp, i=i, kd=kd, cp=cp:
                        gp.tensor_copy(
                            out=xT_bf[:, kd * T + cp * 1024:
                                      kd * T + (cp + 1) * 1024],
                            in_=xstg[i % 2][:]),
                        deps=[dt_], inc=env.s_pl)
            xtoks.append(ct)
        sch.run("p0")


def _qnorms(env):
    nc = env.nc
    NCH = env.NCH
    with nc.sbuf_tensor("qsq0", [128, CH], BF16) as qsq0, \
         nc.sbuf_tensor("qsq1", [128, CH], BF16) as qsq1, \
         nc.psum_tensor("psn16", [16, CH], F32) as psn16:
        qsq = [qsq0, qsq1]
        sch = Sched(nc)
        sq_toks = []
        prev_red = None
        prev_mx = None
        i = 0
        for chq in range(NCH):
            grp_tok = None
            for kd in range(8):
                deps = [sq_toks[i - 2][1]] if i >= 2 else []
                st_ = sch.op("scalar", lambda sc, i=i, kd=kd, chq=chq:
                             sc.activation(
                                 out=qsq[i % 2][:],
                                 in_=env.qT[kd][:, chq * CH:(chq + 1) * CH],
                                 func=AF.Square),
                             deps=deps, inc=env.s_ac)
                mdeps = [st_, env.consts_tok]
                if kd == 0 and prev_red is not None:
                    mdeps.append(prev_red)
                mt = sch.op("tensor", lambda te, i=i, kd=kd:
                            te.matmul(psn16[:],
                                      env.sel16[:, kd * 16:(kd + 1) * 16],
                                      qsq[i % 2][:],
                                      start=(kd == 0), stop=(kd == 7)),
                            deps=mdeps, inc=env.s_pe)
                sq_toks.append((st_, mt))
                grp_tok = mt
                i += 1
            rd = sch.op("vector", lambda ve: ve.reduce_max(
                out=env.nrm_tmp[:], in_=psn16[:], axis=AXX),
                deps=[grp_tok] + ([prev_mx] if prev_mx else []), inc=env.s_dv)
            mx = sch.op("vector", lambda ve: ve.tensor_max(
                out=env.qmax16[:], in0=env.qmax16[:], in1=env.nrm_tmp[:]),
                deps=[rd], inc=env.s_dv)
            prev_red = rd
            prev_mx = mx
        sch.run("p2n")


def _attn_group(env, g, wuk_bf, wuv_bf):
    nc = env.nc
    T, NCH, NKT = env.T, env.NCH, env.NKT
    VW = HG * 65
    with nc.sbuf_tensor(f"kTg{g}", [128, 2 * T], BF16) as kTg, \
         nc.sbuf_tensor(f"vaug{g}", [128, NKT * VW], BF16) as vaug, \
         nc.sbuf_tensor(f"bias4_{g}", [128, 4], F32) as bias4, \
         nc.sbuf_tensor(f"kmax4_{g}", [4, 1], F32) as kmax4, \
         nc.sbuf_tensor(f"ktmp4_{g}", [4, 1], F32) as ktmp4, \
         nc.sbuf_tensor(f"brow_{g}", [1, 4], F32) as brow, \
         nc.sbuf_tensor(f"probT_{g}", [128, 6 * CH], BF16) as probT, \
         nc.sbuf_tensor(f"probM_{g}", [128, 4 * CH], BF16) as probM, \
         nc.sbuf_tensor(f"recipS_{g}", [1, 2 * CH], F32) as recipS, \
         nc.sbuf_tensor(f"rbS_{g}", [64, 2 * CH], F32) as rbS, \
         nc.sbuf_tensor(f"ksq0_{g}", [128, CH], BF16) as ksq0, \
         nc.sbuf_tensor(f"ksq1_{g}", [128, CH], BF16) as ksq1, \
         nc.sbuf_tensor(f"ctxs_{g}", [64, 2 * CH], BF16) as ctxs:

        groups_k = [(p, ch) for p in range(2) for ch in range(NCH)]
        ktoks = _proj_phase(
            env, f"kT{g}", groups_k, 2,
            lambda gk, k: wuk_bf[:, k * D + g * 256 + gk[0] * 128:
                                 k * D + g * 256 + (gk[0] + 1) * 128],
            lambda gk, k: env.c_kvT[k][:, gk[1] * CH:(gk[1] + 1) * CH],
            lambda ve, gk, ps: ve.tensor_copy(
                out=kTg[:, gk[0] * T + gk[1] * CH:
                        gk[0] * T + (gk[1] + 1) * CH],
                in_=ps[:]),
        )

        sch = Sched(nc)

        def ones_cols(gp):
            inst = None
            for kt in range(NKT):
                for hh in range(HG):
                    inst = gp.memset(vaug[:, kt * VW + hh * 65 + 64:
                                          kt * VW + hh * 65 + 65], 1.0)
            return inst
        vones_tok = sch.op("gpsimd", ones_cols, inc=env.s_pl)
        sch.run(f"vones{g}")

        def v_evac(ve, kt, ps):
            inst = None
            for hh in range(HG):
                inst = ve.tensor_copy(
                    out=vaug[:, kt * VW + hh * 65: kt * VW + hh * 65 + 64],
                    in_=ps[:, hh * 64:(hh + 1) * 64])
            return inst
        vtoks = _proj_phase(
            env, f"v{g}", list(range(NKT)), 2,
            lambda kt, k: env.c_kvT[k][:, kt * 128:(kt + 1) * 128],
            lambda kt, k: wuv_bf[:, k * D + g * 256: k * D + (g + 1) * 256],
            v_evac, psum_n=HG * 64,
        )

        bias_tok = None
        with nc.psum_tensor(f"psn4_{g}", [4, CH], F32) as psn4, \
             nc.psum_tensor(f"psb4_{g}", [128, 4], F32) as psb4:
            ksq = [ksq0, ksq1]
            sch = Sched(nc)
            km0 = sch.op("vector", lambda ve: ve.memset(kmax4[:], 0.0),
                         inc=env.s_dv)
            i = 0
            prev_red = None
            prev_mx = km0
            sq_list = []
            for chq in range(NCH):
                grp_tok = None
                for p in range(2):
                    deps = [sq_list[i - 2][1]] if i >= 2 else []
                    st_ = sch.op(
                        "scalar", lambda sc, i=i, p=p, chq=chq:
                        sc.activation(
                            out=ksq[i % 2][:],
                            in_=kTg[:, p * T + chq * CH:
                                    p * T + (chq + 1) * CH],
                            func=AF.Square),
                        deps=deps + [ktoks[(p, chq)]], inc=env.s_ac)
                    mdeps = [st_, env.consts_tok]
                    if p == 0 and prev_red is not None:
                        mdeps.append(prev_red)
                    mt = sch.op("tensor", lambda te, i=i, p=p:
                                te.matmul(psn4[:],
                                          env.sel8h[:, p * 4:(p + 1) * 4],
                                          ksq[i % 2][:],
                                          start=(p == 0), stop=(p == 1)),
                                deps=mdeps, inc=env.s_pe)
                    sq_list.append((st_, mt))
                    grp_tok = mt
                    i += 1
                rd = sch.op("vector", lambda ve: ve.reduce_max(
                    out=ktmp4[:], in_=psn4[:], axis=AXX),
                    deps=[grp_tok, prev_mx], inc=env.s_dv)
                mx = sch.op("vector", lambda ve: ve.tensor_max(
                    out=kmax4[:], in0=kmax4[:], in1=ktmp4[:]),
                    deps=[rd], inc=env.s_dv)
                prev_red = rd
                prev_mx = mx
            qg = sch.op("sync", lambda sy: sy.dma_start(
                out=env.qmaxG[:], in_=env.qmax16[g * 4:(g + 1) * 4, :],
                allow_slow_non_contiguous=True), inc=env.s_msc, inck=16)
            t1 = sch.op("vector", lambda ve: ve.tensor_mul(
                out=ktmp4[:], in0=kmax4[:], in1=env.qmaxG[:]),
                deps=[prev_mx, prev_red, qg], inc=env.s_dv)
            t2 = sch.op("scalar", lambda sc: sc.activation(
                out=ktmp4[:], in_=ktmp4[:], func=AF.Sqrt),
                deps=[t1], inc=env.s_ac)
            t3 = sch.op("vector", lambda ve: ve.tensor_scalar_mul(
                out=ktmp4[:], in0=ktmp4[:], scalar1=-0.125),
                deps=[t2], inc=env.s_dv)
            t4 = sch.op("sync", lambda sy: sy.dma_start(
                out=brow[:], in_=ktmp4[:], allow_slow_non_contiguous=True),
                deps=[t3], inc=env.s_msc, inck=16)
            t5 = sch.op("tensor", lambda te: te.matmul(
                psb4[:], env.ones_f1[0:1, :], brow[:], start=True, stop=True),
                deps=[t4], inc=env.s_pe)
            bias_tok = sch.op("vector", lambda ve: ve.tensor_copy(
                out=bias4[:], in_=psb4[:]), deps=[t5], inc=env.s_dv)
            sch.run(f"knrm{g}")

        _attn_loop(env, g, kTg, vaug, bias4, probT, probM, recipS, rbS,
                   ctxs, ktoks, vtoks, vones_tok, bias_tok)


def _attn_loop(env, g, kTg, vaug, bias4, probT, probM, recipS, rbS,
               ctxs, ktoks, vtoks, vones_tok, bias_tok):
    nc = env.nc
    T, NCH, NKT = env.T, env.NCH, env.NKT
    VW = HG * 65
    PW = 2 * CH
    jobs = [(hh, qc) for hh in range(HG) for qc in range(NCH)]
    with nc.psum_tensor(f"ps_s0_{g}", [128, PW], F32) as ps_s0, \
         nc.psum_tensor(f"ps_s1_{g}", [128, PW], F32) as ps_s1, \
         nc.psum_tensor(f"ps_c0_{g}", [65, CH], F32) as ps_c0, \
         nc.psum_tensor(f"ps_c1_{g}", [65, CH], F32) as ps_c1, \
         nc.psum_tensor(f"ps_r0_{g}", [64, CH], F32) as ps_r0, \
         nc.psum_tensor(f"ps_r1_{g}", [64, CH], F32) as ps_r1:
        ps_s = [ps_s0, ps_s1]
        ps_c = [ps_c0, ps_c1]
        ps_r = [ps_r0, ps_r1]
        sch = Sched(nc)
        pslot_free = [[], [], []]
        mslot_free = [None, None, None, None]
        exp_toks = {}
        mask_toks = {}
        mask_slot = {}
        pv_toks = {}
        job_dve = {}
        job_bc = {}
        gp_ = 0
        gt = 0
        d = 0
        for j, (hh, qc) in enumerate(jobs):
            nkt = min(NKT, 4 * (qc + 1))
            npair = nkt // 2
            h_glob = g * HG + hh
            pt = hh // 2
            rb = 64 * (hh % 2)
            qtile = h_glob // 2
            qb = 64 * (h_glob % 2)
            base_gt = gt
            base_gp = gp_
            for i in range(npair + 1):
                if i < npair:
                    pr = base_gp + i
                    sct = None
                    for half in range(2):
                        kt = 2 * i + half
                        deps = [exp_toks.get(pr - 2), ktoks[(pt, kt // 4)]]
                        sct = sch.op(
                            "tensor",
                            lambda te, pt=pt, rb=rb, kt=kt, qc=qc, qtile=qtile,
                            qb=qb, pr=pr, half=half:
                            te.matmul(
                                ps_s[pr % 2][:, half * CH:(half + 1) * CH],
                                kTg[rb:rb + 64,
                                    pt * T + kt * 128: pt * T + (kt + 1) * 128],
                                env.qT[qtile][qb:qb + 64, qc * CH:(qc + 1) * CH],
                                start=True, stop=True),
                            deps=deps, inc=env.s_pe)
                    edeps = [sct, bias_tok] + pslot_free[pr % 3]
                    et = sch.op(
                        "scalar", lambda sc, pr=pr, hh=hh:
                        sc.activation(
                            out=probT[:, (pr % 3) * PW:(pr % 3 + 1) * PW],
                            in_=ps_s[pr % 2][:], func=AF.Exp,
                            scale=0.125, bias=bias4[:, hh:hh + 1]),
                        deps=edeps, inc=env.s_ac)
                    exp_toks[pr] = et
                    pslot_free[pr % 3] = []
                    for half in range(2):
                        kt = 2 * i + half
                        t_this = base_gt + kt
                        if kt >= nkt - 4:
                            mdeps = [et, mslot_free[d % 4]]
                            mt = sch.op(
                                "gpsimd",
                                lambda gp, pr=pr, half=half, d=d, qc=qc, kt=kt:
                                gp.affine_select(
                                    out=probM[:, (d % 4) * CH:(d % 4 + 1) * CH],
                                    in_=probT[:, (pr % 3) * PW + half * CH:
                                              (pr % 3) * PW + (half + 1) * CH],
                                    pattern=[[1, CH]], compare_op=ALU.is_ge,
                                    fill=0.0, base=qc * CH - kt * 128,
                                    channel_multiplier=-1),
                                deps=mdeps, inc=env.s_pl)
                            mask_toks[t_this] = mt
                            mask_slot[t_this] = d % 4
                            pslot_free[pr % 3].append(mt)
                            d += 1
                if i >= 1:
                    ip = i - 1
                    pr = base_gp + ip
                    for half in range(2):
                        ii = 2 * ip + half
                        t_pv = base_gt + ii
                        diag = ii >= nkt - 4
                        deps = [vtoks[ii], vones_tok]
                        if diag:
                            deps.append(mask_toks[t_pv])
                            rhs_slot = ('m', mask_slot[t_pv])
                        else:
                            deps.append(exp_toks[pr])
                            rhs_slot = ('t', (pr % 3, half))
                        if ii == 0 and j >= 2:
                            deps.append(job_dve[j - 2][2])

                        def mk_pv(ii=ii, j=j, hh=hh, nkt=nkt, rhs_slot=rhs_slot):
                            def f(te):
                                if rhs_slot[0] == 'm':
                                    rhs = probM[:, rhs_slot[1] * CH:
                                                (rhs_slot[1] + 1) * CH]
                                else:
                                    sl, hf = rhs_slot[1]
                                    rhs = probT[:, sl * PW + hf * CH:
                                                sl * PW + (hf + 1) * CH]
                                return te.matmul(
                                    ps_c[j % 2][:],
                                    vaug[:, ii * VW + hh * 65:
                                         ii * VW + hh * 65 + 65],
                                    rhs, start=(ii == 0), stop=(ii == nkt - 1))
                            return f
                        pt_ = sch.op("tensor", mk_pv(), deps=deps, inc=env.s_pe)
                        pv_toks[t_pv] = pt_
                        if diag:
                            mslot_free[mask_slot[t_pv]] = pt_
                        else:
                            pslot_free[pr % 3].append(pt_)
            gt = base_gt + nkt
            gp_ = base_gp + npair
            last_pv = pv_toks[gt - 1]
            rdeps = [last_pv]
            if j >= 2:
                rdeps.append(job_bc[j - 2])
            r1 = sch.op("vector", lambda ve, j=j:
                        ve.reciprocal(
                            out=recipS[0:1, (j % 2) * CH:(j % 2 + 1) * CH],
                            in_=ps_c[j % 2][64:65, :]),
                        deps=rdeps, inc=env.s_dv)
            bdeps = [r1]
            if j >= 2:
                bdeps.append(job_dve[j - 2][1])
            b1 = sch.op("tensor", lambda te, j=j:
                        te.matmul(ps_r[j % 2][:], env.ones_f1[0:1, 0:64],
                                  recipS[0:1, (j % 2) * CH:(j % 2 + 1) * CH],
                                  start=True, stop=True),
                        deps=bdeps, inc=env.s_pe)
            job_bc[j] = b1
            e1 = sch.op("vector", lambda ve, j=j:
                        ve.tensor_copy(
                            out=rbS[:, (j % 2) * CH:(j % 2 + 1) * CH],
                            in_=ps_r[j % 2][:]),
                        deps=[b1], inc=env.s_dv)
            cdeps = [e1]
            if j >= 2:
                cdeps.append(job_dve[j - 2][3])
            c1 = sch.op("vector",
                        lambda ve, j=j:
                        ve.tensor_mul(
                            out=ctxs[:, (j % 2) * CH:(j % 2 + 1) * CH],
                            in0=ps_c[j % 2][0:64, :],
                            in1=rbS[:, (j % 2) * CH:(j % 2 + 1) * CH]),
                        deps=cdeps, inc=env.s_dv)
            cd = sch.op("gpsimd",
                        lambda gp, j=j, h_glob=h_glob, qc=qc:
                        gp.dma_start(
                            out=env.ctxh[h_glob * 64:(h_glob + 1) * 64,
                                         qc * CH:(qc + 1) * CH],
                            in_=ctxs[:, (j % 2) * CH:(j % 2 + 1) * CH]),
                        deps=[c1], inc=env.s_cd[j % 2], inck=16)
            job_dve[j] = (r1, e1, c1, cd)
        sch.drain(env.s_cd[0])
        sch.drain(env.s_cd[1])
        sch.run(f"attn{g}")


def _p6(env, wo_bf, rw_bf):
    nc = env.nc
    T, NCH = env.T, env.NCH
    with nc.sbuf_tensor("ctxr", [128, 16 * CH], BF16) as ctxr, \
         nc.sbuf_tensor("xst", [128, 16 * CH], F32) as xst, \
         nc.sbuf_tensor("resid", [128, 8 * CH], F32) as resid, \
         nc.sbuf_tensor("rsq", [128, 2 * CH], F32) as rsq, \
         nc.sbuf_tensor("x1bs", [128, 8 * CH], BF16) as x1bs, \
         nc.sbuf_tensor("ab_sb", [128, 2 * CH], F32) as ab_sb, \
         nc.sbuf_tensor("rows", [1, 6 * CH], F32) as rows, \
         nc.sbuf_tensor("cwtile", [128, E], F32) as cwtile, \
         nc.sbuf_tensor("cwmask", [128, E], F32) as cwmask, \
         nc.sbuf_tensor("cwrow", [128, 5], F32) as cwrow, \
         nc.psum_tensor("ps_o0", [128, CH], F32) as ps_o0, \
         nc.psum_tensor("ps_o1", [128, CH], F32) as ps_o1, \
         nc.psum_tensor("ps_sum", [1, CH], F32) as ps_sum, \
         nc.psum_tensor("ps_ssq", [1, CH], F32) as ps_ssq, \
         nc.psum_tensor("ps_A", [128, CH], F32) as ps_A, \
         nc.psum_tensor("ps_B", [128, CH], F32) as ps_B, \
         nc.psum_tensor("ps_lg", [128, E], F32) as ps_lg, \
         nc.psum_tensor("ps_ct", [8, 128], F32) as ps_ct:
        ps_o = [ps_o0, ps_o1]
        sch = Sched(nc)
        prev = {}
        for chq in range(NCH):
            st = {}
            ctoks = []
            for kd in range(8):
                deps = []
                if chq >= 2:
                    deps.append(prev[chq - 2]['wo'][7])
                ctoks.append(sch.op(
                    "sync", lambda sy, kd=kd, chq=chq:
                    sy.dma_start(
                        out=ctxr[:, ((chq % 2) * 8 + kd) * CH:
                                 ((chq % 2) * 8 + kd + 1) * CH],
                        in_=env.ctxh[kd * 128:(kd + 1) * 128,
                                     chq * CH:(chq + 1) * CH]),
                    deps=deps, inc=env.s_cx[chq % 2], inck=16))
            xtoks = []
            for m in range(8):
                deps = []
                if chq >= 2:
                    deps.append(prev[chq - 2]['ra'][m])
                xtoks.append(sch.op(
                    "sync", lambda sy, m=m, chq=chq:
                    sy.dma_start(
                        out=xst[:, ((chq % 2) * 8 + m) * CH:
                                ((chq % 2) * 8 + m + 1) * CH],
                        in_=env.xT[m * 128:(m + 1) * 128,
                                   chq * CH:(chq + 1) * CH]),
                    deps=deps, inc=env.s_xs[chq % 2], inck=16))
            ra = [None] * 8
            sq = [None] * 8
            sums = [None] * 8
            ssqs = [None] * 8
            wo_toks = [None] * 8
            for m in range(8):
                deps = [ctoks[-1]]
                if chq * 8 + m >= 2:
                    pm = chq * 8 + m - 2
                    deps.append(ra[pm % 8] if pm // 8 == chq
                                else prev[pm // 8]['ra'][pm % 8])
                wo_toks[m] = sch.op(
                    "tensor", lambda te, m=m, chq=chq:
                    [te.matmul(ps_o[m % 2][:],
                               wo_bf[:, kd * D + m * 128:
                                     kd * D + (m + 1) * 128],
                               ctxr[:, ((chq % 2) * 8 + kd) * CH:
                                    ((chq % 2) * 8 + kd + 1) * CH],
                               start=(kd == 0), stop=(kd == 7))
                     for kd in range(8)][-1],
                    deps=deps, inc=env.s_pe)
                rdeps = [wo_toks[m], xtoks[-1]]
                if chq >= 1:
                    rdeps.append(prev[chq - 1]['x1bdma'][7])
                    rdeps.append(prev[chq - 1]['sum'][m])
                    rdeps.append(prev[chq - 1]['lg_last'])
                ra[m] = sch.op(
                    "vector", lambda ve, m=m, chq=chq:
                    ve.tensor_add(
                        out=resid[:, m * CH:(m + 1) * CH],
                        in0=ps_o[m % 2][:],
                        in1=xst[:, ((chq % 2) * 8 + m) * CH:
                                ((chq % 2) * 8 + m + 1) * CH]),
                    deps=rdeps, inc=env.s_dv)
                sdeps = [ra[m]]
                if m >= 2:
                    sdeps.append(ssqs[m - 2])
                elif chq >= 1:
                    sdeps.append(prev[chq - 1]['ssq'][6 + m])
                sq[m] = sch.op(
                    "scalar", lambda sc, m=m:
                    sc.activation(out=rsq[:, (m % 2) * CH:(m % 2 + 1) * CH],
                                  in_=resid[:, m * CH:(m + 1) * CH],
                                  func=AF.Square),
                    deps=sdeps, inc=env.s_ac)
                sums[m] = sch.op(
                    "tensor", lambda te, m=m:
                    te.matmul(ps_sum[:], env.ones_fc[:],
                              resid[:, m * CH:(m + 1) * CH],
                              start=(m == 0), stop=(m == 7)),
                    deps=[ra[m]] + ([prev[chq - 1]['rows0']]
                                    if (m == 0 and chq >= 1) else []),
                    inc=env.s_pe)
                ssqs[m] = sch.op(
                    "tensor", lambda te, m=m:
                    te.matmul(ps_ssq[:], env.ones_fc[:],
                              rsq[:, (m % 2) * CH:(m % 2 + 1) * CH],
                              start=(m == 0), stop=(m == 7)),
                    deps=[sq[m]] + ([prev[chq - 1]['rows1']]
                                    if (m == 0 and chq >= 1) else []),
                    inc=env.s_pe)
            c1 = sch.op("vector", lambda ve: ve.tensor_scalar_mul(
                out=rows[0:1, 0:CH], in0=ps_sum[:], scalar1=1.0 / D),
                deps=[sums[7]], inc=env.s_dv)
            c2 = sch.op("vector", lambda ve: ve.tensor_mul(
                out=rows[0:1, CH:2 * CH], in0=rows[0:1, 0:CH],
                in1=rows[0:1, 0:CH]), deps=[c1], inc=env.s_dv)
            c3 = sch.op("vector", lambda ve: ve.tensor_scalar_mul(
                out=rows[0:1, 4 * CH:5 * CH], in0=ps_ssq[:],
                scalar1=1.0 / D), deps=[ssqs[7]], inc=env.s_dv)
            c4 = sch.op("vector", lambda ve: ve.tensor_sub(
                out=rows[0:1, CH:2 * CH], in0=rows[0:1, 4 * CH:5 * CH],
                in1=rows[0:1, CH:2 * CH]), deps=[c2, c3], inc=env.s_dv)
            c5 = sch.op("scalar", lambda sc: sc.activation(
                out=rows[0:1, 5 * CH:6 * CH], in_=rows[0:1, CH:2 * CH],
                func=AF.Sqrt, bias=env.eps_t[:]), deps=[c4], inc=env.s_ac)
            c6 = sch.op("vector", lambda ve: ve.reciprocal(
                out=rows[0:1, 2 * CH:3 * CH], in_=rows[0:1, 5 * CH:6 * CH]),
                deps=[c5], inc=env.s_dv)
            c7 = sch.op("vector", lambda ve: ve.tensor_mul(
                out=rows[0:1, 3 * CH:4 * CH], in0=rows[0:1, 0:CH],
                in1=rows[0:1, 2 * CH:3 * CH]), deps=[c6], inc=env.s_dv)
            c8 = sch.op("vector", lambda ve: ve.tensor_scalar_mul(
                out=rows[0:1, 3 * CH:4 * CH], in0=rows[0:1, 3 * CH:4 * CH],
                scalar1=-1.0), deps=[c7], inc=env.s_dv)
            st['rows0'] = c1
            st['rows1'] = c3
            bdeps_extra = [prev[chq - 1]['abev']] if chq >= 1 else []
            bA = sch.op("tensor", lambda te: te.matmul(
                ps_A[:], env.ones_f1[0:1, :], rows[0:1, 2 * CH:3 * CH],
                start=True, stop=True), deps=[c6] + bdeps_extra, inc=env.s_pe)
            bB = sch.op("tensor", lambda te: te.matmul(
                ps_B[:], env.ones_f1[0:1, :], rows[0:1, 3 * CH:4 * CH],
                start=True, stop=True), deps=[c8] + bdeps_extra, inc=env.s_pe)
            evdeps = [bA, bB]
            if chq >= 1:
                evdeps.append(prev[chq - 1]['norm_last'])
            abev = sch.op("vector", lambda ve: [
                ve.tensor_copy(out=ab_sb[:, 0:CH], in_=ps_A[:]),
                ve.tensor_copy(out=ab_sb[:, CH:2 * CH], in_=ps_B[:])][-1],
                deps=evdeps, inc=env.s_dv)
            st['abev'] = abev
            norm_toks = [None] * 8
            cast2 = [None] * 8
            for m in range(8):
                n1 = sch.op("vector", lambda ve, m=m: ve.tensor_mul(
                    out=rsq[:, (m % 2) * CH:(m % 2 + 1) * CH],
                    in0=resid[:, m * CH:(m + 1) * CH], in1=ab_sb[:, 0:CH]),
                    deps=[abev, sq[m], sums[m], ssqs[m]], inc=env.s_dv)
                n2 = sch.op("vector", lambda ve, m=m: ve.tensor_add(
                    out=resid[:, m * CH:(m + 1) * CH],
                    in0=rsq[:, (m % 2) * CH:(m % 2 + 1) * CH],
                    in1=ab_sb[:, CH:2 * CH]), deps=[n1], inc=env.s_dv)
                n3 = sch.op("vector", lambda ve, m=m: ve.tensor_copy(
                    out=x1bs[:, m * CH:(m + 1) * CH],
                    in_=resid[:, m * CH:(m + 1) * CH]), deps=[n2],
                    inc=env.s_dv)
                norm_toks[m] = n2
                cast2[m] = n3
            st['norm_last'] = cast2[7]
            st['ra'] = ra
            st['sum'] = sums
            st['ssq'] = ssqs
            x1fdma = [None] * 8
            x1bdma = [None] * 8
            for m in range(8):
                x1fdma[m] = sch.op(
                    "gpsimd", lambda gp, m=m, chq=chq:
                    gp.dma_start(out=env.x1f[m * 128:(m + 1) * 128,
                                             chq * CH:(chq + 1) * CH],
                                 in_=resid[:, m * CH:(m + 1) * CH]),
                    deps=[norm_toks[m]], inc=env.s_xd[chq % 2], inck=16)
                x1bdma[m] = sch.op(
                    "gpsimd", lambda gp, m=m, chq=chq:
                    gp.dma_start(out=env.x1b[m * 128:(m + 1) * 128,
                                             chq * CH:(chq + 1) * CH],
                                 in_=x1bs[:, m * CH:(m + 1) * CH]),
                    deps=[cast2[m]], inc=env.s_xd[chq % 2], inck=16)
            st['x1fdma'] = x1fdma
            st['x1bdma'] = x1bdma
            st['wo'] = wo_toks
            lg_tok = None
            for tt in range(4):
                ldeps = [cast2[kd] for kd in range(8)]
                if tt >= 1:
                    ldeps += [st[f'exp{tt-1}'], st[f'm1red{tt-1}']]
                elif chq >= 1:
                    ldeps += [prev[chq - 1]['exp3'], prev[chq - 1]['m1red3']]
                lg_tok = sch.op(
                    "tensor", lambda te, tt=tt:
                    [te.matmul(ps_lg[:],
                               x1bs[:, kd * CH + tt * 128:
                                    kd * CH + (tt + 1) * 128],
                               rw_bf[:, kd * E:(kd + 1) * E],
                               start=(kd == 0), stop=(kd == 7))
                     for kd in range(8)][-1],
                    deps=ldeps, inc=env.s_pe)
                m1 = sch.op("vector", lambda ve: ve.reduce_max(
                    out=cwrow[:, 0:1], in_=ps_lg[:], axis=AXX),
                    deps=[lg_tok], inc=env.s_dv)
                m1n = sch.op("vector", lambda ve: ve.tensor_scalar_mul(
                    out=cwrow[:, 0:1], in0=cwrow[:, 0:1], scalar1=-1.0),
                    deps=[m1], inc=env.s_dv)
                st[f'm1red{tt}'] = m1n
                exdeps = [m1n]
                if tt >= 1:
                    exdeps.append(st[f'cwev{tt-1}'])
                elif chq >= 1:
                    exdeps.append(prev[chq - 1]['cwev3'])
                ex = sch.op("scalar", lambda sc: sc.activation(
                    out=cwtile[:], in_=ps_lg[:], func=AF.Exp,
                    bias=cwrow[:, 0:1]), deps=exdeps, inc=env.s_ac)
                st[f'exp{tt}'] = ex
                k1 = sch.op("vector", lambda ve: ve.reduce_max(
                    out=cwrow[:, 1:2], in_=cwtile[:], axis=AXX),
                    deps=[ex], inc=env.s_dv)
                k2 = sch.op("vector", lambda ve: ve.tensor_scalar(
                    out=cwmask[:], in0=cwtile[:], scalar1=cwrow[:, 1:2],
                    scalar2=-4.0, op0=ALU.is_ge, op1=ALU.mult),
                    deps=[k1], inc=env.s_dv)
                k3 = sch.op("vector", lambda ve: ve.tensor_add(
                    out=cwmask[:], in0=cwmask[:], in1=cwtile[:]),
                    deps=[k2], inc=env.s_dv)
                k4 = sch.op("vector", lambda ve: ve.reduce_max(
                    out=cwrow[:, 2:3], in_=cwmask[:], axis=AXX),
                    deps=[k3], inc=env.s_dv)
                k5 = sch.op("vector", lambda ve: ve.tensor_tensor(
                    out=cwrow[:, 3:4], in0=cwrow[:, 1:2], in1=cwrow[:, 2:3],
                    op=ALU.add), deps=[k4], inc=env.s_dv)
                k6 = sch.op("vector", lambda ve: ve.reciprocal(
                    out=cwrow[:, 4:5], in_=cwrow[:, 3:4]),
                    deps=[k5], inc=env.s_dv)
                k7 = sch.op("vector", lambda ve: ve.tensor_scalar(
                    out=cwmask[:], in0=cwtile[:], scalar1=cwrow[:, 2:3],
                    scalar2=None, op0=ALU.is_ge), deps=[k6, k4], inc=env.s_dv)
                k8 = sch.op("vector", lambda ve: ve.tensor_mul(
                    out=cwtile[:], in0=cwtile[:], in1=cwmask[:]),
                    deps=[k7], inc=env.s_dv)
                k9 = sch.op("vector", lambda ve: ve.tensor_scalar_mul(
                    out=cwtile[:], in0=cwtile[:], scalar1=cwrow[:, 4:5]),
                    deps=[k8], inc=env.s_dv)
                tdeps = [k9, env.consts_tok]
                if tt >= 1:
                    tdeps.append(st[f'cwev{tt-1}'])
                elif chq >= 1:
                    tdeps.append(prev[chq - 1]['cwev3'])
                tr = sch.op("tensor", lambda te: te.matmul(
                    ps_ct[:], cwtile[:], env.ident[:],
                    start=True, stop=True), deps=tdeps, inc=env.s_pe)
                cwev = sch.op(
                    "vector", lambda ve, chq=chq, tt=tt:
                    ve.tensor_copy(out=env.cwT[:, chq * CH + tt * 128:
                                               chq * CH + (tt + 1) * 128],
                                   in_=ps_ct[:]), deps=[tr], inc=env.s_dv)
                st[f'cwev{tt}'] = cwev
            st['lg_last'] = lg_tok
            prev[chq] = st
        sch.drain(env.s_xd[0])
        sch.drain(env.s_xd[1])
        sch.run("p6")


def _moe_pass(env, th):
    nc = env.nc
    T, THL, NTCH = env.T, env.THL, env.NTCH
    WB = env.wsets
    final_acc = {}
    with nc.sbuf_tensor(f"acc_{th}", [128, 8 * THL], F32) as accT, \
         nc.sbuf_tensor(f"x1h_{th}", [128, 8 * THL], BF16) as x1h:
        with nc.sbuf_tensor(f"w1b_{th}", [128, WB * 8 * 1024], BF16) as w1b, \
             nc.sbuf_tensor(f"w2b_{th}", [128, WB * 8 * 1024], BF16) as w2b, \
             nc.sbuf_tensor(f"hsb_{th}", [128, 8 * THL], BF16) as hsb, \
             nc.sbuf_tensor(f"stgw0_{th}", [128, 1024], F32) as stgw0, \
             nc.sbuf_tensor(f"stgw1_{th}", [128, 1024], F32) as stgw1, \
             nc.sbuf_tensor(f"cwb_{th}", [128, NTCH * CH], BF16) as cwb, \
             nc.psum_tensor(f"ph0_{th}", [128, CH], F32) as ph0, \
             nc.psum_tensor(f"ph1_{th}", [128, CH], F32) as ph1, \
             nc.psum_tensor(f"py0_{th}", [128, CH], F32) as py0, \
             nc.psum_tensor(f"py1_{th}", [128, CH], F32) as py1, \
             nc.psum_tensor(f"pcw_{th}", [128, CH], F32) as pcw:
            stgw = [stgw0, stgw1]
            ps_h = [ph0, ph1]
            ps_y = [py0, py1]
            sch = Sched(nc)
            x1toks = []
            for m in range(8):
                x1toks.append(sch.op(
                    "sync", lambda sy, m=m, th=th:
                    sy.dma_start(out=x1h[:, m * THL:(m + 1) * THL],
                                 in_=env.x1b[m * 128:(m + 1) * 128,
                                             th * THL:(th + 1) * THL]),
                    inc=env.s_h1, inck=16))
            x1last = x1toks[-1]
            acc_clear = []
            for m in range(8):
                acc_clear.append(sch.op(
                    "gpsimd", lambda gp, m=m:
                    gp.memset(accT[:, m * THL:(m + 1) * THL], 0.0),
                    inc=env.s_pl))

            cast_tok = {}
            step_hgrp = {}
            step_silu = {}
            step_scale = {}
            step_ygrp = {}
            step_acc = {}
            cwb_ev = {}
            dma_i = 0
            cast_list = []
            for s in range(16):
                e, fh = s // 2, s % 2
                for widx in range(16):
                    isw1 = widx < 8
                    kk = widx % 8
                    if isw1:
                        src = env.w1[e, kk * 128:(kk + 1) * 128,
                                     fh * 1024:(fh + 1) * 1024]
                    else:
                        src = env.w2[e, fh * 1024 + kk * 128:
                                     fh * 1024 + (kk + 1) * 128, :]
                    deps = [cast_list[dma_i - 2]] if dma_i >= 2 else []
                    dt_ = sch.op("sync", lambda sy, src=src, dma_i=dma_i:
                                 sy.dma_start(out=stgw[dma_i % 2][:], in_=src),
                                 deps=deps, inc=env.s_stg[dma_i % 2], inck=16)
                    cdeps = [dt_]
                    if s >= WB:
                        if isw1:
                            cdeps.append(step_hgrp[(s - WB, 7, NTCH - 1)])
                        else:
                            cdeps.append(step_ygrp[(s - WB, 7, NTCH - 1)])
                    dst = w1b if isw1 else w2b
                    ws = (s % WB) * 8
                    ct = sch.op(
                        "vector", lambda ve, dst=dst, kk=kk, dma_i=dma_i, ws=ws:
                        ve.tensor_copy(
                            out=dst[:, (ws + kk) * 1024:(ws + kk + 1) * 1024],
                            in_=stgw[dma_i % 2][:]),
                        deps=cdeps, inc=env.s_dv)
                    cast_tok[(s, 'w1' if isw1 else 'w2', kk)] = ct
                    cast_list.append(ct)
                    dma_i += 1
                if fh == 0:
                    for tch in range(NTCH):
                        deps = [env.consts_tok]
                        if tch >= 1:
                            deps.append(cwb_ev[(e, tch - 1)])
                        if e >= 1:
                            deps += [step_scale[(2 * e - 1, fm, tch)]
                                     for fm in range(8)]
                            deps.append(cwb_ev[(e - 1, NTCH - 1)])
                        bc = sch.op(
                            "tensor", lambda te, e=e, tch=tch, th=th:
                            te.matmul(pcw[:],
                                      env.sel8e[:, e * 128:(e + 1) * 128],
                                      env.cwT[0:8, th * THL + tch * CH:
                                              th * THL + (tch + 1) * CH],
                                      start=True, stop=True),
                            deps=deps, inc=env.s_pe)
                        cwb_ev[(e, tch)] = sch.op(
                            "vector", lambda ve, tch=tch:
                            ve.tensor_copy(out=cwb[:, tch * CH:(tch + 1) * CH],
                                           in_=pcw[:]),
                            deps=[bc], inc=env.s_dv)
                w1last = cast_tok[(s, 'w1', 7)]
                hidx = 0
                for fm in range(8):
                    for tch in range(NTCH):
                        deps = [w1last, x1last]
                        if hidx >= 2:
                            deps.append(step_silu[(s, (hidx - 2) // NTCH,
                                                  (hidx - 2) % NTCH)])
                        elif s >= 1:
                            php = 8 * NTCH - 2 + hidx
                            deps.append(step_silu[(s - 1, php // NTCH,
                                                  php % NTCH)])
                        ht = sch.op(
                            "tensor", lambda te, s=s, fm=fm, tch=tch,
                            hidx=hidx, WB=WB:
                            [te.matmul(
                                ps_h[hidx % 2][:],
                                env_w1b_slice(w1b, s, kd, fm, WB),
                                x1h[:, kd * THL + tch * CH:
                                    kd * THL + (tch + 1) * CH],
                                start=(kd == 0), stop=(kd == 7))
                             for kd in range(8)][-1],
                            deps=deps, inc=env.s_pe)
                        step_hgrp[(s, fm, tch)] = ht
                        sdeps = [ht]
                        if s >= 1:
                            sdeps.append(step_scale[(s - 1, fm, tch)])
                            sdeps.append(step_ygrp[(s - 1, 7, tch)])
                        stk = sch.op(
                            "scalar", lambda sc, fm=fm, tch=tch, hidx=hidx:
                            sc.activation(
                                out=hsb[:, fm * THL + tch * CH:
                                        fm * THL + (tch + 1) * CH],
                                in_=ps_h[hidx % 2][:], func=AF.Silu),
                            deps=sdeps, inc=env.s_ac)
                        step_silu[(s, fm, tch)] = stk
                        ptk = sch.op(
                            "gpsimd", lambda gp, fm=fm, tch=tch:
                            gp.tensor_mul(
                                out=hsb[:, fm * THL + tch * CH:
                                        fm * THL + (tch + 1) * CH],
                                in0=hsb[:, fm * THL + tch * CH:
                                        fm * THL + (tch + 1) * CH],
                                in1=cwb[:, tch * CH:(tch + 1) * CH]),
                            deps=[stk, cwb_ev[(e, tch)]], inc=env.s_pl)
                        step_scale[(s, fm, tch)] = ptk
                        hidx += 1
                w2last = cast_tok[(s, 'w2', 7)]
                yidx = 0
                for tch in range(NTCH):
                    scale_deps = [step_scale[(s, fm, tch)] for fm in range(8)]
                    for ym in range(8):
                        deps = [w2last] + scale_deps
                        if yidx >= 2:
                            deps.append(step_acc[(s, (yidx - 2) % 8,
                                                 (yidx - 2) // 8)])
                        elif s >= 1:
                            pyp = 8 * NTCH - 2 + yidx
                            deps.append(step_acc[(s - 1, pyp % 8, pyp // 8)])
                        yt = sch.op(
                            "tensor", lambda te, s=s, ym=ym, tch=tch,
                            yidx=yidx, WB=WB:
                            [te.matmul(
                                ps_y[yidx % 2][:],
                                env_w1b_slice(w2b, s, kf, ym, WB),
                                hsb[:, kf * THL + tch * CH:
                                    kf * THL + (tch + 1) * CH],
                                start=(kf == 0), stop=(kf == 7))
                             for kf in range(8)][-1],
                            deps=deps, inc=env.s_pe)
                        step_ygrp[(s, ym, tch)] = yt
                        adeps = [yt]
                        if s >= 1:
                            adeps.append(step_acc[(s - 1, ym, tch)])
                        else:
                            adeps.append(acc_clear[ym])
                        at = sch.op(
                            "vector", lambda ve, ym=ym, tch=tch, yidx=yidx:
                            ve.tensor_add(
                                out=accT[:, ym * THL + tch * CH:
                                         ym * THL + (tch + 1) * CH],
                                in0=accT[:, ym * THL + tch * CH:
                                         ym * THL + (tch + 1) * CH],
                                in1=ps_y[yidx % 2][:]),
                            deps=adeps, inc=env.s_dv)
                        step_acc[(s, ym, tch)] = at
                        yidx += 1
            for ym in range(8):
                for tch in range(NTCH):
                    final_acc[(ym, tch)] = step_acc[(15, ym, tch)]
            if getattr(env, 'dbg_moe', None) is not None:
                for m in range(8):
                    sch.op("gpsimd", lambda gp, m=m, th=th: gp.dma_start(
                        out=env.dbg_moe[m * 128:(m + 1) * 128,
                                        th * THL:(th + 1) * THL],
                        in_=accT[:, m * THL:(m + 1) * THL]),
                        deps=[step_acc[(15, m, tch)] for tch in range(NTCH)],
                        inc=env.s_dbg, inck=16)
            sch.drain(env.s_dbg)
            sch.run(f"moe{th}")

        _ln2(env, th, accT, final_acc)


def env_w1b_slice(wb, s, kd, fm, WB=1):
    base = ((s % WB) * 8 + kd) * 1024
    return wb[:, base + fm * 128: base + (fm + 1) * 128]


def _ln2(env, th, accT, final_acc):
    nc = env.nc
    THL, NTCH = env.THL, env.NTCH
    with nc.sbuf_tensor(f"xst2_{th}", [128, 16 * CH], F32) as xst2, \
         nc.sbuf_tensor(f"res2_{th}", [128, 8 * CH], F32) as res2, \
         nc.sbuf_tensor(f"rsq2_{th}", [128, 2 * CH], F32) as rsq2, \
         nc.sbuf_tensor(f"outf_{th}", [128, 8 * CH], F32) as outf, \
         nc.sbuf_tensor(f"ab2_{th}", [128, 2 * CH], F32) as ab2, \
         nc.sbuf_tensor(f"rows2_{th}", [1, 6 * CH], F32) as rows2, \
         nc.psum_tensor(f"ps2s_{th}", [1, CH], F32) as ps2s, \
         nc.psum_tensor(f"ps2q_{th}", [1, CH], F32) as ps2q, \
         nc.psum_tensor(f"ps2A_{th}", [128, CH], F32) as ps2A, \
         nc.psum_tensor(f"ps2B_{th}", [128, CH], F32) as ps2B:
        sch = Sched(nc)
        prev2 = {}
        for c in range(NTCH):
            st = {}
            chg = th * NTCH + c
            xtoks = []
            for m in range(8):
                deps = [final_acc[(m, c)]]
                if c >= 2:
                    deps.append(prev2[c - 2]['ra'][m])
                xtoks.append(sch.op(
                    "sync", lambda sy, m=m, c=c, chg=chg:
                    sy.dma_start(
                        out=xst2[:, ((c % 2) * 8 + m) * CH:
                                 ((c % 2) * 8 + m + 1) * CH],
                        in_=env.x1f[m * 128:(m + 1) * 128,
                                    chg * CH:(chg + 1) * CH]),
                    deps=deps, inc=env.s_xs[c % 2], inck=16))
            ra = [None] * 8
            sq = [None] * 8
            sums = [None] * 8
            ssqs = [None] * 8
            for m in range(8):
                rdeps = [xtoks[-1], final_acc[(m, c)]]
                if c >= 1:
                    rdeps.append(prev2[c - 1]['sum'][m])
                    rdeps.append(prev2[c - 1]['norm'][m])
                    if 'dbgres' in prev2[c - 1]:
                        rdeps.append(prev2[c - 1]['dbgres'][m])
                ra[m] = sch.op(
                    "vector", lambda ve, m=m, c=c:
                    ve.tensor_add(
                        out=res2[:, m * CH:(m + 1) * CH],
                        in0=accT[:, m * THL + c * CH: m * THL + (c + 1) * CH],
                        in1=xst2[:, ((c % 2) * 8 + m) * CH:
                                 ((c % 2) * 8 + m + 1) * CH]),
                    deps=rdeps, inc=env.s_dv)
                sdeps = [ra[m]]
                if m >= 2:
                    sdeps.append(ssqs[m - 2])
                elif c >= 1:
                    sdeps.append(prev2[c - 1]['ssq'][6 + m])
                sq[m] = sch.op(
                    "scalar", lambda sc, m=m:
                    sc.activation(out=rsq2[:, (m % 2) * CH:(m % 2 + 1) * CH],
                                  in_=res2[:, m * CH:(m + 1) * CH],
                                  func=AF.Square),
                    deps=sdeps, inc=env.s_ac)
                sums[m] = sch.op(
                    "tensor", lambda te, m=m:
                    te.matmul(ps2s[:], env.ones_fc[:],
                              res2[:, m * CH:(m + 1) * CH],
                              start=(m == 0), stop=(m == 7)),
                    deps=[ra[m]] + ([prev2[c - 1]['rows0']]
                                    if (m == 0 and c >= 1) else []),
                    inc=env.s_pe)
                ssqs[m] = sch.op(
                    "tensor", lambda te, m=m:
                    te.matmul(ps2q[:], env.ones_fc[:],
                              rsq2[:, (m % 2) * CH:(m % 2 + 1) * CH],
                              start=(m == 0), stop=(m == 7)),
                    deps=[sq[m]] + ([prev2[c - 1]['rows1']]
                                    if (m == 0 and c >= 1) else []),
                    inc=env.s_pe)
            c1 = sch.op("vector", lambda ve: ve.tensor_scalar_mul(
                out=rows2[0:1, 0:CH], in0=ps2s[:], scalar1=1.0 / D),
                deps=[sums[7]], inc=env.s_dv)
            c2 = sch.op("vector", lambda ve: ve.tensor_mul(
                out=rows2[0:1, CH:2 * CH], in0=rows2[0:1, 0:CH],
                in1=rows2[0:1, 0:CH]), deps=[c1], inc=env.s_dv)
            c3 = sch.op("vector", lambda ve: ve.tensor_scalar_mul(
                out=rows2[0:1, 4 * CH:5 * CH], in0=ps2q[:], scalar1=1.0 / D),
                deps=[ssqs[7]], inc=env.s_dv)
            c4 = sch.op("vector", lambda ve: ve.tensor_sub(
                out=rows2[0:1, CH:2 * CH], in0=rows2[0:1, 4 * CH:5 * CH],
                in1=rows2[0:1, CH:2 * CH]), deps=[c2, c3], inc=env.s_dv)
            c5 = sch.op("scalar", lambda sc: sc.activation(
                out=rows2[0:1, 5 * CH:6 * CH], in_=rows2[0:1, CH:2 * CH],
                func=AF.Sqrt, bias=env.eps_t[:]), deps=[c4], inc=env.s_ac)
            c6 = sch.op("vector", lambda ve: ve.reciprocal(
                out=rows2[0:1, 2 * CH:3 * CH], in_=rows2[0:1, 5 * CH:6 * CH]),
                deps=[c5], inc=env.s_dv)
            c7 = sch.op("vector", lambda ve: ve.tensor_mul(
                out=rows2[0:1, 3 * CH:4 * CH], in0=rows2[0:1, 0:CH],
                in1=rows2[0:1, 2 * CH:3 * CH]), deps=[c6], inc=env.s_dv)
            c8 = sch.op("vector", lambda ve: ve.tensor_scalar_mul(
                out=rows2[0:1, 3 * CH:4 * CH], in0=rows2[0:1, 3 * CH:4 * CH],
                scalar1=-1.0), deps=[c7], inc=env.s_dv)
            st['rows0'] = c1
            st['rows1'] = c3
            bdeps_extra = [prev2[c - 1]['abev']] if c >= 1 else []
            bA = sch.op("tensor", lambda te: te.matmul(
                ps2A[:], env.ones_f1[0:1, :], rows2[0:1, 2 * CH:3 * CH],
                start=True, stop=True), deps=[c6] + bdeps_extra, inc=env.s_pe)
            bB = sch.op("tensor", lambda te: te.matmul(
                ps2B[:], env.ones_f1[0:1, :], rows2[0:1, 3 * CH:4 * CH],
                start=True, stop=True), deps=[c8] + bdeps_extra, inc=env.s_pe)
            evdeps = [bA, bB]
            if c >= 1:
                evdeps.append(prev2[c - 1]['norm'][7])
                if 'dbgab' in prev2[c - 1]:
                    evdeps.append(prev2[c - 1]['dbgab'])
            abev = sch.op("vector", lambda ve: [
                ve.tensor_copy(out=ab2[:, 0:CH], in_=ps2A[:]),
                ve.tensor_copy(out=ab2[:, CH:2 * CH], in_=ps2B[:])][-1],
                deps=evdeps, inc=env.s_dv)
            st['abev'] = abev
            if getattr(env, 'dbg_ab', None) is not None:
                st['dbgab'] = sch.op(
                    "gpsimd", lambda gp, th=th, c=c: gp.dma_start(
                        out=env.dbg_ab[:, (th * env.NTCH + c) * 2 * CH:
                                       (th * env.NTCH + c + 1) * 2 * CH],
                        in_=ab2[:]), deps=[abev], inc=env.s_dbg, inck=16)
                st['dbgres'] = [None] * 8
                for m in range(8):
                    st['dbgres'][m] = sch.op(
                        "gpsimd", lambda gp, m=m, chg=chg: gp.dma_start(
                            out=env.dbg_res2[m * 128:(m + 1) * 128,
                                             chg * CH:(chg + 1) * CH],
                            in_=res2[:, m * CH:(m + 1) * CH]),
                        deps=[ra[m], abev], inc=env.s_dbg, inck=16)
            norm = [None] * 8
            odma = [None] * 8
            for m in range(8):
                n1 = sch.op("vector", lambda ve, m=m: ve.tensor_mul(
                    out=rsq2[:, (m % 2) * CH:(m % 2 + 1) * CH],
                    in0=res2[:, m * CH:(m + 1) * CH], in1=ab2[:, 0:CH]),
                    deps=[abev, sq[m], sums[m], ssqs[m]], inc=env.s_dv)
                ndeps = [n1]
                if c >= 1:
                    ndeps.append(prev2[c - 1]['odma'][7])
                n2 = sch.op("vector", lambda ve, m=m: ve.tensor_add(
                    out=outf[:, m * CH:(m + 1) * CH],
                    in0=rsq2[:, (m % 2) * CH:(m % 2 + 1) * CH],
                    in1=ab2[:, CH:2 * CH]), deps=ndeps, inc=env.s_dv)
                norm[m] = n2
                odma[m] = sch.op(
                    "gpsimd", lambda gp, m=m, chg=chg:
                    gp.dma_start(out=env.outT[m * 128:(m + 1) * 128,
                                              chg * CH:(chg + 1) * CH],
                                 in_=outf[:, m * CH:(m + 1) * CH]),
                    deps=[n2], inc=env.s_od[c % 2], inck=16)
            st['norm'] = norm
            st['odma'] = odma
            st['ra'] = ra
            st['sum'] = sums
            st['ssq'] = ssqs
            prev2[c] = st
        sch.drain(env.s_od[0])
        sch.drain(env.s_od[1])
        sch.drain(env.s_xs[0])
        sch.drain(env.s_xs[1])
        sch.run(f"ln2_{th}")


def build(T=4096, debug=False, wsets=1):
    assert T % 1024 == 0
    env = Env()
    env.T = T
    env.wsets = wsets
    env.NCH = T // CH
    env.NKT = T // 128
    env.THL = T // 2
    env.NTCH = env.THL // CH

    nc = bass.Bass("TRN2", target_bir_lowering=False)
    env.nc = nc

    env.xT = nc.dram_tensor("xT", [D, T], F32, kind="ExternalInput").ap()
    env.wdkv = nc.dram_tensor("wdkv", [D, DL], F32, kind="ExternalInput").ap()
    env.wuk = nc.dram_tensor("wuk", [DL, D], F32, kind="ExternalInput").ap()
    env.wuv = nc.dram_tensor("wuv", [DL, D], F32, kind="ExternalInput").ap()
    env.wuq = nc.dram_tensor("wuq", [D, D], F32, kind="ExternalInput").ap()
    env.wo = nc.dram_tensor("wo", [D, D], F32, kind="ExternalInput").ap()
    env.rw = nc.dram_tensor("rw", [D, E], F32, kind="ExternalInput").ap()
    env.w1 = nc.dram_tensor("w1", [E, D, FF], F32, kind="ExternalInput").ap()
    env.w2 = nc.dram_tensor("w2", [E, FF, D], F32, kind="ExternalInput").ap()
    env.c_ident = nc.dram_tensor("c_ident", [128, 128], F32,
                                 kind="ExternalInput").ap()
    env.c_sel16 = nc.dram_tensor("c_sel16", [128, 128], BF16,
                                 kind="ExternalInput").ap()
    env.c_sel8h = nc.dram_tensor("c_sel8h", [128, 8], BF16,
                                 kind="ExternalInput").ap()
    env.c_sel8e = nc.dram_tensor("c_sel8e", [8, 1024], BF16,
                                 kind="ExternalInput").ap()

    env.outT = nc.dram_tensor("outT", [D, T], F32, kind="ExternalOutput").ap()
    if debug:
        env.x1f = nc.dram_tensor("x1f", [D, T], F32, kind="ExternalOutput").ap()
        env.x1b = nc.dram_tensor("x1b", [D, T], BF16, kind="ExternalOutput").ap()
    else:
        env.x1f = nc.dram_tensor("x1f", [D, T], F32).ap()
        env.x1b = nc.dram_tensor("x1b", [D, T], BF16).ap()
    if debug:
        env.dbg_ckv = nc.dram_tensor("dbg_ckv", [DL, T], BF16,
                                     kind="ExternalOutput").ap()
        env.dbg_q = nc.dram_tensor("dbg_q", [D, T], BF16,
                                   kind="ExternalOutput").ap()
        env.dbg_cw = nc.dram_tensor("dbg_cw", [8, T], BF16,
                                    kind="ExternalOutput").ap()
        env.dbg_moe = nc.dram_tensor("dbg_moe", [D, T], F32,
                                     kind="ExternalOutput").ap()
        env.dbg_res2 = nc.dram_tensor("dbg_res2", [D, T], F32,
                                      kind="ExternalOutput").ap()
        env.dbg_ab = nc.dram_tensor("dbg_ab", [128, 16 * CH], F32,
                                    kind="ExternalOutput").ap()

    ctx = ExitStack()
    EC = ctx.enter_context

    env.s_pe = Ctr(EC(nc.semaphore("s_pe")))
    env.s_ac = Ctr(EC(nc.semaphore("s_ac")))
    env.s_dv = Ctr(EC(nc.semaphore("s_dv")))
    env.s_pl = Ctr(EC(nc.semaphore("s_pl")))
    env.s_cst = Ctr(EC(nc.semaphore("s_cst")))
    env.s_stg = [Ctr(EC(nc.semaphore(f"s_stg{i}"))) for i in range(2)]
    env.s_xstg = [Ctr(EC(nc.semaphore(f"s_xstg{i}"))) for i in range(2)]
    env.s_cx = [Ctr(EC(nc.semaphore(f"s_cx{i}"))) for i in range(2)]
    env.s_xs = [Ctr(EC(nc.semaphore(f"s_xs{i}"))) for i in range(2)]
    env.s_xd = [Ctr(EC(nc.semaphore(f"s_xd{i}"))) for i in range(2)]
    env.s_h1 = Ctr(EC(nc.semaphore("s_h1")))
    env.s_od = [Ctr(EC(nc.semaphore(f"s_od{i}"))) for i in range(2)]
    env.s_cd = [Ctr(EC(nc.semaphore(f"s_cd{i}"))) for i in range(2)]
    env.s_msc = Ctr(EC(nc.semaphore("s_msc")))
    env.s_dbg = Ctr(EC(nc.semaphore("s_dbg")))
    env.all_dma_ctrs = ([env.s_cst, env.s_h1, env.s_msc, env.s_dbg]
                        + env.s_stg + env.s_xstg + env.s_cx + env.s_xs
                        + env.s_xd + env.s_od + env.s_cd)

    env.ones_f1 = EC(nc.sbuf_tensor("ones_f1", [1, 128], F32))
    env.ones_fc = EC(nc.sbuf_tensor("ones_fc", [128, 1], F32))
    env.eps_t = EC(nc.sbuf_tensor("eps_t", [1, 1], F32))
    env.ident = EC(nc.sbuf_tensor("ident", [128, 128], F32))
    env.sel16 = EC(nc.sbuf_tensor("sel16", [128, 128], BF16))
    env.sel8h = EC(nc.sbuf_tensor("sel8h", [128, 8], BF16))
    env.sel8e = EC(nc.sbuf_tensor("sel8e", [8, 1024], BF16))
    env.qmax16 = EC(nc.sbuf_tensor("qmax16", [16, 1], F32))
    env.qmaxG = EC(nc.sbuf_tensor("qmaxG", [4, 1], F32))
    env.nrm_tmp = EC(nc.sbuf_tensor("nrm_tmp", [16, 1], F32))
    env.cwT = EC(nc.sbuf_tensor("cwT", [8, T], BF16))
    env.ctxh = nc.dram_tensor("ctxh", [D, T], BF16,
                              kind="ExternalOutput" if debug else "Internal").ap()

    es_w = ExitStack()
    wuk_bf = es_w.enter_context(nc.sbuf_tensor("wuk_bf", [128, 2 * D], BF16))
    wuv_bf = es_w.enter_context(nc.sbuf_tensor("wuv_bf", [128, 2 * D], BF16))
    es_ckq = ExitStack()
    env.c_kvT = [es_ckq.enter_context(nc.sbuf_tensor(f"ckv{i}", [128, T], BF16))
                 for i in range(2)]
    env.qT = [es_ckq.enter_context(nc.sbuf_tensor(f"qT{i}", [128, T], BF16))
              for i in range(8)]

    es_x = ExitStack()
    wdkv_bf = es_x.enter_context(nc.sbuf_tensor("wdkv_bf", [128, 8 * DL], BF16))
    wuq_bf = es_x.enter_context(nc.sbuf_tensor("wuq_bf", [128, 8 * D], BF16))
    xT_bf = es_x.enter_context(nc.sbuf_tensor("xT_bf", [128, 8 * env.T], BF16))

    _p0(env, wdkv_bf, wuk_bf, wuv_bf, wuq_bf, xT_bf)

    groups_ckv = [(m, ch) for m in range(2) for ch in range(env.NCH)]
    _proj_phase(
        env, "ckv", groups_ckv, 8,
        lambda g, k: wdkv_bf[:, k * DL + g[0] * 128: k * DL + (g[0] + 1) * 128],
        lambda g, k: xT_bf[:, k * T + g[1] * CH: k * T + (g[1] + 1) * CH],
        lambda ve, g, ps: ve.tensor_copy(
            out=env.c_kvT[g[0]][:, g[1] * CH:(g[1] + 1) * CH], in_=ps[:]),
    )
    groups_q = [(m, ch) for m in range(8) for ch in range(env.NCH)]
    _proj_phase(
        env, "q", groups_q, 8,
        lambda g, k: wuq_bf[:, k * D + g[0] * 128: k * D + (g[0] + 1) * 128],
        lambda g, k: xT_bf[:, k * T + g[1] * CH: k * T + (g[1] + 1) * CH],
        lambda ve, g, ps: ve.tensor_copy(
            out=env.qT[g[0]][:, g[1] * CH:(g[1] + 1) * CH], in_=ps[:]),
    )
    _qnorms(env)

    if debug:
        sch = Sched(nc)
        for m in range(2):
            sch.op("gpsimd", lambda gp, m=m: gp.dma_start(
                out=env.dbg_ckv[m * 128:(m + 1) * 128, :],
                in_=env.c_kvT[m][:]), inc=env.s_dbg, inck=16)
        for m in range(8):
            sch.op("gpsimd", lambda gp, m=m: gp.dma_start(
                out=env.dbg_q[m * 128:(m + 1) * 128, :], in_=env.qT[m][:]),
                inc=env.s_dbg, inck=16)
        sch.drain(env.s_dbg)
        sch.run("dbgq")

    es_x.close()

    for g in range(NH // HG):
        _attn_group(env, g, wuk_bf, wuv_bf)

    es_ckq.close()
    es_w.close()

    es_p6 = ExitStack()
    wo_bf = es_p6.enter_context(nc.sbuf_tensor("wo_bf", [128, 8 * D], BF16))
    rw_bf = es_p6.enter_context(nc.sbuf_tensor("rw_bf", [128, 8 * E], BF16))
    with nc.sbuf_tensor("wstg0b", [128, D], F32) as wstg0b, \
         nc.sbuf_tensor("wstg1b", [128, D], F32) as wstg1b:
        wp2 = []
        for kd in range(8):
            wp2.append((wo_bf, kd * D, D, env.wo[kd * 128:(kd + 1) * 128, :]))
        for kd in range(8):
            wp2.append((rw_bf, kd * E, E, env.rw[kd * 128:(kd + 1) * 128, :]))
        _stream_pieces(env, wp2, [wstg0b, wstg1b])

    _p6(env, wo_bf, rw_bf)
    es_p6.close()

    for th in range(2):
        _moe_pass(env, th)

    if debug:
        sch = Sched(nc)
        sch.op("gpsimd", lambda gp: gp.dma_start(out=env.dbg_cw[:],
                                                 in_=env.cwT[:]),
               inc=env.s_dbg, inck=16)
        sch.drain(env.s_dbg)
        sch.run("dbgcw")

    with nc.Block("fin") as blk:
        def fin(gp):
            for ctr in env.all_dma_ctrs:
                if ctr.n > 0:
                    gp.wait_ge(ctr.h, ctr.n)
        blk.gpsimd(fin)

    ctx.close()
    return nc


def _consts():
    ident = np.eye(128, dtype=np.float32)
    sel16 = np.zeros((128, 128), dtype=ml_dtypes.bfloat16)
    for kd in range(8):
        sel16[0:64, 16 * kd + 2 * kd] = 1
        sel16[64:128, 16 * kd + 2 * kd + 1] = 1
    sel8h = np.zeros((128, 8), dtype=ml_dtypes.bfloat16)
    for p in range(2):
        sel8h[0:64, 4 * p + 2 * p] = 1
        sel8h[64:128, 4 * p + 2 * p + 1] = 1
    sel8e = np.zeros((8, 1024), dtype=ml_dtypes.bfloat16)
    for e in range(8):
        sel8e[e, e * 128:(e + 1) * 128] = 1
    return {"c_ident": ident, "c_sel16": sel16, "c_sel8h": sel8h,
            "c_sel8e": sel8e}


def kernel(L0_attn_x, L0_attn_W_DKV, L0_attn_W_UK, L0_attn_W_UV, L0_attn_W_UQ,
           L0_attn_W_O, L0_ffn_router_w, L0_ffn_w1, L0_ffn_w2):
    x = np.ascontiguousarray(np.asarray(L0_attn_x, dtype=np.float32))
    assert x.shape == (8, 4096, 1024)
    shared = {
        "wdkv": np.ascontiguousarray(np.asarray(L0_attn_W_DKV, np.float32)),
        "wuk": np.ascontiguousarray(np.asarray(L0_attn_W_UK, np.float32)),
        "wuv": np.ascontiguousarray(np.asarray(L0_attn_W_UV, np.float32)),
        "wuq": np.ascontiguousarray(np.asarray(L0_attn_W_UQ, np.float32)),
        "wo": np.ascontiguousarray(np.asarray(L0_attn_W_O, np.float32)),
        "rw": np.ascontiguousarray(np.asarray(L0_ffn_router_w, np.float32)),
        "w1": np.ascontiguousarray(np.asarray(L0_ffn_w1, np.float32)),
        "w2": np.ascontiguousarray(np.asarray(L0_ffn_w2, np.float32)),
    }
    shared.update(_consts())
    nc = build(T=4096, debug=False)
    in_maps = []
    for b in range(8):
        m = dict(shared)
        m["xT"] = np.ascontiguousarray(x[b].T)
        in_maps.append(m)
    res = run_bass_kernel_spmd(nc, in_maps, core_ids=list(range(8)))
    out = np.stack([np.asarray(r["outT"], dtype=np.float32).T
                    for r in res.results], axis=0)
    return out


# version 05
# speedup vs baseline: 1.0538x; 1.0112x over previous
import numpy as np
import ml_dtypes
from contextlib import ExitStack

import concourse.bass as bass
import concourse.mybir as mybir
from concourse.bass_utils import run_bass_kernel_spmd

AF = mybir.ActivationFunctionType
ALU = mybir.AluOpType
AXX = mybir.AxisListType.X
F32 = mybir.dt.float32
BF16 = mybir.dt.bfloat16

D = 1024
DL = 256
NH = 16
HD = 64
E = 8
FF = 2048
CH = 512
HG = 4
EPS_LN = 1e-5
ENGINES = ("sync", "tensor", "scalar", "vector", "gpsimd")


class Ctr:
    def __init__(self, h):
        self.h = h
        self.n = 0


class Tok:
    __slots__ = ("ctr", "val", "k")

    def __init__(self, ctr, k, val):
        self.ctr = ctr
        self.k = k
        self.val = val


class Sched:

    def __init__(self, nc):
        self.nc = nc
        self.items = []

    def op(self, eng, fn, deps=(), inc=None, inck=1):
        deps = tuple(d for d in deps if d is not None)
        tok = None
        if inc is not None:
            inc.n += inck
            tok = Tok(inc, inck, inc.n)
        self.items.append((eng, fn, deps, tok))
        return tok

    def drain(self, ctr, eng="gpsimd"):
        v = ctr.n
        if v > 0:
            self.op(eng, lambda e, h=ctr.h, v=v: e.wait_ge(h, v))

    def run(self, name=None):
        with self.nc.Block(name) as blk:
            for eng in ENGINES:
                items = [it for it in self.items if it[0] == eng]
                if not items:
                    continue

                def body(e, items=items):
                    last = {}
                    for (_, fn, deps, tok) in items:
                        for d in deps:
                            if last.get(id(d.ctr), -1) < d.val:
                                e.wait_ge(d.ctr.h, d.val)
                                last[id(d.ctr)] = d.val
                        inst = fn(e)
                        if tok is not None:
                            inst.then_inc(tok.ctr.h, tok.k)
                getattr(blk, eng)(body)
        self.items = []


class Env:
    pass


def _stream_pieces(env, pieces, stg, dma_ctr=None, cast_eng="vector"):
    nc = env.nc
    sch = Sched(nc)
    toks = []
    cast_ctr = env.s_dv if cast_eng == "vector" else env.s_pl
    for i, (dst, c0, ncol, src) in enumerate(pieces):
        deps = [toks[i - 2]] if i >= 2 else []
        dt_ = sch.op("sync", lambda sy, i=i, ncol=ncol, src=src:
                     sy.dma_start(out=stg[i % 2][:, 0:ncol], in_=src),
                     deps=deps, inc=env.s_stg[i % 2], inck=16)
        ct = sch.op(cast_eng, lambda e, i=i, dst=dst, c0=c0, ncol=ncol:
                    e.tensor_copy(out=dst[:, c0:c0 + ncol],
                                  in_=stg[i % 2][:, 0:ncol]),
                    deps=[dt_], inc=cast_ctr)
        toks.append(ct)
    sch.run("stream")
    return toks


def _proj_phase(env, name, groups, nk, lhs_fn, rhs_fn, evac_fn, psum_n=CH):
    nc = env.nc
    toks = {}
    with nc.psum_tensor(f"{name}p0", [128, psum_n], F32) as p0, \
         nc.psum_tensor(f"{name}p1", [128, psum_n], F32) as p1:
        ps = [p0, p1]
        sch = Sched(nc)
        for gi, g in enumerate(groups):
            deps = [toks[groups[gi - 2]]] if gi >= 2 else []

            def mk_mm(g=g, gi=gi):
                def f(te):
                    for k in range(nk):
                        inst = te.matmul(ps[gi % 2][:], lhs_fn(g, k),
                                         rhs_fn(g, k),
                                         start=(k == 0), stop=(k == nk - 1))
                    return inst
                return f
            mt = sch.op("tensor", mk_mm(), deps=deps, inc=env.s_pe)

            def mk_ev(g=g, gi=gi):
                def f(ve):
                    return evac_fn(ve, g, ps[gi % 2])
                return f
            toks[g] = sch.op("vector", mk_ev(), deps=[mt], inc=env.s_dv)
        sch.run(name)
    return toks


def _p0(env, wdkv_bf, wuk_bf, wuv_bf, wuq_bf, xT_bf):
    nc = env.nc
    T = env.T
    with nc.sbuf_tensor("wstg0", [128, D], F32) as wstg0, \
         nc.sbuf_tensor("wstg1", [128, D], F32) as wstg1, \
         nc.sbuf_tensor("xstg0", [128, 1024], F32) as xstg0, \
         nc.sbuf_tensor("xstg1", [128, 1024], F32) as xstg1:
        wstg = [wstg0, wstg1]
        xstg = [xstg0, xstg1]
        sch = Sched(nc)
        ctok = None
        for dst, src in ((env.ident, env.c_ident), (env.sel16, env.c_sel16),
                         (env.sel8h, env.c_sel8h), (env.sel8e, env.c_sel8e)):
            ctok = sch.op("sync", lambda sy, dst=dst, src=src:
                          sy.dma_start(out=dst[:], in_=src),
                          inc=env.s_cst, inck=16)
        env.consts_tok = ctok
        sch.op("vector", lambda ve: ve.memset(env.ones_f1[:], 1.0))
        sch.op("vector", lambda ve: ve.memset(env.ones_fc[:], 1.0))
        sch.op("vector", lambda ve: ve.memset(env.eps_t[:], EPS_LN))
        sch.op("vector", lambda ve: ve.memset(env.qmax16[:], 0.0))
        wpieces = []
        for kd in range(8):
            wpieces.append((wdkv_bf, kd * DL, DL,
                            env.wdkv[kd * 128:(kd + 1) * 128, :]))
        for kl in range(2):
            wpieces.append((wuk_bf, kl * D, D,
                            env.wuk[kl * 128:(kl + 1) * 128, :]))
        for kl in range(2):
            wpieces.append((wuv_bf, kl * D, D,
                            env.wuv[kl * 128:(kl + 1) * 128, :]))
        for kd in range(8):
            wpieces.append((wuq_bf, kd * D, D,
                            env.wuq[kd * 128:(kd + 1) * 128, :]))
        wtoks = []
        for i, (dst, c0, ncol, src) in enumerate(wpieces):
            deps = [wtoks[i - 2]] if i >= 2 else []
            dt_ = sch.op("sync", lambda sy, i=i, ncol=ncol, src=src:
                         sy.dma_start(out=wstg[i % 2][:, 0:ncol], in_=src),
                         deps=deps, inc=env.s_stg[i % 2], inck=16)
            ct = sch.op("vector", lambda ve, i=i, dst=dst, c0=c0, ncol=ncol:
                        ve.tensor_copy(out=dst[:, c0:c0 + ncol],
                                       in_=wstg[i % 2][:, 0:ncol]),
                        deps=[dt_], inc=env.s_dv)
            wtoks.append(ct)
        xtoks = []
        xpieces = [(kd, cp) for kd in range(8) for cp in range(T // 1024)]
        for i, (kd, cp) in enumerate(xpieces):
            deps = [xtoks[i - 2]] if i >= 2 else []
            dt_ = sch.op("sync", lambda sy, i=i, kd=kd, cp=cp:
                         sy.dma_start(out=xstg[i % 2][:],
                                      in_=env.xT[kd * 128:(kd + 1) * 128,
                                                 cp * 1024:(cp + 1) * 1024]),
                         deps=deps, inc=env.s_xstg[i % 2], inck=16)
            ct = sch.op("gpsimd", lambda gp, i=i, kd=kd, cp=cp:
                        gp.tensor_copy(
                            out=xT_bf[:, kd * T + cp * 1024:
                                      kd * T + (cp + 1) * 1024],
                            in_=xstg[i % 2][:]),
                        deps=[dt_], inc=env.s_pl)
            xtoks.append(ct)
        sch.run("p0")


def _qnorms(env):
    nc = env.nc
    NCH = env.NCH
    with nc.sbuf_tensor("qsq0", [128, CH], BF16) as qsq0, \
         nc.sbuf_tensor("qsq1", [128, CH], BF16) as qsq1, \
         nc.psum_tensor("psn16", [16, CH], F32) as psn16:
        qsq = [qsq0, qsq1]
        sch = Sched(nc)
        sq_toks = []
        prev_red = None
        prev_mx = None
        i = 0
        for chq in range(NCH):
            grp_tok = None
            for kd in range(8):
                deps = [sq_toks[i - 2][1]] if i >= 2 else []
                st_ = sch.op("scalar", lambda sc, i=i, kd=kd, chq=chq:
                             sc.activation(
                                 out=qsq[i % 2][:],
                                 in_=env.qT[kd][:, chq * CH:(chq + 1) * CH],
                                 func=AF.Square),
                             deps=deps, inc=env.s_ac)
                mdeps = [st_, env.consts_tok]
                if kd == 0 and prev_red is not None:
                    mdeps.append(prev_red)
                mt = sch.op("tensor", lambda te, i=i, kd=kd:
                            te.matmul(psn16[:],
                                      env.sel16[:, kd * 16:(kd + 1) * 16],
                                      qsq[i % 2][:],
                                      start=(kd == 0), stop=(kd == 7)),
                            deps=mdeps, inc=env.s_pe)
                sq_toks.append((st_, mt))
                grp_tok = mt
                i += 1
            rd = sch.op("vector", lambda ve: ve.reduce_max(
                out=env.nrm_tmp[:], in_=psn16[:], axis=AXX),
                deps=[grp_tok] + ([prev_mx] if prev_mx else []), inc=env.s_dv)
            mx = sch.op("vector", lambda ve: ve.tensor_max(
                out=env.qmax16[:], in0=env.qmax16[:], in1=env.nrm_tmp[:]),
                deps=[rd], inc=env.s_dv)
            prev_red = rd
            prev_mx = mx
        sch.run("p2n")


def _attn_group(env, g, wuk_bf, wuv_bf):
    nc = env.nc
    T, NCH, NKT = env.T, env.NCH, env.NKT
    VW = HG * 65
    with nc.sbuf_tensor(f"kTg{g}", [128, 2 * T], BF16) as kTg, \
         nc.sbuf_tensor(f"vaug{g}", [128, NKT * VW], BF16) as vaug, \
         nc.sbuf_tensor(f"bias4_{g}", [128, 4], F32) as bias4, \
         nc.sbuf_tensor(f"kmax4_{g}", [4, 1], F32) as kmax4, \
         nc.sbuf_tensor(f"ktmp4_{g}", [4, 1], F32) as ktmp4, \
         nc.sbuf_tensor(f"brow_{g}", [1, 4], F32) as brow, \
         nc.sbuf_tensor(f"probT_{g}", [128, 6 * CH], BF16) as probT, \
         nc.sbuf_tensor(f"probM_{g}", [128, 4 * CH], BF16) as probM, \
         nc.sbuf_tensor(f"recipS_{g}", [1, 2 * CH], F32) as recipS, \
         nc.sbuf_tensor(f"rbS_{g}", [64, 2 * CH], F32) as rbS, \
         nc.sbuf_tensor(f"ksq0_{g}", [128, CH], BF16) as ksq0, \
         nc.sbuf_tensor(f"ksq1_{g}", [128, CH], BF16) as ksq1, \
         nc.sbuf_tensor(f"ctxs_{g}", [64, 2 * CH], BF16) as ctxs:

        groups_k = [(p, ch) for p in range(2) for ch in range(NCH)]
        ktoks = _proj_phase(
            env, f"kT{g}", groups_k, 2,
            lambda gk, k: wuk_bf[:, k * D + g * 256 + gk[0] * 128:
                                 k * D + g * 256 + (gk[0] + 1) * 128],
            lambda gk, k: env.c_kvT[k][:, gk[1] * CH:(gk[1] + 1) * CH],
            lambda ve, gk, ps: ve.tensor_copy(
                out=kTg[:, gk[0] * T + gk[1] * CH:
                        gk[0] * T + (gk[1] + 1) * CH],
                in_=ps[:]),
        )

        sch = Sched(nc)

        def ones_cols(gp):
            inst = None
            for kt in range(NKT):
                for hh in range(HG):
                    inst = gp.memset(vaug[:, kt * VW + hh * 65 + 64:
                                          kt * VW + hh * 65 + 65], 1.0)
            return inst
        vones_tok = sch.op("gpsimd", ones_cols, inc=env.s_pl)
        sch.run(f"vones{g}")

        def v_evac(ve, kt, ps):
            inst = None
            for hh in range(HG):
                inst = ve.tensor_copy(
                    out=vaug[:, kt * VW + hh * 65: kt * VW + hh * 65 + 64],
                    in_=ps[:, hh * 64:(hh + 1) * 64])
            return inst
        vtoks = _proj_phase(
            env, f"v{g}", list(range(NKT)), 2,
            lambda kt, k: env.c_kvT[k][:, kt * 128:(kt + 1) * 128],
            lambda kt, k: wuv_bf[:, k * D + g * 256: k * D + (g + 1) * 256],
            v_evac, psum_n=HG * 64,
        )

        bias_tok = None
        with nc.psum_tensor(f"psn4_{g}", [4, CH], F32) as psn4, \
             nc.psum_tensor(f"psb4_{g}", [128, 4], F32) as psb4:
            ksq = [ksq0, ksq1]
            sch = Sched(nc)
            km0 = sch.op("vector", lambda ve: ve.memset(kmax4[:], 0.0),
                         inc=env.s_dv)
            i = 0
            prev_red = None
            prev_mx = km0
            sq_list = []
            for chq in range(NCH):
                grp_tok = None
                for p in range(2):
                    deps = [sq_list[i - 2][1]] if i >= 2 else []
                    st_ = sch.op(
                        "scalar", lambda sc, i=i, p=p, chq=chq:
                        sc.activation(
                            out=ksq[i % 2][:],
                            in_=kTg[:, p * T + chq * CH:
                                    p * T + (chq + 1) * CH],
                            func=AF.Square),
                        deps=deps + [ktoks[(p, chq)]], inc=env.s_ac)
                    mdeps = [st_, env.consts_tok]
                    if p == 0 and prev_red is not None:
                        mdeps.append(prev_red)
                    mt = sch.op("tensor", lambda te, i=i, p=p:
                                te.matmul(psn4[:],
                                          env.sel8h[:, p * 4:(p + 1) * 4],
                                          ksq[i % 2][:],
                                          start=(p == 0), stop=(p == 1)),
                                deps=mdeps, inc=env.s_pe)
                    sq_list.append((st_, mt))
                    grp_tok = mt
                    i += 1
                rd = sch.op("vector", lambda ve: ve.reduce_max(
                    out=ktmp4[:], in_=psn4[:], axis=AXX),
                    deps=[grp_tok, prev_mx], inc=env.s_dv)
                mx = sch.op("vector", lambda ve: ve.tensor_max(
                    out=kmax4[:], in0=kmax4[:], in1=ktmp4[:]),
                    deps=[rd], inc=env.s_dv)
                prev_red = rd
                prev_mx = mx
            qg = sch.op("sync", lambda sy: sy.dma_start(
                out=env.qmaxG[:], in_=env.qmax16[g * 4:(g + 1) * 4, :],
                allow_slow_non_contiguous=True), inc=env.s_msc, inck=16)
            t1 = sch.op("vector", lambda ve: ve.tensor_mul(
                out=ktmp4[:], in0=kmax4[:], in1=env.qmaxG[:]),
                deps=[prev_mx, prev_red, qg], inc=env.s_dv)
            t2 = sch.op("scalar", lambda sc: sc.activation(
                out=ktmp4[:], in_=ktmp4[:], func=AF.Sqrt),
                deps=[t1], inc=env.s_ac)
            t3 = sch.op("vector", lambda ve: ve.tensor_scalar_mul(
                out=ktmp4[:], in0=ktmp4[:], scalar1=-0.125),
                deps=[t2], inc=env.s_dv)
            t4 = sch.op("sync", lambda sy: sy.dma_start(
                out=brow[:], in_=ktmp4[:], allow_slow_non_contiguous=True),
                deps=[t3], inc=env.s_msc, inck=16)
            t5 = sch.op("tensor", lambda te: te.matmul(
                psb4[:], env.ones_f1[0:1, :], brow[:], start=True, stop=True),
                deps=[t4], inc=env.s_pe)
            bias_tok = sch.op("vector", lambda ve: ve.tensor_copy(
                out=bias4[:], in_=psb4[:]), deps=[t5], inc=env.s_dv)
            sch.run(f"knrm{g}")

        _attn_loop(env, g, kTg, vaug, bias4, probT, probM, recipS, rbS,
                   ctxs, ktoks, vtoks, vones_tok, bias_tok)


def _attn_loop(env, g, kTg, vaug, bias4, probT, probM, recipS, rbS,
               ctxs, ktoks, vtoks, vones_tok, bias_tok):
    nc = env.nc
    T, NCH, NKT = env.T, env.NCH, env.NKT
    VW = HG * 65
    PW = 2 * CH
    jobs = [(hh, qc) for hh in range(HG) for qc in range(NCH)]
    with nc.psum_tensor(f"ps_s0_{g}", [128, PW], F32) as ps_s0, \
         nc.psum_tensor(f"ps_s1_{g}", [128, PW], F32) as ps_s1, \
         nc.psum_tensor(f"ps_c0_{g}", [65, CH], F32) as ps_c0, \
         nc.psum_tensor(f"ps_c1_{g}", [65, CH], F32) as ps_c1, \
         nc.psum_tensor(f"ps_r0_{g}", [64, CH], F32) as ps_r0, \
         nc.psum_tensor(f"ps_r1_{g}", [64, CH], F32) as ps_r1:
        ps_s = [ps_s0, ps_s1]
        ps_c = [ps_c0, ps_c1]
        ps_r = [ps_r0, ps_r1]
        sch = Sched(nc)
        pslot_free = [[], [], []]
        mslot_free = [None, None, None, None]
        exp_toks = {}
        mask_toks = {}
        mask_slot = {}
        pv_toks = {}
        job_dve = {}
        job_bc = {}
        gp_ = 0
        gt = 0
        d = 0
        for j, (hh, qc) in enumerate(jobs):
            nkt = min(NKT, 4 * (qc + 1))
            npair = nkt // 2
            h_glob = g * HG + hh
            pt = hh // 2
            rb = 64 * (hh % 2)
            qtile = h_glob // 2
            qb = 64 * (h_glob % 2)
            base_gt = gt
            base_gp = gp_
            for i in range(npair + 1):
                if i < npair:
                    pr = base_gp + i
                    sct = None
                    for half in range(2):
                        kt = 2 * i + half
                        deps = [exp_toks.get(pr - 2), ktoks[(pt, kt // 4)]]
                        sct = sch.op(
                            "tensor",
                            lambda te, pt=pt, rb=rb, kt=kt, qc=qc, qtile=qtile,
                            qb=qb, pr=pr, half=half:
                            te.matmul(
                                ps_s[pr % 2][:, half * CH:(half + 1) * CH],
                                kTg[rb:rb + 64,
                                    pt * T + kt * 128: pt * T + (kt + 1) * 128],
                                env.qT[qtile][qb:qb + 64, qc * CH:(qc + 1) * CH],
                                start=True, stop=True),
                            deps=deps, inc=env.s_pe)
                    edeps = [sct, bias_tok] + pslot_free[pr % 3]
                    et = sch.op(
                        "scalar", lambda sc, pr=pr, hh=hh:
                        sc.activation(
                            out=probT[:, (pr % 3) * PW:(pr % 3 + 1) * PW],
                            in_=ps_s[pr % 2][:], func=AF.Exp,
                            scale=0.125, bias=bias4[:, hh:hh + 1]),
                        deps=edeps, inc=env.s_ac)
                    exp_toks[pr] = et
                    pslot_free[pr % 3] = []
                    for half in range(2):
                        kt = 2 * i + half
                        t_this = base_gt + kt
                        if kt >= nkt - 4:
                            mdeps = [et, mslot_free[d % 4]]
                            mt = sch.op(
                                "gpsimd",
                                lambda gp, pr=pr, half=half, d=d, qc=qc, kt=kt:
                                gp.affine_select(
                                    out=probM[:, (d % 4) * CH:(d % 4 + 1) * CH],
                                    in_=probT[:, (pr % 3) * PW + half * CH:
                                              (pr % 3) * PW + (half + 1) * CH],
                                    pattern=[[1, CH]], compare_op=ALU.is_ge,
                                    fill=0.0, base=qc * CH - kt * 128,
                                    channel_multiplier=-1),
                                deps=mdeps, inc=env.s_pl)
                            mask_toks[t_this] = mt
                            mask_slot[t_this] = d % 4
                            pslot_free[pr % 3].append(mt)
                            d += 1
                if i >= 1:
                    ip = i - 1
                    pr = base_gp + ip
                    for half in range(2):
                        ii = 2 * ip + half
                        t_pv = base_gt + ii
                        diag = ii >= nkt - 4
                        deps = [vtoks[ii], vones_tok]
                        if diag:
                            deps.append(mask_toks[t_pv])
                            rhs_slot = ('m', mask_slot[t_pv])
                        else:
                            deps.append(exp_toks[pr])
                            rhs_slot = ('t', (pr % 3, half))
                        if ii == 0 and j >= 2:
                            deps.append(job_dve[j - 2][2])

                        def mk_pv(ii=ii, j=j, hh=hh, nkt=nkt, rhs_slot=rhs_slot):
                            def f(te):
                                if rhs_slot[0] == 'm':
                                    rhs = probM[:, rhs_slot[1] * CH:
                                                (rhs_slot[1] + 1) * CH]
                                else:
                                    sl, hf = rhs_slot[1]
                                    rhs = probT[:, sl * PW + hf * CH:
                                                sl * PW + (hf + 1) * CH]
                                return te.matmul(
                                    ps_c[j % 2][:],
                                    vaug[:, ii * VW + hh * 65:
                                         ii * VW + hh * 65 + 65],
                                    rhs, start=(ii == 0), stop=(ii == nkt - 1))
                            return f
                        pt_ = sch.op("tensor", mk_pv(), deps=deps, inc=env.s_pe)
                        pv_toks[t_pv] = pt_
                        if diag:
                            mslot_free[mask_slot[t_pv]] = pt_
                        else:
                            pslot_free[pr % 3].append(pt_)
            gt = base_gt + nkt
            gp_ = base_gp + npair
            last_pv = pv_toks[gt - 1]
            rdeps = [last_pv]
            if j >= 2:
                rdeps.append(job_bc[j - 2])
            r1 = sch.op("vector", lambda ve, j=j:
                        ve.reciprocal(
                            out=recipS[0:1, (j % 2) * CH:(j % 2 + 1) * CH],
                            in_=ps_c[j % 2][64:65, :]),
                        deps=rdeps, inc=env.s_dv)
            bdeps = [r1]
            if j >= 2:
                bdeps.append(job_dve[j - 2][1])
            b1 = sch.op("tensor", lambda te, j=j:
                        te.matmul(ps_r[j % 2][:], env.ones_f1[0:1, 0:64],
                                  recipS[0:1, (j % 2) * CH:(j % 2 + 1) * CH],
                                  start=True, stop=True),
                        deps=bdeps, inc=env.s_pe)
            job_bc[j] = b1
            e1 = sch.op("vector", lambda ve, j=j:
                        ve.tensor_copy(
                            out=rbS[:, (j % 2) * CH:(j % 2 + 1) * CH],
                            in_=ps_r[j % 2][:]),
                        deps=[b1], inc=env.s_dv)
            cdeps = [e1]
            if j >= 2:
                cdeps.append(job_dve[j - 2][3])
            c1 = sch.op("vector",
                        lambda ve, j=j:
                        ve.tensor_mul(
                            out=ctxs[:, (j % 2) * CH:(j % 2 + 1) * CH],
                            in0=ps_c[j % 2][0:64, :],
                            in1=rbS[:, (j % 2) * CH:(j % 2 + 1) * CH]),
                        deps=cdeps, inc=env.s_dv)
            cd = sch.op("gpsimd",
                        lambda gp, j=j, h_glob=h_glob, qc=qc:
                        gp.dma_start(
                            out=env.ctxh[h_glob * 64:(h_glob + 1) * 64,
                                         qc * CH:(qc + 1) * CH],
                            in_=ctxs[:, (j % 2) * CH:(j % 2 + 1) * CH]),
                        deps=[c1], inc=env.s_cd[j % 2], inck=16)
            job_dve[j] = (r1, e1, c1, cd)
        sch.drain(env.s_cd[0])
        sch.drain(env.s_cd[1])
        sch.run(f"attn{g}")


def _p6(env, wo_bf, rw_bf):
    nc = env.nc
    T, NCH = env.T, env.NCH
    with nc.sbuf_tensor("ctxr", [128, 16 * CH], BF16) as ctxr, \
         nc.sbuf_tensor("xst", [128, 16 * CH], F32) as xst, \
         nc.sbuf_tensor("resid", [128, 8 * CH], F32) as resid, \
         nc.sbuf_tensor("rsq", [128, 2 * CH], F32) as rsq, \
         nc.sbuf_tensor("x1bs", [128, 8 * CH], BF16) as x1bs, \
         nc.sbuf_tensor("ab_sb", [128, 2 * CH], F32) as ab_sb, \
         nc.sbuf_tensor("rows", [1, 6 * CH], F32) as rows, \
         nc.sbuf_tensor("cwtile", [128, E], F32) as cwtile, \
         nc.sbuf_tensor("cwmask", [128, E], F32) as cwmask, \
         nc.sbuf_tensor("cwrow", [128, 5], F32) as cwrow, \
         nc.psum_tensor("ps_o0", [128, CH], F32) as ps_o0, \
         nc.psum_tensor("ps_o1", [128, CH], F32) as ps_o1, \
         nc.psum_tensor("ps_sum", [1, CH], F32) as ps_sum, \
         nc.psum_tensor("ps_ssq", [1, CH], F32) as ps_ssq, \
         nc.psum_tensor("ps_A", [128, CH], F32) as ps_A, \
         nc.psum_tensor("ps_B", [128, CH], F32) as ps_B, \
         nc.psum_tensor("ps_lg", [128, E], F32) as ps_lg, \
         nc.psum_tensor("ps_ct", [8, 128], F32) as ps_ct:
        ps_o = [ps_o0, ps_o1]
        sch = Sched(nc)
        prev = {}
        for chq in range(NCH):
            st = {}
            ctoks = []
            for kd in range(8):
                deps = []
                if chq >= 2:
                    deps.append(prev[chq - 2]['wo'][7])
                ctoks.append(sch.op(
                    "sync", lambda sy, kd=kd, chq=chq:
                    sy.dma_start(
                        out=ctxr[:, ((chq % 2) * 8 + kd) * CH:
                                 ((chq % 2) * 8 + kd + 1) * CH],
                        in_=env.ctxh[kd * 128:(kd + 1) * 128,
                                     chq * CH:(chq + 1) * CH]),
                    deps=deps, inc=env.s_cx[chq % 2], inck=16))
            xtoks = []
            for m in range(8):
                deps = []
                if chq >= 2:
                    deps.append(prev[chq - 2]['ra'][m])
                xtoks.append(sch.op(
                    "sync", lambda sy, m=m, chq=chq:
                    sy.dma_start(
                        out=xst[:, ((chq % 2) * 8 + m) * CH:
                                ((chq % 2) * 8 + m + 1) * CH],
                        in_=env.xT[m * 128:(m + 1) * 128,
                                   chq * CH:(chq + 1) * CH]),
                    deps=deps, inc=env.s_xs[chq % 2], inck=16))
            ra = [None] * 8
            sq = [None] * 8
            sums = [None] * 8
            ssqs = [None] * 8
            wo_toks = [None] * 8
            for m in range(8):
                deps = [ctoks[-1]]
                if chq * 8 + m >= 2:
                    pm = chq * 8 + m - 2
                    deps.append(ra[pm % 8] if pm // 8 == chq
                                else prev[pm // 8]['ra'][pm % 8])
                wo_toks[m] = sch.op(
                    "tensor", lambda te, m=m, chq=chq:
                    [te.matmul(ps_o[m % 2][:],
                               wo_bf[:, kd * D + m * 128:
                                     kd * D + (m + 1) * 128],
                               ctxr[:, ((chq % 2) * 8 + kd) * CH:
                                    ((chq % 2) * 8 + kd + 1) * CH],
                               start=(kd == 0), stop=(kd == 7))
                     for kd in range(8)][-1],
                    deps=deps, inc=env.s_pe)
                rdeps = [wo_toks[m], xtoks[-1]]
                if chq >= 1:
                    rdeps.append(prev[chq - 1]['x1bdma'][7])
                    rdeps.append(prev[chq - 1]['sum'][m])
                    rdeps.append(prev[chq - 1]['lg_last'])
                ra[m] = sch.op(
                    "vector", lambda ve, m=m, chq=chq:
                    ve.tensor_add(
                        out=resid[:, m * CH:(m + 1) * CH],
                        in0=ps_o[m % 2][:],
                        in1=xst[:, ((chq % 2) * 8 + m) * CH:
                                ((chq % 2) * 8 + m + 1) * CH]),
                    deps=rdeps, inc=env.s_dv)
                sdeps = [ra[m]]
                if m >= 2:
                    sdeps.append(ssqs[m - 2])
                elif chq >= 1:
                    sdeps.append(prev[chq - 1]['ssq'][6 + m])
                sq[m] = sch.op(
                    "scalar", lambda sc, m=m:
                    sc.activation(out=rsq[:, (m % 2) * CH:(m % 2 + 1) * CH],
                                  in_=resid[:, m * CH:(m + 1) * CH],
                                  func=AF.Square),
                    deps=sdeps, inc=env.s_ac)
                sums[m] = sch.op(
                    "tensor", lambda te, m=m:
                    te.matmul(ps_sum[:], env.ones_fc[:],
                              resid[:, m * CH:(m + 1) * CH],
                              start=(m == 0), stop=(m == 7)),
                    deps=[ra[m]] + ([prev[chq - 1]['rows0']]
                                    if (m == 0 and chq >= 1) else []),
                    inc=env.s_pe)
                ssqs[m] = sch.op(
                    "tensor", lambda te, m=m:
                    te.matmul(ps_ssq[:], env.ones_fc[:],
                              rsq[:, (m % 2) * CH:(m % 2 + 1) * CH],
                              start=(m == 0), stop=(m == 7)),
                    deps=[sq[m]] + ([prev[chq - 1]['rows1']]
                                    if (m == 0 and chq >= 1) else []),
                    inc=env.s_pe)
            c1 = sch.op("vector", lambda ve: ve.tensor_scalar_mul(
                out=rows[0:1, 0:CH], in0=ps_sum[:], scalar1=1.0 / D),
                deps=[sums[7]], inc=env.s_dv)
            c2 = sch.op("vector", lambda ve: ve.tensor_mul(
                out=rows[0:1, CH:2 * CH], in0=rows[0:1, 0:CH],
                in1=rows[0:1, 0:CH]), deps=[c1], inc=env.s_dv)
            c3 = sch.op("vector", lambda ve: ve.tensor_scalar_mul(
                out=rows[0:1, 4 * CH:5 * CH], in0=ps_ssq[:],
                scalar1=1.0 / D), deps=[ssqs[7]], inc=env.s_dv)
            c4 = sch.op("vector", lambda ve: ve.tensor_sub(
                out=rows[0:1, CH:2 * CH], in0=rows[0:1, 4 * CH:5 * CH],
                in1=rows[0:1, CH:2 * CH]), deps=[c2, c3], inc=env.s_dv)
            c5 = sch.op("scalar", lambda sc: sc.activation(
                out=rows[0:1, 5 * CH:6 * CH], in_=rows[0:1, CH:2 * CH],
                func=AF.Sqrt, bias=env.eps_t[:]), deps=[c4], inc=env.s_ac)
            c6 = sch.op("vector", lambda ve: ve.reciprocal(
                out=rows[0:1, 2 * CH:3 * CH], in_=rows[0:1, 5 * CH:6 * CH]),
                deps=[c5], inc=env.s_dv)
            c7 = sch.op("vector", lambda ve: ve.tensor_mul(
                out=rows[0:1, 3 * CH:4 * CH], in0=rows[0:1, 0:CH],
                in1=rows[0:1, 2 * CH:3 * CH]), deps=[c6], inc=env.s_dv)
            c8 = sch.op("vector", lambda ve: ve.tensor_scalar_mul(
                out=rows[0:1, 3 * CH:4 * CH], in0=rows[0:1, 3 * CH:4 * CH],
                scalar1=-1.0), deps=[c7], inc=env.s_dv)
            st['rows0'] = c1
            st['rows1'] = c3
            bdeps_extra = [prev[chq - 1]['abev']] if chq >= 1 else []
            bA = sch.op("tensor", lambda te: te.matmul(
                ps_A[:], env.ones_f1[0:1, :], rows[0:1, 2 * CH:3 * CH],
                start=True, stop=True), deps=[c6] + bdeps_extra, inc=env.s_pe)
            bB = sch.op("tensor", lambda te: te.matmul(
                ps_B[:], env.ones_f1[0:1, :], rows[0:1, 3 * CH:4 * CH],
                start=True, stop=True), deps=[c8] + bdeps_extra, inc=env.s_pe)
            evdeps = [bA, bB]
            if chq >= 1:
                evdeps.append(prev[chq - 1]['norm_last'])
            abev = sch.op("vector", lambda ve: [
                ve.tensor_copy(out=ab_sb[:, 0:CH], in_=ps_A[:]),
                ve.tensor_copy(out=ab_sb[:, CH:2 * CH], in_=ps_B[:])][-1],
                deps=evdeps, inc=env.s_dv)
            st['abev'] = abev
            norm_toks = [None] * 8
            cast2 = [None] * 8
            for m in range(8):
                n1 = sch.op("vector", lambda ve, m=m: ve.tensor_mul(
                    out=rsq[:, (m % 2) * CH:(m % 2 + 1) * CH],
                    in0=resid[:, m * CH:(m + 1) * CH], in1=ab_sb[:, 0:CH]),
                    deps=[abev, sq[m], sums[m], ssqs[m]], inc=env.s_dv)
                n2 = sch.op("vector", lambda ve, m=m: ve.tensor_add(
                    out=resid[:, m * CH:(m + 1) * CH],
                    in0=rsq[:, (m % 2) * CH:(m % 2 + 1) * CH],
                    in1=ab_sb[:, CH:2 * CH]), deps=[n1], inc=env.s_dv)
                n3 = sch.op("vector", lambda ve, m=m: ve.tensor_copy(
                    out=x1bs[:, m * CH:(m + 1) * CH],
                    in_=resid[:, m * CH:(m + 1) * CH]), deps=[n2],
                    inc=env.s_dv)
                norm_toks[m] = n2
                cast2[m] = n3
            st['norm_last'] = cast2[7]
            st['ra'] = ra
            st['sum'] = sums
            st['ssq'] = ssqs
            x1fdma = [None] * 8
            x1bdma = [None] * 8
            for m in range(8):
                x1fdma[m] = sch.op(
                    "gpsimd", lambda gp, m=m, chq=chq:
                    gp.dma_start(out=env.x1f[m * 128:(m + 1) * 128,
                                             chq * CH:(chq + 1) * CH],
                                 in_=resid[:, m * CH:(m + 1) * CH]),
                    deps=[norm_toks[m]], inc=env.s_xd[chq % 2], inck=16)
                x1bdma[m] = sch.op(
                    "gpsimd", lambda gp, m=m, chq=chq:
                    gp.dma_start(out=env.x1b[m * 128:(m + 1) * 128,
                                             chq * CH:(chq + 1) * CH],
                                 in_=x1bs[:, m * CH:(m + 1) * CH]),
                    deps=[cast2[m]], inc=env.s_xd[chq % 2], inck=16)
            st['x1fdma'] = x1fdma
            st['x1bdma'] = x1bdma
            st['wo'] = wo_toks
            lg_tok = None
            for tt in range(4):
                ldeps = [cast2[kd] for kd in range(8)]
                if tt >= 1:
                    ldeps += [st[f'exp{tt-1}'], st[f'm1red{tt-1}']]
                elif chq >= 1:
                    ldeps += [prev[chq - 1]['exp3'], prev[chq - 1]['m1red3']]
                lg_tok = sch.op(
                    "tensor", lambda te, tt=tt:
                    [te.matmul(ps_lg[:],
                               x1bs[:, kd * CH + tt * 128:
                                    kd * CH + (tt + 1) * 128],
                               rw_bf[:, kd * E:(kd + 1) * E],
                               start=(kd == 0), stop=(kd == 7))
                     for kd in range(8)][-1],
                    deps=ldeps, inc=env.s_pe)
                m1 = sch.op("vector", lambda ve: ve.reduce_max(
                    out=cwrow[:, 0:1], in_=ps_lg[:], axis=AXX),
                    deps=[lg_tok], inc=env.s_dv)
                m1n = sch.op("vector", lambda ve: ve.tensor_scalar_mul(
                    out=cwrow[:, 0:1], in0=cwrow[:, 0:1], scalar1=-1.0),
                    deps=[m1], inc=env.s_dv)
                st[f'm1red{tt}'] = m1n
                exdeps = [m1n]
                if tt >= 1:
                    exdeps.append(st[f'cwev{tt-1}'])
                elif chq >= 1:
                    exdeps.append(prev[chq - 1]['cwev3'])
                ex = sch.op("scalar", lambda sc: sc.activation(
                    out=cwtile[:], in_=ps_lg[:], func=AF.Exp,
                    bias=cwrow[:, 0:1]), deps=exdeps, inc=env.s_ac)
                st[f'exp{tt}'] = ex
                k1 = sch.op("vector", lambda ve: ve.reduce_max(
                    out=cwrow[:, 1:2], in_=cwtile[:], axis=AXX),
                    deps=[ex], inc=env.s_dv)
                k2 = sch.op("vector", lambda ve: ve.tensor_scalar(
                    out=cwmask[:], in0=cwtile[:], scalar1=cwrow[:, 1:2],
                    scalar2=-4.0, op0=ALU.is_ge, op1=ALU.mult),
                    deps=[k1], inc=env.s_dv)
                k3 = sch.op("vector", lambda ve: ve.tensor_add(
                    out=cwmask[:], in0=cwmask[:], in1=cwtile[:]),
                    deps=[k2], inc=env.s_dv)
                k4 = sch.op("vector", lambda ve: ve.reduce_max(
                    out=cwrow[:, 2:3], in_=cwmask[:], axis=AXX),
                    deps=[k3], inc=env.s_dv)
                k5 = sch.op("vector", lambda ve: ve.tensor_tensor(
                    out=cwrow[:, 3:4], in0=cwrow[:, 1:2], in1=cwrow[:, 2:3],
                    op=ALU.add), deps=[k4], inc=env.s_dv)
                k6 = sch.op("vector", lambda ve: ve.reciprocal(
                    out=cwrow[:, 4:5], in_=cwrow[:, 3:4]),
                    deps=[k5], inc=env.s_dv)
                k7 = sch.op("vector", lambda ve: ve.tensor_scalar(
                    out=cwmask[:], in0=cwtile[:], scalar1=cwrow[:, 2:3],
                    scalar2=None, op0=ALU.is_ge), deps=[k6, k4], inc=env.s_dv)
                k8 = sch.op("vector", lambda ve: ve.tensor_mul(
                    out=cwtile[:], in0=cwtile[:], in1=cwmask[:]),
                    deps=[k7], inc=env.s_dv)
                k9 = sch.op("vector", lambda ve: ve.tensor_scalar_mul(
                    out=cwtile[:], in0=cwtile[:], scalar1=cwrow[:, 4:5]),
                    deps=[k8], inc=env.s_dv)
                tdeps = [k9, env.consts_tok]
                if tt >= 1:
                    tdeps.append(st[f'cwev{tt-1}'])
                elif chq >= 1:
                    tdeps.append(prev[chq - 1]['cwev3'])
                tr = sch.op("tensor", lambda te: te.matmul(
                    ps_ct[:], cwtile[:], env.ident[:],
                    start=True, stop=True), deps=tdeps, inc=env.s_pe)
                cwev = sch.op(
                    "vector", lambda ve, chq=chq, tt=tt:
                    ve.tensor_copy(out=env.cwT[:, chq * CH + tt * 128:
                                               chq * CH + (tt + 1) * 128],
                                   in_=ps_ct[:]), deps=[tr], inc=env.s_dv)
                st[f'cwev{tt}'] = cwev
            st['lg_last'] = lg_tok
            prev[chq] = st
        sch.drain(env.s_xd[0])
        sch.drain(env.s_xd[1])
        sch.run("p6")


def _moe_pass(env, th):
    nc = env.nc
    T, THL, NTCH = env.T, env.THL, env.NTCH
    WB = env.wsets
    final_acc = {}
    with nc.sbuf_tensor(f"acc_{th}", [128, 8 * THL], F32) as accT, \
         nc.sbuf_tensor(f"x1h_{th}", [128, 8 * THL], BF16) as x1h:
        with nc.sbuf_tensor(f"w1b_{th}", [128, 2 * 8 * 1024], BF16) as w1b, \
             nc.sbuf_tensor(f"w2b_{th}", [128, 8 * 1024], BF16) as w2b, \
             nc.sbuf_tensor(f"hsb_{th}", [128, 8 * THL], BF16) as hsb, \
             nc.sbuf_tensor(f"stgw0_{th}", [128, 1024], F32) as stgw0, \
             nc.sbuf_tensor(f"stgw1_{th}", [128, 1024], F32) as stgw1, \
             nc.sbuf_tensor(f"cwb_{th}", [128, 2 * NTCH * CH], BF16) as cwb, \
             nc.psum_tensor(f"ph0_{th}", [128, CH], F32) as ph0, \
             nc.psum_tensor(f"ph1_{th}", [128, CH], F32) as ph1, \
             nc.psum_tensor(f"py0_{th}", [128, CH], F32) as py0, \
             nc.psum_tensor(f"py1_{th}", [128, CH], F32) as py1, \
             nc.psum_tensor(f"pcw0_{th}", [128, CH], F32) as pcw0, \
             nc.psum_tensor(f"pcw1_{th}", [128, CH], F32) as pcw1:
            stgw = [stgw0, stgw1]
            ps_h = [ph0, ph1]
            ps_y = [py0, py1]
            pcw = [pcw0, pcw1]
            sch = Sched(nc)
            x1toks = []
            for m in range(8):
                x1toks.append(sch.op(
                    "sync", lambda sy, m=m, th=th:
                    sy.dma_start(out=x1h[:, m * THL:(m + 1) * THL],
                                 in_=env.x1b[m * 128:(m + 1) * 128,
                                             th * THL:(th + 1) * THL]),
                    inc=env.s_h1, inck=16))
            x1last = x1toks[-1]
            acc_clear = []
            for m in range(8):
                acc_clear.append(sch.op(
                    "gpsimd", lambda gp, m=m:
                    gp.memset(accT[:, m * THL:(m + 1) * THL], 0.0),
                    inc=env.s_pl))

            cast_tok = {}
            step_hgrp = {}
            step_silu = {}
            step_scale = {}
            step_ygrp = {}
            step_acc = {}
            cwb_ev = {}
            dma_i = 0
            cast_list = []
            for s in range(16):
                e, fh = s // 2, s % 2
                for widx in range(16):
                    isw1 = widx < 8
                    kk = widx % 8
                    if isw1:
                        src = env.w1[e, kk * 128:(kk + 1) * 128,
                                     fh * 1024:(fh + 1) * 1024]
                    else:
                        src = env.w2[e, fh * 1024 + kk * 128:
                                     fh * 1024 + (kk + 1) * 128, :]
                    deps = [cast_list[dma_i - 2]] if dma_i >= 2 else []
                    dt_ = sch.op("sync", lambda sy, src=src, dma_i=dma_i:
                                 sy.dma_start(out=stgw[dma_i % 2][:], in_=src),
                                 deps=deps, inc=env.s_stg[dma_i % 2], inck=16)
                    cdeps = [dt_]
                    if isw1:
                        if s >= 2:
                            cdeps.append(step_hgrp[(s - 2, 7, NTCH - 1)])
                        ws = (s % 2) * 8
                    else:
                        if s >= 1:
                            cdeps.append(step_ygrp[(s - 1, 7, NTCH - 1)])
                        ws = 0
                    dst = w1b if isw1 else w2b
                    ct = sch.op(
                        "vector", lambda ve, dst=dst, kk=kk, dma_i=dma_i, ws=ws:
                        ve.tensor_copy(
                            out=dst[:, (ws + kk) * 1024:(ws + kk + 1) * 1024],
                            in_=stgw[dma_i % 2][:]),
                        deps=cdeps, inc=env.s_dv)
                    cast_tok[(s, 'w1' if isw1 else 'w2', kk)] = ct
                    cast_list.append(ct)
                    dma_i += 1
                if fh == 0:
                    for tch in range(NTCH):
                        deps = [env.consts_tok]
                        if tch >= 2:
                            deps.append(cwb_ev[(e, tch - 2)])
                        if e >= 2:
                            deps += [step_scale[(2 * e - 3, fm, tch)]
                                     for fm in range(8)]
                            deps.append(cwb_ev[(e - 2, NTCH - 1)])
                        bc = sch.op(
                            "tensor", lambda te, e=e, tch=tch, th=th:
                            te.matmul(pcw[tch % 2][:],
                                      env.sel8e[:, e * 128:(e + 1) * 128],
                                      env.cwT[0:8, th * THL + tch * CH:
                                              th * THL + (tch + 1) * CH],
                                      start=True, stop=True),
                            deps=deps, inc=env.s_pe)
                        cwb_ev[(e, tch)] = sch.op(
                            "scalar", lambda sc, e=e, tch=tch:
                            sc.copy(out=cwb[:, ((e % 2) * NTCH + tch) * CH:
                                            ((e % 2) * NTCH + tch + 1) * CH],
                                    in_=pcw[tch % 2][:]),
                            deps=[bc], inc=env.s_ac)
                w1last = cast_tok[(s, 'w1', 7)]
                hidx = 0
                for fm in range(8):
                    for tch in range(NTCH):
                        deps = [w1last, x1last]
                        if hidx >= 2:
                            deps.append(step_silu[(s, (hidx - 2) // NTCH,
                                                  (hidx - 2) % NTCH)])
                        elif s >= 1:
                            php = 8 * NTCH - 2 + hidx
                            deps.append(step_silu[(s - 1, php // NTCH,
                                                  php % NTCH)])
                        ht = sch.op(
                            "tensor", lambda te, s=s, fm=fm, tch=tch,
                            hidx=hidx, WB=WB:
                            [te.matmul(
                                ps_h[hidx % 2][:],
                                env_w1b_slice(w1b, s, kd, fm, 2),
                                x1h[:, kd * THL + tch * CH:
                                    kd * THL + (tch + 1) * CH],
                                start=(kd == 0), stop=(kd == 7))
                             for kd in range(8)][-1],
                            deps=deps, inc=env.s_pe)
                        step_hgrp[(s, fm, tch)] = ht
                        sdeps = [ht]
                        if s >= 1:
                            sdeps.append(step_scale[(s - 1, fm, tch)])
                            sdeps.append(step_ygrp[(s - 1, 7, tch)])
                        stk = sch.op(
                            "scalar", lambda sc, fm=fm, tch=tch, hidx=hidx:
                            sc.activation(
                                out=hsb[:, fm * THL + tch * CH:
                                        fm * THL + (tch + 1) * CH],
                                in_=ps_h[hidx % 2][:], func=AF.Silu),
                            deps=sdeps, inc=env.s_ac)
                        step_silu[(s, fm, tch)] = stk
                        ptk = sch.op(
                            "gpsimd", lambda gp, fm=fm, tch=tch, e=e:
                            gp.tensor_mul(
                                out=hsb[:, fm * THL + tch * CH:
                                        fm * THL + (tch + 1) * CH],
                                in0=hsb[:, fm * THL + tch * CH:
                                        fm * THL + (tch + 1) * CH],
                                in1=cwb[:, ((e % 2) * NTCH + tch) * CH:
                                        ((e % 2) * NTCH + tch + 1) * CH]),
                            deps=[stk, cwb_ev[(e, tch)]], inc=env.s_pl)
                        step_scale[(s, fm, tch)] = ptk
                        hidx += 1
                w2last = cast_tok[(s, 'w2', 7)]
                yidx = 0
                for tch in range(NTCH):
                    scale_deps = [step_scale[(s, fm, tch)] for fm in range(8)]
                    for ym in range(8):
                        deps = [w2last] + scale_deps
                        if yidx >= 2:
                            deps.append(step_acc[(s, (yidx - 2) % 8,
                                                 (yidx - 2) // 8)])
                        elif s >= 1:
                            pyp = 8 * NTCH - 2 + yidx
                            deps.append(step_acc[(s - 1, pyp % 8, pyp // 8)])
                        yt = sch.op(
                            "tensor", lambda te, s=s, ym=ym, tch=tch,
                            yidx=yidx, WB=WB:
                            [te.matmul(
                                ps_y[yidx % 2][:],
                                env_w1b_slice(w2b, s, kf, ym, 1),
                                hsb[:, kf * THL + tch * CH:
                                    kf * THL + (tch + 1) * CH],
                                start=(kf == 0), stop=(kf == 7))
                             for kf in range(8)][-1],
                            deps=deps, inc=env.s_pe)
                        step_ygrp[(s, ym, tch)] = yt
                        adeps = [yt]
                        if s >= 1:
                            adeps.append(step_acc[(s - 1, ym, tch)])
                        else:
                            adeps.append(acc_clear[ym])
                        at = sch.op(
                            "vector", lambda ve, ym=ym, tch=tch, yidx=yidx:
                            ve.tensor_add(
                                out=accT[:, ym * THL + tch * CH:
                                         ym * THL + (tch + 1) * CH],
                                in0=accT[:, ym * THL + tch * CH:
                                         ym * THL + (tch + 1) * CH],
                                in1=ps_y[yidx % 2][:]),
                            deps=adeps, inc=env.s_dv)
                        step_acc[(s, ym, tch)] = at
                        yidx += 1
            for ym in range(8):
                for tch in range(NTCH):
                    final_acc[(ym, tch)] = step_acc[(15, ym, tch)]
            if getattr(env, 'dbg_moe', None) is not None:
                for m in range(8):
                    sch.op("gpsimd", lambda gp, m=m, th=th: gp.dma_start(
                        out=env.dbg_moe[m * 128:(m + 1) * 128,
                                        th * THL:(th + 1) * THL],
                        in_=accT[:, m * THL:(m + 1) * THL]),
                        deps=[step_acc[(15, m, tch)] for tch in range(NTCH)],
                        inc=env.s_dbg, inck=16)
            sch.drain(env.s_dbg)
            sch.run(f"moe{th}")

        _ln2(env, th, accT, final_acc)


def env_w1b_slice(wb, s, kd, fm, WB=1):
    base = ((s % WB) * 8 + kd) * 1024
    return wb[:, base + fm * 128: base + (fm + 1) * 128]


def _ln2(env, th, accT, final_acc):
    nc = env.nc
    THL, NTCH = env.THL, env.NTCH
    with nc.sbuf_tensor(f"xst2_{th}", [128, 16 * CH], F32) as xst2, \
         nc.sbuf_tensor(f"res2_{th}", [128, 8 * CH], F32) as res2, \
         nc.sbuf_tensor(f"rsq2_{th}", [128, 2 * CH], F32) as rsq2, \
         nc.sbuf_tensor(f"outf_{th}", [128, 8 * CH], F32) as outf, \
         nc.sbuf_tensor(f"ab2_{th}", [128, 2 * CH], F32) as ab2, \
         nc.sbuf_tensor(f"rows2_{th}", [1, 6 * CH], F32) as rows2, \
         nc.psum_tensor(f"ps2s_{th}", [1, CH], F32) as ps2s, \
         nc.psum_tensor(f"ps2q_{th}", [1, CH], F32) as ps2q, \
         nc.psum_tensor(f"ps2A_{th}", [128, CH], F32) as ps2A, \
         nc.psum_tensor(f"ps2B_{th}", [128, CH], F32) as ps2B:
        sch = Sched(nc)
        prev2 = {}
        for c in range(NTCH):
            st = {}
            chg = th * NTCH + c
            xtoks = []
            for m in range(8):
                deps = [final_acc[(m, c)]]
                if c >= 2:
                    deps.append(prev2[c - 2]['ra'][m])
                xtoks.append(sch.op(
                    "sync", lambda sy, m=m, c=c, chg=chg:
                    sy.dma_start(
                        out=xst2[:, ((c % 2) * 8 + m) * CH:
                                 ((c % 2) * 8 + m + 1) * CH],
                        in_=env.x1f[m * 128:(m + 1) * 128,
                                    chg * CH:(chg + 1) * CH]),
                    deps=deps, inc=env.s_xs[c % 2], inck=16))
            ra = [None] * 8
            sq = [None] * 8
            sums = [None] * 8
            ssqs = [None] * 8
            for m in range(8):
                rdeps = [xtoks[-1], final_acc[(m, c)]]
                if c >= 1:
                    rdeps.append(prev2[c - 1]['sum'][m])
                    rdeps.append(prev2[c - 1]['norm'][m])
                    if 'dbgres' in prev2[c - 1]:
                        rdeps.append(prev2[c - 1]['dbgres'][m])
                ra[m] = sch.op(
                    "vector", lambda ve, m=m, c=c:
                    ve.tensor_add(
                        out=res2[:, m * CH:(m + 1) * CH],
                        in0=accT[:, m * THL + c * CH: m * THL + (c + 1) * CH],
                        in1=xst2[:, ((c % 2) * 8 + m) * CH:
                                 ((c % 2) * 8 + m + 1) * CH]),
                    deps=rdeps, inc=env.s_dv)
                sdeps = [ra[m]]
                if m >= 2:
                    sdeps.append(ssqs[m - 2])
                elif c >= 1:
                    sdeps.append(prev2[c - 1]['ssq'][6 + m])
                sq[m] = sch.op(
                    "scalar", lambda sc, m=m:
                    sc.activation(out=rsq2[:, (m % 2) * CH:(m % 2 + 1) * CH],
                                  in_=res2[:, m * CH:(m + 1) * CH],
                                  func=AF.Square),
                    deps=sdeps, inc=env.s_ac)
                sums[m] = sch.op(
                    "tensor", lambda te, m=m:
                    te.matmul(ps2s[:], env.ones_fc[:],
                              res2[:, m * CH:(m + 1) * CH],
                              start=(m == 0), stop=(m == 7)),
                    deps=[ra[m]] + ([prev2[c - 1]['rows0']]
                                    if (m == 0 and c >= 1) else []),
                    inc=env.s_pe)
                ssqs[m] = sch.op(
                    "tensor", lambda te, m=m:
                    te.matmul(ps2q[:], env.ones_fc[:],
                              rsq2[:, (m % 2) * CH:(m % 2 + 1) * CH],
                              start=(m == 0), stop=(m == 7)),
                    deps=[sq[m]] + ([prev2[c - 1]['rows1']]
                                    if (m == 0 and c >= 1) else []),
                    inc=env.s_pe)
            c1 = sch.op("vector", lambda ve: ve.tensor_scalar_mul(
                out=rows2[0:1, 0:CH], in0=ps2s[:], scalar1=1.0 / D),
                deps=[sums[7]], inc=env.s_dv)
            c2 = sch.op("vector", lambda ve: ve.tensor_mul(
                out=rows2[0:1, CH:2 * CH], in0=rows2[0:1, 0:CH],
                in1=rows2[0:1, 0:CH]), deps=[c1], inc=env.s_dv)
            c3 = sch.op("vector", lambda ve: ve.tensor_scalar_mul(
                out=rows2[0:1, 4 * CH:5 * CH], in0=ps2q[:], scalar1=1.0 / D),
                deps=[ssqs[7]], inc=env.s_dv)
            c4 = sch.op("vector", lambda ve: ve.tensor_sub(
                out=rows2[0:1, CH:2 * CH], in0=rows2[0:1, 4 * CH:5 * CH],
                in1=rows2[0:1, CH:2 * CH]), deps=[c2, c3], inc=env.s_dv)
            c5 = sch.op("scalar", lambda sc: sc.activation(
                out=rows2[0:1, 5 * CH:6 * CH], in_=rows2[0:1, CH:2 * CH],
                func=AF.Sqrt, bias=env.eps_t[:]), deps=[c4], inc=env.s_ac)
            c6 = sch.op("vector", lambda ve: ve.reciprocal(
                out=rows2[0:1, 2 * CH:3 * CH], in_=rows2[0:1, 5 * CH:6 * CH]),
                deps=[c5], inc=env.s_dv)
            c7 = sch.op("vector", lambda ve: ve.tensor_mul(
                out=rows2[0:1, 3 * CH:4 * CH], in0=rows2[0:1, 0:CH],
                in1=rows2[0:1, 2 * CH:3 * CH]), deps=[c6], inc=env.s_dv)
            c8 = sch.op("vector", lambda ve: ve.tensor_scalar_mul(
                out=rows2[0:1, 3 * CH:4 * CH], in0=rows2[0:1, 3 * CH:4 * CH],
                scalar1=-1.0), deps=[c7], inc=env.s_dv)
            st['rows0'] = c1
            st['rows1'] = c3
            bdeps_extra = [prev2[c - 1]['abev']] if c >= 1 else []
            bA = sch.op("tensor", lambda te: te.matmul(
                ps2A[:], env.ones_f1[0:1, :], rows2[0:1, 2 * CH:3 * CH],
                start=True, stop=True), deps=[c6] + bdeps_extra, inc=env.s_pe)
            bB = sch.op("tensor", lambda te: te.matmul(
                ps2B[:], env.ones_f1[0:1, :], rows2[0:1, 3 * CH:4 * CH],
                start=True, stop=True), deps=[c8] + bdeps_extra, inc=env.s_pe)
            evdeps = [bA, bB]
            if c >= 1:
                evdeps.append(prev2[c - 1]['norm'][7])
                if 'dbgab' in prev2[c - 1]:
                    evdeps.append(prev2[c - 1]['dbgab'])
            abev = sch.op("vector", lambda ve: [
                ve.tensor_copy(out=ab2[:, 0:CH], in_=ps2A[:]),
                ve.tensor_copy(out=ab2[:, CH:2 * CH], in_=ps2B[:])][-1],
                deps=evdeps, inc=env.s_dv)
            st['abev'] = abev
            if getattr(env, 'dbg_ab', None) is not None:
                st['dbgab'] = sch.op(
                    "gpsimd", lambda gp, th=th, c=c: gp.dma_start(
                        out=env.dbg_ab[:, (th * env.NTCH + c) * 2 * CH:
                                       (th * env.NTCH + c + 1) * 2 * CH],
                        in_=ab2[:]), deps=[abev], inc=env.s_dbg, inck=16)
                st['dbgres'] = [None] * 8
                for m in range(8):
                    st['dbgres'][m] = sch.op(
                        "gpsimd", lambda gp, m=m, chg=chg: gp.dma_start(
                            out=env.dbg_res2[m * 128:(m + 1) * 128,
                                             chg * CH:(chg + 1) * CH],
                            in_=res2[:, m * CH:(m + 1) * CH]),
                        deps=[ra[m], abev], inc=env.s_dbg, inck=16)
            norm = [None] * 8
            odma = [None] * 8
            for m in range(8):
                n1 = sch.op("vector", lambda ve, m=m: ve.tensor_mul(
                    out=rsq2[:, (m % 2) * CH:(m % 2 + 1) * CH],
                    in0=res2[:, m * CH:(m + 1) * CH], in1=ab2[:, 0:CH]),
                    deps=[abev, sq[m], sums[m], ssqs[m]], inc=env.s_dv)
                ndeps = [n1]
                if c >= 1:
                    ndeps.append(prev2[c - 1]['odma'][7])
                n2 = sch.op("vector", lambda ve, m=m: ve.tensor_add(
                    out=outf[:, m * CH:(m + 1) * CH],
                    in0=rsq2[:, (m % 2) * CH:(m % 2 + 1) * CH],
                    in1=ab2[:, CH:2 * CH]), deps=ndeps, inc=env.s_dv)
                norm[m] = n2
                odma[m] = sch.op(
                    "gpsimd", lambda gp, m=m, chg=chg:
                    gp.dma_start(out=env.outT[m * 128:(m + 1) * 128,
                                              chg * CH:(chg + 1) * CH],
                                 in_=outf[:, m * CH:(m + 1) * CH]),
                    deps=[n2], inc=env.s_od[c % 2], inck=16)
            st['norm'] = norm
            st['odma'] = odma
            st['ra'] = ra
            st['sum'] = sums
            st['ssq'] = ssqs
            prev2[c] = st
        sch.drain(env.s_od[0])
        sch.drain(env.s_od[1])
        sch.drain(env.s_xs[0])
        sch.drain(env.s_xs[1])
        sch.run(f"ln2_{th}")


def build(T=4096, debug=False, wsets=1):
    assert T % 1024 == 0
    env = Env()
    env.T = T
    env.wsets = wsets
    env.NCH = T // CH
    env.NKT = T // 128
    env.THL = T // 2
    env.NTCH = env.THL // CH

    nc = bass.Bass("TRN2", target_bir_lowering=False)
    env.nc = nc

    env.xT = nc.dram_tensor("xT", [D, T], F32, kind="ExternalInput").ap()
    env.wdkv = nc.dram_tensor("wdkv", [D, DL], F32, kind="ExternalInput").ap()
    env.wuk = nc.dram_tensor("wuk", [DL, D], F32, kind="ExternalInput").ap()
    env.wuv = nc.dram_tensor("wuv", [DL, D], F32, kind="ExternalInput").ap()
    env.wuq = nc.dram_tensor("wuq", [D, D], F32, kind="ExternalInput").ap()
    env.wo = nc.dram_tensor("wo", [D, D], F32, kind="ExternalInput").ap()
    env.rw = nc.dram_tensor("rw", [D, E], F32, kind="ExternalInput").ap()
    env.w1 = nc.dram_tensor("w1", [E, D, FF], F32, kind="ExternalInput").ap()
    env.w2 = nc.dram_tensor("w2", [E, FF, D], F32, kind="ExternalInput").ap()
    env.c_ident = nc.dram_tensor("c_ident", [128, 128], F32,
                                 kind="ExternalInput").ap()
    env.c_sel16 = nc.dram_tensor("c_sel16", [128, 128], BF16,
                                 kind="ExternalInput").ap()
    env.c_sel8h = nc.dram_tensor("c_sel8h", [128, 8], BF16,
                                 kind="ExternalInput").ap()
    env.c_sel8e = nc.dram_tensor("c_sel8e", [8, 1024], BF16,
                                 kind="ExternalInput").ap()

    env.outT = nc.dram_tensor("outT", [D, T], F32, kind="ExternalOutput").ap()
    if debug:
        env.x1f = nc.dram_tensor("x1f", [D, T], F32, kind="ExternalOutput").ap()
        env.x1b = nc.dram_tensor("x1b", [D, T], BF16, kind="ExternalOutput").ap()
    else:
        env.x1f = nc.dram_tensor("x1f", [D, T], F32).ap()
        env.x1b = nc.dram_tensor("x1b", [D, T], BF16).ap()
    if debug:
        env.dbg_ckv = nc.dram_tensor("dbg_ckv", [DL, T], BF16,
                                     kind="ExternalOutput").ap()
        env.dbg_q = nc.dram_tensor("dbg_q", [D, T], BF16,
                                   kind="ExternalOutput").ap()
        env.dbg_cw = nc.dram_tensor("dbg_cw", [8, T], BF16,
                                    kind="ExternalOutput").ap()
        env.dbg_moe = nc.dram_tensor("dbg_moe", [D, T], F32,
                                     kind="ExternalOutput").ap()
        env.dbg_res2 = nc.dram_tensor("dbg_res2", [D, T], F32,
                                      kind="ExternalOutput").ap()
        env.dbg_ab = nc.dram_tensor("dbg_ab", [128, 16 * CH], F32,
                                    kind="ExternalOutput").ap()

    ctx = ExitStack()
    EC = ctx.enter_context

    env.s_pe = Ctr(EC(nc.semaphore("s_pe")))
    env.s_ac = Ctr(EC(nc.semaphore("s_ac")))
    env.s_dv = Ctr(EC(nc.semaphore("s_dv")))
    env.s_pl = Ctr(EC(nc.semaphore("s_pl")))
    env.s_cst = Ctr(EC(nc.semaphore("s_cst")))
    env.s_stg = [Ctr(EC(nc.semaphore(f"s_stg{i}"))) for i in range(2)]
    env.s_xstg = [Ctr(EC(nc.semaphore(f"s_xstg{i}"))) for i in range(2)]
    env.s_cx = [Ctr(EC(nc.semaphore(f"s_cx{i}"))) for i in range(2)]
    env.s_xs = [Ctr(EC(nc.semaphore(f"s_xs{i}"))) for i in range(2)]
    env.s_xd = [Ctr(EC(nc.semaphore(f"s_xd{i}"))) for i in range(2)]
    env.s_h1 = Ctr(EC(nc.semaphore("s_h1")))
    env.s_od = [Ctr(EC(nc.semaphore(f"s_od{i}"))) for i in range(2)]
    env.s_cd = [Ctr(EC(nc.semaphore(f"s_cd{i}"))) for i in range(2)]
    env.s_msc = Ctr(EC(nc.semaphore("s_msc")))
    env.s_dbg = Ctr(EC(nc.semaphore("s_dbg")))
    env.all_dma_ctrs = ([env.s_cst, env.s_h1, env.s_msc, env.s_dbg]
                        + env.s_stg + env.s_xstg + env.s_cx + env.s_xs
                        + env.s_xd + env.s_od + env.s_cd)

    env.ones_f1 = EC(nc.sbuf_tensor("ones_f1", [1, 128], F32))
    env.ones_fc = EC(nc.sbuf_tensor("ones_fc", [128, 1], F32))
    env.eps_t = EC(nc.sbuf_tensor("eps_t", [1, 1], F32))
    env.ident = EC(nc.sbuf_tensor("ident", [128, 128], F32))
    env.sel16 = EC(nc.sbuf_tensor("sel16", [128, 128], BF16))
    env.sel8h = EC(nc.sbuf_tensor("sel8h", [128, 8], BF16))
    env.sel8e = EC(nc.sbuf_tensor("sel8e", [8, 1024], BF16))
    env.qmax16 = EC(nc.sbuf_tensor("qmax16", [16, 1], F32))
    env.qmaxG = EC(nc.sbuf_tensor("qmaxG", [4, 1], F32))
    env.nrm_tmp = EC(nc.sbuf_tensor("nrm_tmp", [16, 1], F32))
    env.cwT = EC(nc.sbuf_tensor("cwT", [8, T], BF16))
    env.ctxh = nc.dram_tensor("ctxh", [D, T], BF16,
                              kind="ExternalOutput" if debug else "Internal").ap()

    es_w = ExitStack()
    wuk_bf = es_w.enter_context(nc.sbuf_tensor("wuk_bf", [128, 2 * D], BF16))
    wuv_bf = es_w.enter_context(nc.sbuf_tensor("wuv_bf", [128, 2 * D], BF16))
    es_ckq = ExitStack()
    env.c_kvT = [es_ckq.enter_context(nc.sbuf_tensor(f"ckv{i}", [128, T], BF16))
                 for i in range(2)]
    env.qT = [es_ckq.enter_context(nc.sbuf_tensor(f"qT{i}", [128, T], BF16))
              for i in range(8)]

    es_x = ExitStack()
    wdkv_bf = es_x.enter_context(nc.sbuf_tensor("wdkv_bf", [128, 8 * DL], BF16))
    wuq_bf = es_x.enter_context(nc.sbuf_tensor("wuq_bf", [128, 8 * D], BF16))
    xT_bf = es_x.enter_context(nc.sbuf_tensor("xT_bf", [128, 8 * env.T], BF16))

    _p0(env, wdkv_bf, wuk_bf, wuv_bf, wuq_bf, xT_bf)

    groups_ckv = [(m, ch) for m in range(2) for ch in range(env.NCH)]
    _proj_phase(
        env, "ckv", groups_ckv, 8,
        lambda g, k: wdkv_bf[:, k * DL + g[0] * 128: k * DL + (g[0] + 1) * 128],
        lambda g, k: xT_bf[:, k * T + g[1] * CH: k * T + (g[1] + 1) * CH],
        lambda ve, g, ps: ve.tensor_copy(
            out=env.c_kvT[g[0]][:, g[1] * CH:(g[1] + 1) * CH], in_=ps[:]),
    )
    groups_q = [(m, ch) for m in range(8) for ch in range(env.NCH)]
    _proj_phase(
        env, "q", groups_q, 8,
        lambda g, k: wuq_bf[:, k * D + g[0] * 128: k * D + (g[0] + 1) * 128],
        lambda g, k: xT_bf[:, k * T + g[1] * CH: k * T + (g[1] + 1) * CH],
        lambda ve, g, ps: ve.tensor_copy(
            out=env.qT[g[0]][:, g[1] * CH:(g[1] + 1) * CH], in_=ps[:]),
    )
    _qnorms(env)

    if debug:
        sch = Sched(nc)
        for m in range(2):
            sch.op("gpsimd", lambda gp, m=m: gp.dma_start(
                out=env.dbg_ckv[m * 128:(m + 1) * 128, :],
                in_=env.c_kvT[m][:]), inc=env.s_dbg, inck=16)
        for m in range(8):
            sch.op("gpsimd", lambda gp, m=m: gp.dma_start(
                out=env.dbg_q[m * 128:(m + 1) * 128, :], in_=env.qT[m][:]),
                inc=env.s_dbg, inck=16)
        sch.drain(env.s_dbg)
        sch.run("dbgq")

    es_x.close()

    for g in range(NH // HG):
        _attn_group(env, g, wuk_bf, wuv_bf)

    es_ckq.close()
    es_w.close()

    es_p6 = ExitStack()
    wo_bf = es_p6.enter_context(nc.sbuf_tensor("wo_bf", [128, 8 * D], BF16))
    rw_bf = es_p6.enter_context(nc.sbuf_tensor("rw_bf", [128, 8 * E], BF16))
    with nc.sbuf_tensor("wstg0b", [128, D], F32) as wstg0b, \
         nc.sbuf_tensor("wstg1b", [128, D], F32) as wstg1b:
        wp2 = []
        for kd in range(8):
            wp2.append((wo_bf, kd * D, D, env.wo[kd * 128:(kd + 1) * 128, :]))
        for kd in range(8):
            wp2.append((rw_bf, kd * E, E, env.rw[kd * 128:(kd + 1) * 128, :]))
        _stream_pieces(env, wp2, [wstg0b, wstg1b])

    _p6(env, wo_bf, rw_bf)
    es_p6.close()

    for th in range(2):
        _moe_pass(env, th)

    if debug:
        sch = Sched(nc)
        sch.op("gpsimd", lambda gp: gp.dma_start(out=env.dbg_cw[:],
                                                 in_=env.cwT[:]),
               inc=env.s_dbg, inck=16)
        sch.drain(env.s_dbg)
        sch.run("dbgcw")

    with nc.Block("fin") as blk:
        def fin(gp):
            for ctr in env.all_dma_ctrs:
                if ctr.n > 0:
                    gp.wait_ge(ctr.h, ctr.n)
        blk.gpsimd(fin)

    ctx.close()
    return nc


def _consts():
    ident = np.eye(128, dtype=np.float32)
    sel16 = np.zeros((128, 128), dtype=ml_dtypes.bfloat16)
    for kd in range(8):
        sel16[0:64, 16 * kd + 2 * kd] = 1
        sel16[64:128, 16 * kd + 2 * kd + 1] = 1
    sel8h = np.zeros((128, 8), dtype=ml_dtypes.bfloat16)
    for p in range(2):
        sel8h[0:64, 4 * p + 2 * p] = 1
        sel8h[64:128, 4 * p + 2 * p + 1] = 1
    sel8e = np.zeros((8, 1024), dtype=ml_dtypes.bfloat16)
    for e in range(8):
        sel8e[e, e * 128:(e + 1) * 128] = 1
    return {"c_ident": ident, "c_sel16": sel16, "c_sel8h": sel8h,
            "c_sel8e": sel8e}


def kernel(L0_attn_x, L0_attn_W_DKV, L0_attn_W_UK, L0_attn_W_UV, L0_attn_W_UQ,
           L0_attn_W_O, L0_ffn_router_w, L0_ffn_w1, L0_ffn_w2):
    x = np.ascontiguousarray(np.asarray(L0_attn_x, dtype=np.float32))
    assert x.shape == (8, 4096, 1024)
    shared = {
        "wdkv": np.ascontiguousarray(np.asarray(L0_attn_W_DKV, np.float32)),
        "wuk": np.ascontiguousarray(np.asarray(L0_attn_W_UK, np.float32)),
        "wuv": np.ascontiguousarray(np.asarray(L0_attn_W_UV, np.float32)),
        "wuq": np.ascontiguousarray(np.asarray(L0_attn_W_UQ, np.float32)),
        "wo": np.ascontiguousarray(np.asarray(L0_attn_W_O, np.float32)),
        "rw": np.ascontiguousarray(np.asarray(L0_ffn_router_w, np.float32)),
        "w1": np.ascontiguousarray(np.asarray(L0_ffn_w1, np.float32)),
        "w2": np.ascontiguousarray(np.asarray(L0_ffn_w2, np.float32)),
    }
    shared.update(_consts())
    nc = build(T=4096, debug=False)
    in_maps = []
    for b in range(8):
        m = dict(shared)
        m["xT"] = np.ascontiguousarray(x[b].T)
        in_maps.append(m)
    res = run_bass_kernel_spmd(nc, in_maps, core_ids=list(range(8)))
    out = np.stack([np.asarray(r["outT"], dtype=np.float32).T
                    for r in res.results], axis=0)
    return out


# version 06
# speedup vs baseline: 1.0980x; 1.0419x over previous
import numpy as np
import ml_dtypes
from contextlib import ExitStack

import concourse.bass as bass
import concourse.mybir as mybir
from concourse.bass_utils import run_bass_kernel_spmd

AF = mybir.ActivationFunctionType
ALU = mybir.AluOpType
AXX = mybir.AxisListType.X
F32 = mybir.dt.float32
BF16 = mybir.dt.bfloat16

D = 1024
DL = 256
NH = 16
HD = 64
E = 8
FF = 2048
CH = 512
HG = 4
EPS_LN = 1e-5
ENGINES = ("sync", "tensor", "scalar", "vector", "gpsimd")


class Ctr:
    def __init__(self, h):
        self.h = h
        self.n = 0


class Tok:
    __slots__ = ("ctr", "val", "k")

    def __init__(self, ctr, k, val):
        self.ctr = ctr
        self.k = k
        self.val = val


class Sched:

    def __init__(self, nc):
        self.nc = nc
        self.items = []

    def op(self, eng, fn, deps=(), inc=None, inck=1):
        deps = tuple(d for d in deps if d is not None)
        tok = None
        if inc is not None:
            inc.n += inck
            tok = Tok(inc, inck, inc.n)
        self.items.append((eng, fn, deps, tok))
        return tok

    def drain(self, ctr, eng="gpsimd"):
        v = ctr.n
        if v > 0:
            self.op(eng, lambda e, h=ctr.h, v=v: e.wait_ge(h, v))

    def run(self, name=None):
        with self.nc.Block(name) as blk:
            for eng in ENGINES:
                items = [it for it in self.items if it[0] == eng]
                if not items:
                    continue

                def body(e, items=items):
                    last = {}
                    for (_, fn, deps, tok) in items:
                        for d in deps:
                            if last.get(id(d.ctr), -1) < d.val:
                                e.wait_ge(d.ctr.h, d.val)
                                last[id(d.ctr)] = d.val
                        inst = fn(e)
                        if tok is not None:
                            inst.then_inc(tok.ctr.h, tok.k)
                getattr(blk, eng)(body)
        self.items = []


class Env:
    pass


def _stream_pieces(env, pieces, stg, dma_ctr=None, cast_eng="vector"):
    nc = env.nc
    sch = Sched(nc)
    toks = []
    cast_ctr = env.s_dv if cast_eng == "vector" else env.s_pl
    for i, (dst, c0, ncol, src) in enumerate(pieces):
        deps = [toks[i - 2]] if i >= 2 else []
        dt_ = sch.op("sync", lambda sy, i=i, ncol=ncol, src=src:
                     sy.dma_start(out=stg[i % 2][:, 0:ncol], in_=src),
                     deps=deps, inc=env.s_stg[i % 2], inck=16)
        ct = sch.op(cast_eng, lambda e, i=i, dst=dst, c0=c0, ncol=ncol:
                    e.tensor_copy(out=dst[:, c0:c0 + ncol],
                                  in_=stg[i % 2][:, 0:ncol]),
                    deps=[dt_], inc=cast_ctr)
        toks.append(ct)
    sch.run("stream")
    return toks


def _proj_phase(env, name, groups, nk, lhs_fn, rhs_fn, evac_fn, psum_n=CH):
    nc = env.nc
    toks = {}
    with nc.psum_tensor(f"{name}p0", [128, psum_n], F32) as p0, \
         nc.psum_tensor(f"{name}p1", [128, psum_n], F32) as p1:
        ps = [p0, p1]
        sch = Sched(nc)
        for gi, g in enumerate(groups):
            deps = [toks[groups[gi - 2]]] if gi >= 2 else []

            def mk_mm(g=g, gi=gi):
                def f(te):
                    for k in range(nk):
                        inst = te.matmul(ps[gi % 2][:], lhs_fn(g, k),
                                         rhs_fn(g, k),
                                         start=(k == 0), stop=(k == nk - 1))
                    return inst
                return f
            mt = sch.op("tensor", mk_mm(), deps=deps, inc=env.s_pe)

            def mk_ev(g=g, gi=gi):
                def f(ve):
                    return evac_fn(ve, g, ps[gi % 2])
                return f
            toks[g] = sch.op("vector", mk_ev(), deps=[mt], inc=env.s_dv)
        sch.run(name)
    return toks


def _p0(env, wdkv_bf, wuk_bf, wuv_bf, wuq_bf, xT_bf):
    nc = env.nc
    T = env.T
    with nc.sbuf_tensor("wstg0", [128, D], F32) as wstg0, \
         nc.sbuf_tensor("wstg1", [128, D], F32) as wstg1, \
         nc.sbuf_tensor("xstg0", [128, 1024], F32) as xstg0, \
         nc.sbuf_tensor("xstg1", [128, 1024], F32) as xstg1:
        wstg = [wstg0, wstg1]
        xstg = [xstg0, xstg1]
        sch = Sched(nc)
        ctok = None
        for dst, src in ((env.ident, env.c_ident), (env.sel16, env.c_sel16),
                         (env.sel8h, env.c_sel8h), (env.sel8e, env.c_sel8e)):
            ctok = sch.op("sync", lambda sy, dst=dst, src=src:
                          sy.dma_start(out=dst[:], in_=src),
                          inc=env.s_cst, inck=16)
        env.consts_tok = ctok
        sch.op("vector", lambda ve: ve.memset(env.ones_f1[:], 1.0))
        sch.op("vector", lambda ve: ve.memset(env.ones_fc[:], 1.0))
        sch.op("vector", lambda ve: ve.memset(env.eps_t[:], EPS_LN))
        sch.op("vector", lambda ve: ve.memset(env.qmax16[:], 0.0))
        wpieces = []
        for kd in range(8):
            wpieces.append((wdkv_bf, kd * DL, DL,
                            env.wdkv[kd * 128:(kd + 1) * 128, :]))
        for kl in range(2):
            wpieces.append((wuk_bf, kl * D, D,
                            env.wuk[kl * 128:(kl + 1) * 128, :]))
        for kl in range(2):
            wpieces.append((wuv_bf, kl * D, D,
                            env.wuv[kl * 128:(kl + 1) * 128, :]))
        for kd in range(8):
            wpieces.append((wuq_bf, kd * D, D,
                            env.wuq[kd * 128:(kd + 1) * 128, :]))
        wtoks = []
        for i, (dst, c0, ncol, src) in enumerate(wpieces):
            deps = [wtoks[i - 2]] if i >= 2 else []
            dt_ = sch.op("sync", lambda sy, i=i, ncol=ncol, src=src:
                         sy.dma_start(out=wstg[i % 2][:, 0:ncol], in_=src),
                         deps=deps, inc=env.s_stg[i % 2], inck=16)
            ct = sch.op("vector", lambda ve, i=i, dst=dst, c0=c0, ncol=ncol:
                        ve.tensor_copy(out=dst[:, c0:c0 + ncol],
                                       in_=wstg[i % 2][:, 0:ncol]),
                        deps=[dt_], inc=env.s_dv)
            wtoks.append(ct)
        xtoks = []
        xpieces = [(kd, cp) for kd in range(8) for cp in range(T // 1024)]
        for i, (kd, cp) in enumerate(xpieces):
            deps = [xtoks[i - 2]] if i >= 2 else []
            dt_ = sch.op("sync", lambda sy, i=i, kd=kd, cp=cp:
                         sy.dma_start(out=xstg[i % 2][:],
                                      in_=env.xT[kd * 128:(kd + 1) * 128,
                                                 cp * 1024:(cp + 1) * 1024]),
                         deps=deps, inc=env.s_xstg[i % 2], inck=16)
            ct = sch.op("gpsimd", lambda gp, i=i, kd=kd, cp=cp:
                        gp.tensor_copy(
                            out=xT_bf[:, kd * T + cp * 1024:
                                      kd * T + (cp + 1) * 1024],
                            in_=xstg[i % 2][:]),
                        deps=[dt_], inc=env.s_pl)
            xtoks.append(ct)
        sch.run("p0")


def _qnorms(env):
    nc = env.nc
    NCH = env.NCH
    with nc.sbuf_tensor("qsq0", [128, CH], BF16) as qsq0, \
         nc.sbuf_tensor("qsq1", [128, CH], BF16) as qsq1, \
         nc.psum_tensor("psn16", [16, CH], F32) as psn16:
        qsq = [qsq0, qsq1]
        sch = Sched(nc)
        sq_toks = []
        prev_red = None
        prev_mx = None
        i = 0
        for chq in range(NCH):
            grp_tok = None
            for kd in range(8):
                deps = [sq_toks[i - 2][1]] if i >= 2 else []
                st_ = sch.op("scalar", lambda sc, i=i, kd=kd, chq=chq:
                             sc.activation(
                                 out=qsq[i % 2][:],
                                 in_=env.qT[kd][:, chq * CH:(chq + 1) * CH],
                                 func=AF.Square),
                             deps=deps, inc=env.s_ac)
                mdeps = [st_, env.consts_tok]
                if kd == 0 and prev_red is not None:
                    mdeps.append(prev_red)
                mt = sch.op("tensor", lambda te, i=i, kd=kd:
                            te.matmul(psn16[:],
                                      env.sel16[:, kd * 16:(kd + 1) * 16],
                                      qsq[i % 2][:],
                                      start=(kd == 0), stop=(kd == 7)),
                            deps=mdeps, inc=env.s_pe)
                sq_toks.append((st_, mt))
                grp_tok = mt
                i += 1
            rd = sch.op("vector", lambda ve: ve.reduce_max(
                out=env.nrm_tmp[:], in_=psn16[:], axis=AXX),
                deps=[grp_tok] + ([prev_mx] if prev_mx else []), inc=env.s_dv)
            mx = sch.op("vector", lambda ve: ve.tensor_max(
                out=env.qmax16[:], in0=env.qmax16[:], in1=env.nrm_tmp[:]),
                deps=[rd], inc=env.s_dv)
            prev_red = rd
            prev_mx = mx
        sch.run("p2n")


def _attn_group(env, g, wuk_bf, wuv_bf):
    nc = env.nc
    T, NCH, NKT = env.T, env.NCH, env.NKT
    VW = HG * 65
    with nc.sbuf_tensor(f"kTg{g}", [128, 2 * T], BF16) as kTg, \
         nc.sbuf_tensor(f"vaug{g}", [128, NKT * VW], BF16) as vaug, \
         nc.sbuf_tensor(f"bias4_{g}", [128, 4], F32) as bias4, \
         nc.sbuf_tensor(f"kmax4_{g}", [4, 1], F32) as kmax4, \
         nc.sbuf_tensor(f"ktmp4_{g}", [4, 1], F32) as ktmp4, \
         nc.sbuf_tensor(f"brow_{g}", [1, 4], F32) as brow, \
         nc.sbuf_tensor(f"probT_{g}", [128, 6 * CH], BF16) as probT, \
         nc.sbuf_tensor(f"probM_{g}", [128, 4 * CH], BF16) as probM, \
         nc.sbuf_tensor(f"recipS_{g}", [1, 2 * CH], F32) as recipS, \
         nc.sbuf_tensor(f"rbS_{g}", [64, 2 * CH], F32) as rbS, \
         nc.sbuf_tensor(f"ksq0_{g}", [128, CH], BF16) as ksq0, \
         nc.sbuf_tensor(f"ksq1_{g}", [128, CH], BF16) as ksq1, \
         nc.sbuf_tensor(f"ctxs_{g}", [64, 2 * CH], BF16) as ctxs:

        groups_k = [(p, ch) for p in range(2) for ch in range(NCH)]
        ktoks = _proj_phase(
            env, f"kT{g}", groups_k, 2,
            lambda gk, k: wuk_bf[:, k * D + g * 256 + gk[0] * 128:
                                 k * D + g * 256 + (gk[0] + 1) * 128],
            lambda gk, k: env.c_kvT[k][:, gk[1] * CH:(gk[1] + 1) * CH],
            lambda ve, gk, ps: ve.tensor_copy(
                out=kTg[:, gk[0] * T + gk[1] * CH:
                        gk[0] * T + (gk[1] + 1) * CH],
                in_=ps[:]),
        )

        sch = Sched(nc)

        def ones_cols(gp):
            inst = None
            for kt in range(NKT):
                for hh in range(HG):
                    inst = gp.memset(vaug[:, kt * VW + hh * 65 + 64:
                                          kt * VW + hh * 65 + 65], 1.0)
            return inst
        vones_tok = sch.op("gpsimd", ones_cols, inc=env.s_pl)
        sch.run(f"vones{g}")

        def v_evac(ve, kt, ps):
            inst = None
            for hh in range(HG):
                inst = ve.tensor_copy(
                    out=vaug[:, kt * VW + hh * 65: kt * VW + hh * 65 + 64],
                    in_=ps[:, hh * 64:(hh + 1) * 64])
            return inst
        vtoks = _proj_phase(
            env, f"v{g}", list(range(NKT)), 2,
            lambda kt, k: env.c_kvT[k][:, kt * 128:(kt + 1) * 128],
            lambda kt, k: wuv_bf[:, k * D + g * 256: k * D + (g + 1) * 256],
            v_evac, psum_n=HG * 64,
        )

        bias_tok = None
        with nc.psum_tensor(f"psn4_{g}", [4, CH], F32) as psn4, \
             nc.psum_tensor(f"psb4_{g}", [128, 4], F32) as psb4:
            ksq = [ksq0, ksq1]
            sch = Sched(nc)
            km0 = sch.op("vector", lambda ve: ve.memset(kmax4[:], 0.0),
                         inc=env.s_dv)
            i = 0
            prev_red = None
            prev_mx = km0
            sq_list = []
            for chq in range(NCH):
                grp_tok = None
                for p in range(2):
                    deps = [sq_list[i - 2][1]] if i >= 2 else []
                    st_ = sch.op(
                        "scalar", lambda sc, i=i, p=p, chq=chq:
                        sc.activation(
                            out=ksq[i % 2][:],
                            in_=kTg[:, p * T + chq * CH:
                                    p * T + (chq + 1) * CH],
                            func=AF.Square),
                        deps=deps + [ktoks[(p, chq)]], inc=env.s_ac)
                    mdeps = [st_, env.consts_tok]
                    if p == 0 and prev_red is not None:
                        mdeps.append(prev_red)
                    mt = sch.op("tensor", lambda te, i=i, p=p:
                                te.matmul(psn4[:],
                                          env.sel8h[:, p * 4:(p + 1) * 4],
                                          ksq[i % 2][:],
                                          start=(p == 0), stop=(p == 1)),
                                deps=mdeps, inc=env.s_pe)
                    sq_list.append((st_, mt))
                    grp_tok = mt
                    i += 1
                rd = sch.op("vector", lambda ve: ve.reduce_max(
                    out=ktmp4[:], in_=psn4[:], axis=AXX),
                    deps=[grp_tok, prev_mx], inc=env.s_dv)
                mx = sch.op("vector", lambda ve: ve.tensor_max(
                    out=kmax4[:], in0=kmax4[:], in1=ktmp4[:]),
                    deps=[rd], inc=env.s_dv)
                prev_red = rd
                prev_mx = mx
            qg = sch.op("sync", lambda sy: sy.dma_start(
                out=env.qmaxG[:], in_=env.qmax16[g * 4:(g + 1) * 4, :],
                allow_slow_non_contiguous=True), inc=env.s_msc, inck=16)
            t1 = sch.op("vector", lambda ve: ve.tensor_mul(
                out=ktmp4[:], in0=kmax4[:], in1=env.qmaxG[:]),
                deps=[prev_mx, prev_red, qg], inc=env.s_dv)
            t2 = sch.op("scalar", lambda sc: sc.activation(
                out=ktmp4[:], in_=ktmp4[:], func=AF.Sqrt),
                deps=[t1], inc=env.s_ac)
            t3 = sch.op("vector", lambda ve: ve.tensor_scalar_mul(
                out=ktmp4[:], in0=ktmp4[:], scalar1=-0.125),
                deps=[t2], inc=env.s_dv)
            t4 = sch.op("sync", lambda sy: sy.dma_start(
                out=brow[:], in_=ktmp4[:], allow_slow_non_contiguous=True),
                deps=[t3], inc=env.s_msc, inck=16)
            t5 = sch.op("tensor", lambda te: te.matmul(
                psb4[:], env.ones_f1[0:1, :], brow[:], start=True, stop=True),
                deps=[t4], inc=env.s_pe)
            bias_tok = sch.op("vector", lambda ve: ve.tensor_copy(
                out=bias4[:], in_=psb4[:]), deps=[t5], inc=env.s_dv)
            sch.run(f"knrm{g}")

        _attn_loop(env, g, kTg, vaug, bias4, probT, probM, recipS, rbS,
                   ctxs, ktoks, vtoks, vones_tok, bias_tok)


def _attn_loop(env, g, kTg, vaug, bias4, probT, probM, recipS, rbS,
               ctxs, ktoks, vtoks, vones_tok, bias_tok):
    nc = env.nc
    T, NCH, NKT = env.T, env.NCH, env.NKT
    VW = HG * 65
    PW = 2 * CH
    jobs = [(hh, qc) for qc in range(NCH - 1, -1, -1) for hh in range(HG)]
    with nc.psum_tensor(f"ps_s0_{g}", [128, PW], F32) as ps_s0, \
         nc.psum_tensor(f"ps_s1_{g}", [128, PW], F32) as ps_s1, \
         nc.psum_tensor(f"ps_c0_{g}", [65, CH], F32) as ps_c0, \
         nc.psum_tensor(f"ps_c1_{g}", [65, CH], F32) as ps_c1, \
         nc.psum_tensor(f"ps_c2_{g}", [65, CH], F32) as ps_c2, \
         nc.psum_tensor(f"ps_r0_{g}", [64, CH], F32) as ps_r0:
        ps_s = [ps_s0, ps_s1]
        ps_c = [ps_c0, ps_c1, ps_c2]
        ps_r = [ps_r0]
        sch = Sched(nc)
        pslot_free = [[], [], []]
        mslot_free = [None, None, None, None]
        exp_toks = {}
        mask_toks = {}
        mask_slot = {}
        pv_toks = {}
        job_dve = {}
        job_bc = {}
        gp_ = 0
        gt = 0
        d = 0
        for j, (hh, qc) in enumerate(jobs):
            nkt = min(NKT, 4 * (qc + 1))
            npair = nkt // 2
            h_glob = g * HG + hh
            pt = hh // 2
            rb = 64 * (hh % 2)
            qtile = h_glob // 2
            qb = 64 * (h_glob % 2)
            base_gt = gt
            base_gp = gp_
            for i in range(npair + 1):
                if i < npair:
                    pr = base_gp + i
                    sct = None
                    for half in range(2):
                        kt = 2 * i + half
                        deps = [exp_toks.get(pr - 2), ktoks[(pt, kt // 4)]]
                        sct = sch.op(
                            "tensor",
                            lambda te, pt=pt, rb=rb, kt=kt, qc=qc, qtile=qtile,
                            qb=qb, pr=pr, half=half:
                            te.matmul(
                                ps_s[pr % 2][:, half * CH:(half + 1) * CH],
                                kTg[rb:rb + 64,
                                    pt * T + kt * 128: pt * T + (kt + 1) * 128],
                                env.qT[qtile][qb:qb + 64, qc * CH:(qc + 1) * CH],
                                start=True, stop=True),
                            deps=deps, inc=env.s_pe)
                    edeps = [sct, bias_tok] + pslot_free[pr % 3]
                    et = sch.op(
                        "scalar", lambda sc, pr=pr, hh=hh:
                        sc.activation(
                            out=probT[:, (pr % 3) * PW:(pr % 3 + 1) * PW],
                            in_=ps_s[pr % 2][:], func=AF.Exp,
                            scale=0.125, bias=bias4[:, hh:hh + 1]),
                        deps=edeps, inc=env.s_ac)
                    exp_toks[pr] = et
                    pslot_free[pr % 3] = []
                    for half in range(2):
                        kt = 2 * i + half
                        t_this = base_gt + kt
                        if kt >= nkt - 4:
                            mdeps = [et, mslot_free[d % 4]]
                            mt = sch.op(
                                "gpsimd",
                                lambda gp, pr=pr, half=half, d=d, qc=qc, kt=kt:
                                gp.affine_select(
                                    out=probM[:, (d % 4) * CH:(d % 4 + 1) * CH],
                                    in_=probT[:, (pr % 3) * PW + half * CH:
                                              (pr % 3) * PW + (half + 1) * CH],
                                    pattern=[[1, CH]], compare_op=ALU.is_ge,
                                    fill=0.0, base=qc * CH - kt * 128,
                                    channel_multiplier=-1),
                                deps=mdeps, inc=env.s_pl)
                            mask_toks[t_this] = mt
                            mask_slot[t_this] = d % 4
                            pslot_free[pr % 3].append(mt)
                            d += 1
                if i >= 1:
                    ip = i - 1
                    pr = base_gp + ip
                    for half in range(2):
                        ii = 2 * ip + half
                        t_pv = base_gt + ii
                        diag = ii >= nkt - 4
                        deps = [vtoks[ii], vones_tok]
                        if diag:
                            deps.append(mask_toks[t_pv])
                            rhs_slot = ('m', mask_slot[t_pv])
                        else:
                            deps.append(exp_toks[pr])
                            rhs_slot = ('t', (pr % 3, half))
                        if ii == 0 and j >= 3:
                            deps.append(job_dve[j - 3][2])

                        def mk_pv(ii=ii, j=j, hh=hh, nkt=nkt, rhs_slot=rhs_slot):
                            def f(te):
                                if rhs_slot[0] == 'm':
                                    rhs = probM[:, rhs_slot[1] * CH:
                                                (rhs_slot[1] + 1) * CH]
                                else:
                                    sl, hf = rhs_slot[1]
                                    rhs = probT[:, sl * PW + hf * CH:
                                                sl * PW + (hf + 1) * CH]
                                return te.matmul(
                                    ps_c[j % 3][:],
                                    vaug[:, ii * VW + hh * 65:
                                         ii * VW + hh * 65 + 65],
                                    rhs, start=(ii == 0), stop=(ii == nkt - 1))
                            return f
                        pt_ = sch.op("tensor", mk_pv(), deps=deps, inc=env.s_pe)
                        pv_toks[t_pv] = pt_
                        if diag:
                            mslot_free[mask_slot[t_pv]] = pt_
                        else:
                            pslot_free[pr % 3].append(pt_)
            gt = base_gt + nkt
            gp_ = base_gp + npair
            last_pv = pv_toks[gt - 1]
            rdeps = [last_pv]
            if j >= 2:
                rdeps.append(job_bc[j - 2])
            r1 = sch.op("vector", lambda ve, j=j:
                        ve.reciprocal(
                            out=recipS[0:1, (j % 2) * CH:(j % 2 + 1) * CH],
                            in_=ps_c[j % 3][64:65, :]),
                        deps=rdeps, inc=env.s_dv)
            bdeps = [r1]
            if j >= 1:
                bdeps.append(job_dve[j - 1][1])
            b1 = sch.op("tensor", lambda te, j=j:
                        te.matmul(ps_r[0][:], env.ones_f1[0:1, 0:64],
                                  recipS[0:1, (j % 2) * CH:(j % 2 + 1) * CH],
                                  start=True, stop=True),
                        deps=bdeps, inc=env.s_pe)
            job_bc[j] = b1
            e1 = sch.op("vector", lambda ve, j=j:
                        ve.tensor_copy(
                            out=rbS[:, (j % 2) * CH:(j % 2 + 1) * CH],
                            in_=ps_r[0][:]),
                        deps=[b1], inc=env.s_dv)
            cdeps = [e1]
            if j >= 2:
                cdeps.append(job_dve[j - 2][3])
            c1 = sch.op("vector",
                        lambda ve, j=j:
                        ve.tensor_mul(
                            out=ctxs[:, (j % 2) * CH:(j % 2 + 1) * CH],
                            in0=ps_c[j % 3][0:64, :],
                            in1=rbS[:, (j % 2) * CH:(j % 2 + 1) * CH]),
                        deps=cdeps, inc=env.s_dv)
            cd = sch.op("gpsimd",
                        lambda gp, j=j, h_glob=h_glob, qc=qc:
                        gp.dma_start(
                            out=env.ctxh[h_glob * 64:(h_glob + 1) * 64,
                                         qc * CH:(qc + 1) * CH],
                            in_=ctxs[:, (j % 2) * CH:(j % 2 + 1) * CH]),
                        deps=[c1], inc=env.s_cd[j % 2], inck=16)
            job_dve[j] = (r1, e1, c1, cd)
        sch.drain(env.s_cd[0])
        sch.drain(env.s_cd[1])
        sch.run(f"attn{g}")


def _p6(env, wo_bf, rw_bf):
    nc = env.nc
    T, NCH = env.T, env.NCH
    with nc.sbuf_tensor("ctxr", [128, 16 * CH], BF16) as ctxr, \
         nc.sbuf_tensor("xst", [128, 16 * CH], F32) as xst, \
         nc.sbuf_tensor("resid", [128, 8 * CH], F32) as resid, \
         nc.sbuf_tensor("rsq", [128, 8 * CH], F32) as rsq, \
         nc.sbuf_tensor("x1bs", [128, 8 * CH], BF16) as x1bs, \
         nc.sbuf_tensor("ab_sb", [128, 2 * CH], F32) as ab_sb, \
         nc.sbuf_tensor("rows", [1, 6 * CH], F32) as rows, \
         nc.sbuf_tensor("cwtile", [128, E], F32) as cwtile, \
         nc.sbuf_tensor("cwmask", [128, E], F32) as cwmask, \
         nc.sbuf_tensor("cwrow", [128, 5], F32) as cwrow, \
         nc.psum_tensor("ps_o0", [128, CH], F32) as ps_o0, \
         nc.psum_tensor("ps_o1", [128, CH], F32) as ps_o1, \
         nc.psum_tensor("ps_sum", [1, CH], F32) as ps_sum, \
         nc.psum_tensor("ps_ssq", [1, CH], F32) as ps_ssq, \
         nc.psum_tensor("ps_A", [128, CH], F32) as ps_A, \
         nc.psum_tensor("ps_B", [128, CH], F32) as ps_B, \
         nc.psum_tensor("ps_lg", [128, E], F32) as ps_lg, \
         nc.psum_tensor("ps_ct", [8, 128], F32) as ps_ct:
        ps_o = [ps_o0, ps_o1]
        sch = Sched(nc)
        prev = {}
        for chq in range(NCH):
            st = {}
            ctoks = []
            for kd in range(8):
                deps = []
                if chq >= 2:
                    deps.append(prev[chq - 2]['wo'][7])
                ctoks.append(sch.op(
                    "sync", lambda sy, kd=kd, chq=chq:
                    sy.dma_start(
                        out=ctxr[:, ((chq % 2) * 8 + kd) * CH:
                                 ((chq % 2) * 8 + kd + 1) * CH],
                        in_=env.ctxh[kd * 128:(kd + 1) * 128,
                                     chq * CH:(chq + 1) * CH]),
                    deps=deps, inc=env.s_cx[chq % 2], inck=16))
            xtoks = []
            for m in range(8):
                deps = []
                if chq >= 2:
                    deps.append(prev[chq - 2]['ra'][m])
                xtoks.append(sch.op(
                    "sync", lambda sy, m=m, chq=chq:
                    sy.dma_start(
                        out=xst[:, ((chq % 2) * 8 + m) * CH:
                                ((chq % 2) * 8 + m + 1) * CH],
                        in_=env.xT[m * 128:(m + 1) * 128,
                                   chq * CH:(chq + 1) * CH]),
                    deps=deps, inc=env.s_xs[chq % 2], inck=16))
            ra = [None] * 8
            sq = [None] * 8
            sums = [None] * 8
            ssqs = [None] * 8
            wo_toks = [None] * 8
            for m in range(8):
                deps = [ctoks[-1]]
                if chq * 8 + m >= 2:
                    pm = chq * 8 + m - 2
                    deps.append(ra[pm % 8] if pm // 8 == chq
                                else prev[pm // 8]['ra'][pm % 8])
                wo_toks[m] = sch.op(
                    "tensor", lambda te, m=m, chq=chq:
                    [te.matmul(ps_o[m % 2][:],
                               wo_bf[:, kd * D + m * 128:
                                     kd * D + (m + 1) * 128],
                               ctxr[:, ((chq % 2) * 8 + kd) * CH:
                                    ((chq % 2) * 8 + kd + 1) * CH],
                               start=(kd == 0), stop=(kd == 7))
                     for kd in range(8)][-1],
                    deps=deps, inc=env.s_pe)
                rdeps = [wo_toks[m], xtoks[-1]]
                if chq >= 1:
                    rdeps.append(prev[chq - 1]['x1bdma'][7])
                    rdeps.append(prev[chq - 1]['sum'][m])
                    rdeps.append(prev[chq - 1]['lg_last'])
                ra[m] = sch.op(
                    "vector", lambda ve, m=m, chq=chq:
                    ve.tensor_add(
                        out=resid[:, m * CH:(m + 1) * CH],
                        in0=ps_o[m % 2][:],
                        in1=xst[:, ((chq % 2) * 8 + m) * CH:
                                ((chq % 2) * 8 + m + 1) * CH]),
                    deps=rdeps, inc=env.s_dv)
                sdeps = [ra[m]]
                if chq >= 1:
                    sdeps.append(prev[chq - 1]['ssq'][m])
                    sdeps.append(prev[chq - 1]['n1'][m])
                sq[m] = sch.op(
                    "scalar", lambda sc, m=m:
                    sc.activation(out=rsq[:, m * CH:(m + 1) * CH],
                                  in_=resid[:, m * CH:(m + 1) * CH],
                                  func=AF.Square),
                    deps=sdeps, inc=env.s_ac)
            for m in range(8):
                sums[m] = sch.op(
                    "tensor", lambda te, m=m:
                    te.matmul(ps_sum[:], env.ones_fc[:],
                              resid[:, m * CH:(m + 1) * CH],
                              start=(m == 0), stop=(m == 7)),
                    deps=[ra[m]] + ([prev[chq - 1]['rows0']]
                                    if (m == 0 and chq >= 1) else []),
                    inc=env.s_pe)
                ssqs[m] = sch.op(
                    "tensor", lambda te, m=m:
                    te.matmul(ps_ssq[:], env.ones_fc[:],
                              rsq[:, m * CH:(m + 1) * CH],
                              start=(m == 0), stop=(m == 7)),
                    deps=[sq[m]] + ([prev[chq - 1]['rows1']]
                                    if (m == 0 and chq >= 1) else []),
                    inc=env.s_pe)
            c1 = sch.op("vector", lambda ve: ve.tensor_scalar_mul(
                out=rows[0:1, 0:CH], in0=ps_sum[:], scalar1=1.0 / D),
                deps=[sums[7]], inc=env.s_dv)
            c2 = sch.op("vector", lambda ve: ve.tensor_mul(
                out=rows[0:1, CH:2 * CH], in0=rows[0:1, 0:CH],
                in1=rows[0:1, 0:CH]), deps=[c1], inc=env.s_dv)
            c3 = sch.op("vector", lambda ve: ve.tensor_scalar_mul(
                out=rows[0:1, 4 * CH:5 * CH], in0=ps_ssq[:],
                scalar1=1.0 / D), deps=[ssqs[7]], inc=env.s_dv)
            c4 = sch.op("vector", lambda ve: ve.tensor_sub(
                out=rows[0:1, CH:2 * CH], in0=rows[0:1, 4 * CH:5 * CH],
                in1=rows[0:1, CH:2 * CH]), deps=[c2, c3], inc=env.s_dv)
            c5 = sch.op("scalar", lambda sc: sc.activation(
                out=rows[0:1, 5 * CH:6 * CH], in_=rows[0:1, CH:2 * CH],
                func=AF.Sqrt, bias=env.eps_t[:]), deps=[c4], inc=env.s_ac)
            c6 = sch.op("vector", lambda ve: ve.reciprocal(
                out=rows[0:1, 2 * CH:3 * CH], in_=rows[0:1, 5 * CH:6 * CH]),
                deps=[c5], inc=env.s_dv)
            c7 = sch.op("vector", lambda ve: ve.tensor_mul(
                out=rows[0:1, 3 * CH:4 * CH], in0=rows[0:1, 0:CH],
                in1=rows[0:1, 2 * CH:3 * CH]), deps=[c6], inc=env.s_dv)
            c8 = sch.op("vector", lambda ve: ve.tensor_scalar_mul(
                out=rows[0:1, 3 * CH:4 * CH], in0=rows[0:1, 3 * CH:4 * CH],
                scalar1=-1.0), deps=[c7], inc=env.s_dv)
            st['rows0'] = c1
            st['rows1'] = c3
            bdeps_extra = [prev[chq - 1]['abev']] if chq >= 1 else []
            bA = sch.op("tensor", lambda te: te.matmul(
                ps_A[:], env.ones_f1[0:1, :], rows[0:1, 2 * CH:3 * CH],
                start=True, stop=True), deps=[c6] + bdeps_extra, inc=env.s_pe)
            bB = sch.op("tensor", lambda te: te.matmul(
                ps_B[:], env.ones_f1[0:1, :], rows[0:1, 3 * CH:4 * CH],
                start=True, stop=True), deps=[c8] + bdeps_extra, inc=env.s_pe)
            evdeps = [bA, bB]
            if chq >= 1:
                evdeps.append(prev[chq - 1]['norm_last'])
            abev = sch.op("vector", lambda ve: [
                ve.tensor_copy(out=ab_sb[:, 0:CH], in_=ps_A[:]),
                ve.tensor_copy(out=ab_sb[:, CH:2 * CH], in_=ps_B[:])][-1],
                deps=evdeps, inc=env.s_dv)
            st['abev'] = abev
            norm_toks = [None] * 8
            cast2 = [None] * 8
            n1s = [None] * 8
            for m in range(8):
                n1 = sch.op("vector", lambda ve, m=m: ve.tensor_mul(
                    out=rsq[:, m * CH:(m + 1) * CH],
                    in0=resid[:, m * CH:(m + 1) * CH], in1=ab_sb[:, 0:CH]),
                    deps=[abev, sq[m], sums[m], ssqs[m]], inc=env.s_dv)
                n1s[m] = n1
                n2 = sch.op("vector", lambda ve, m=m: ve.tensor_add(
                    out=resid[:, m * CH:(m + 1) * CH],
                    in0=rsq[:, m * CH:(m + 1) * CH],
                    in1=ab_sb[:, CH:2 * CH]), deps=[n1], inc=env.s_dv)
                n3 = sch.op("vector", lambda ve, m=m: ve.tensor_copy(
                    out=x1bs[:, m * CH:(m + 1) * CH],
                    in_=resid[:, m * CH:(m + 1) * CH]), deps=[n2],
                    inc=env.s_dv)
                norm_toks[m] = n2
                cast2[m] = n3
            st['norm_last'] = cast2[7]
            st['ra'] = ra
            st['sum'] = sums
            st['ssq'] = ssqs
            st['n1'] = n1s
            x1fdma = [None] * 8
            x1bdma = [None] * 8
            for m in range(8):
                x1fdma[m] = sch.op(
                    "gpsimd", lambda gp, m=m, chq=chq:
                    gp.dma_start(out=env.x1f[m * 128:(m + 1) * 128,
                                             chq * CH:(chq + 1) * CH],
                                 in_=resid[:, m * CH:(m + 1) * CH]),
                    deps=[norm_toks[m]], inc=env.s_xd[chq % 2], inck=16)
                x1bdma[m] = sch.op(
                    "gpsimd", lambda gp, m=m, chq=chq:
                    gp.dma_start(out=env.x1b[m * 128:(m + 1) * 128,
                                             chq * CH:(chq + 1) * CH],
                                 in_=x1bs[:, m * CH:(m + 1) * CH]),
                    deps=[cast2[m]], inc=env.s_xd[chq % 2], inck=16)
            st['x1fdma'] = x1fdma
            st['x1bdma'] = x1bdma
            st['wo'] = wo_toks
            lg_tok = None
            for tt in range(4):
                ldeps = [cast2[kd] for kd in range(8)]
                if tt >= 1:
                    ldeps += [st[f'exp{tt-1}'], st[f'm1red{tt-1}']]
                elif chq >= 1:
                    ldeps += [prev[chq - 1]['exp3'], prev[chq - 1]['m1red3']]
                lg_tok = sch.op(
                    "tensor", lambda te, tt=tt:
                    [te.matmul(ps_lg[:],
                               x1bs[:, kd * CH + tt * 128:
                                    kd * CH + (tt + 1) * 128],
                               rw_bf[:, kd * E:(kd + 1) * E],
                               start=(kd == 0), stop=(kd == 7))
                     for kd in range(8)][-1],
                    deps=ldeps, inc=env.s_pe)
                m1 = sch.op("vector", lambda ve: ve.reduce_max(
                    out=cwrow[:, 0:1], in_=ps_lg[:], axis=AXX),
                    deps=[lg_tok], inc=env.s_dv)
                m1n = sch.op("vector", lambda ve: ve.tensor_scalar_mul(
                    out=cwrow[:, 0:1], in0=cwrow[:, 0:1], scalar1=-1.0),
                    deps=[m1], inc=env.s_dv)
                st[f'm1red{tt}'] = m1n
                exdeps = [m1n]
                if tt >= 1:
                    exdeps.append(st[f'cwev{tt-1}'])
                elif chq >= 1:
                    exdeps.append(prev[chq - 1]['cwev3'])
                ex = sch.op("scalar", lambda sc: sc.activation(
                    out=cwtile[:], in_=ps_lg[:], func=AF.Exp,
                    bias=cwrow[:, 0:1]), deps=exdeps, inc=env.s_ac)
                st[f'exp{tt}'] = ex
                k1 = sch.op("vector", lambda ve: ve.reduce_max(
                    out=cwrow[:, 1:2], in_=cwtile[:], axis=AXX),
                    deps=[ex], inc=env.s_dv)
                k2 = sch.op("vector", lambda ve: ve.tensor_scalar(
                    out=cwmask[:], in0=cwtile[:], scalar1=cwrow[:, 1:2],
                    scalar2=-4.0, op0=ALU.is_ge, op1=ALU.mult),
                    deps=[k1], inc=env.s_dv)
                k3 = sch.op("vector", lambda ve: ve.tensor_add(
                    out=cwmask[:], in0=cwmask[:], in1=cwtile[:]),
                    deps=[k2], inc=env.s_dv)
                k4 = sch.op("vector", lambda ve: ve.reduce_max(
                    out=cwrow[:, 2:3], in_=cwmask[:], axis=AXX),
                    deps=[k3], inc=env.s_dv)
                k5 = sch.op("vector", lambda ve: ve.tensor_tensor(
                    out=cwrow[:, 3:4], in0=cwrow[:, 1:2], in1=cwrow[:, 2:3],
                    op=ALU.add), deps=[k4], inc=env.s_dv)
                k6 = sch.op("vector", lambda ve: ve.reciprocal(
                    out=cwrow[:, 4:5], in_=cwrow[:, 3:4]),
                    deps=[k5], inc=env.s_dv)
                k7 = sch.op("vector", lambda ve: ve.tensor_scalar(
                    out=cwmask[:], in0=cwtile[:], scalar1=cwrow[:, 2:3],
                    scalar2=None, op0=ALU.is_ge), deps=[k6, k4], inc=env.s_dv)
                k8 = sch.op("vector", lambda ve: ve.tensor_mul(
                    out=cwtile[:], in0=cwtile[:], in1=cwmask[:]),
                    deps=[k7], inc=env.s_dv)
                k9 = sch.op("vector", lambda ve: ve.tensor_scalar_mul(
                    out=cwtile[:], in0=cwtile[:], scalar1=cwrow[:, 4:5]),
                    deps=[k8], inc=env.s_dv)
                tdeps = [k9, env.consts_tok]
                if tt >= 1:
                    tdeps.append(st[f'cwev{tt-1}'])
                elif chq >= 1:
                    tdeps.append(prev[chq - 1]['cwev3'])
                tr = sch.op("tensor", lambda te: te.matmul(
                    ps_ct[:], cwtile[:], env.ident[:],
                    start=True, stop=True), deps=tdeps, inc=env.s_pe)
                cwev = sch.op(
                    "vector", lambda ve, chq=chq, tt=tt:
                    ve.tensor_copy(out=env.cwT[:, chq * CH + tt * 128:
                                               chq * CH + (tt + 1) * 128],
                                   in_=ps_ct[:]), deps=[tr], inc=env.s_dv)
                st[f'cwev{tt}'] = cwev
            st['lg_last'] = lg_tok
            prev[chq] = st
        sch.drain(env.s_xd[0])
        sch.drain(env.s_xd[1])
        sch.run("p6")


def _moe_pass(env, th):
    nc = env.nc
    T, THL, NTCH = env.T, env.THL, env.NTCH
    WB = env.wsets
    final_acc = {}
    with nc.sbuf_tensor(f"acc_{th}", [128, 8 * THL], F32) as accT, \
         nc.sbuf_tensor(f"x1h_{th}", [128, 8 * THL], BF16) as x1h:
        with nc.sbuf_tensor(f"w1b_{th}", [128, 2 * 8 * 1024], BF16) as w1b, \
             nc.sbuf_tensor(f"w2b_{th}", [128, 8 * 1024], BF16) as w2b, \
             nc.sbuf_tensor(f"hsb_{th}", [128, 8 * THL], BF16) as hsb, \
             nc.sbuf_tensor(f"stgw0_{th}", [128, 1024], F32) as stgw0, \
             nc.sbuf_tensor(f"stgw1_{th}", [128, 1024], F32) as stgw1, \
             nc.sbuf_tensor(f"cwb_{th}", [128, 2 * NTCH * CH], BF16) as cwb, \
             nc.psum_tensor(f"ph0_{th}", [128, CH], F32) as ph0, \
             nc.psum_tensor(f"ph1_{th}", [128, CH], F32) as ph1, \
             nc.psum_tensor(f"py0_{th}", [128, CH], F32) as py0, \
             nc.psum_tensor(f"py1_{th}", [128, CH], F32) as py1, \
             nc.psum_tensor(f"pcw0_{th}", [128, CH], F32) as pcw0, \
             nc.psum_tensor(f"pcw1_{th}", [128, CH], F32) as pcw1:
            stgw = [stgw0, stgw1]
            ps_h = [ph0, ph1]
            ps_y = [py0, py1]
            pcw = [pcw0, pcw1]
            sch = Sched(nc)
            x1toks = []
            for m in range(8):
                x1toks.append(sch.op(
                    "sync", lambda sy, m=m, th=th:
                    sy.dma_start(out=x1h[:, m * THL:(m + 1) * THL],
                                 in_=env.x1b[m * 128:(m + 1) * 128,
                                             th * THL:(th + 1) * THL]),
                    inc=env.s_h1, inck=16))
            x1last = x1toks[-1]
            acc_clear = []
            for m in range(8):
                acc_clear.append(sch.op(
                    "gpsimd", lambda gp, m=m:
                    gp.memset(accT[:, m * THL:(m + 1) * THL], 0.0),
                    inc=env.s_pl))

            cast_tok = {}
            step_hgrp = {}
            step_silu = {}
            step_scale = {}
            step_ygrp = {}
            step_acc = {}
            cwb_ev = {}
            dma_i = 0
            cast_list = []
            for s in range(16):
                e, fh = s // 2, s % 2
                for widx in range(16):
                    isw1 = widx < 8
                    kk = widx % 8
                    if isw1:
                        src = env.w1[e, kk * 128:(kk + 1) * 128,
                                     fh * 1024:(fh + 1) * 1024]
                    else:
                        src = env.w2[e, fh * 1024 + kk * 128:
                                     fh * 1024 + (kk + 1) * 128, :]
                    deps = [cast_list[dma_i - 2]] if dma_i >= 2 else []
                    dt_ = sch.op("sync", lambda sy, src=src, dma_i=dma_i:
                                 sy.dma_start(out=stgw[dma_i % 2][:], in_=src),
                                 deps=deps, inc=env.s_stg[dma_i % 2], inck=16)
                    cdeps = [dt_]
                    if isw1:
                        if s >= 2:
                            cdeps.append(step_hgrp[(s - 2, 7, NTCH - 1)])
                        ws = (s % 2) * 8
                    else:
                        if s >= 1:
                            cdeps.append(step_ygrp[(s - 1, 7, NTCH - 1)])
                        ws = 0
                    dst = w1b if isw1 else w2b
                    ct = sch.op(
                        "vector", lambda ve, dst=dst, kk=kk, dma_i=dma_i, ws=ws:
                        ve.tensor_copy(
                            out=dst[:, (ws + kk) * 1024:(ws + kk + 1) * 1024],
                            in_=stgw[dma_i % 2][:]),
                        deps=cdeps, inc=env.s_dv)
                    cast_tok[(s, 'w1' if isw1 else 'w2', kk)] = ct
                    cast_list.append(ct)
                    dma_i += 1
                if fh == 0:
                    for tch in range(NTCH):
                        deps = [env.consts_tok]
                        if tch >= 2:
                            deps.append(cwb_ev[(e, tch - 2)])
                        if e >= 2:
                            deps += [step_scale[(2 * e - 3, fm, tch)]
                                     for fm in range(8)]
                            deps.append(cwb_ev[(e - 2, NTCH - 1)])
                        bc = sch.op(
                            "tensor", lambda te, e=e, tch=tch, th=th:
                            te.matmul(pcw[tch % 2][:],
                                      env.sel8e[:, e * 128:(e + 1) * 128],
                                      env.cwT[0:8, th * THL + tch * CH:
                                              th * THL + (tch + 1) * CH],
                                      start=True, stop=True),
                            deps=deps, inc=env.s_pe)
                        cwb_ev[(e, tch)] = sch.op(
                            "scalar", lambda sc, e=e, tch=tch:
                            sc.copy(out=cwb[:, ((e % 2) * NTCH + tch) * CH:
                                            ((e % 2) * NTCH + tch + 1) * CH],
                                    in_=pcw[tch % 2][:]),
                            deps=[bc], inc=env.s_ac)
                w1last = cast_tok[(s, 'w1', 7)]
                hidx = 0
                for fm in range(8):
                    for tch in range(NTCH):
                        deps = [w1last, x1last]
                        if hidx >= 2:
                            deps.append(step_silu[(s, (hidx - 2) // NTCH,
                                                  (hidx - 2) % NTCH)])
                        elif s >= 1:
                            php = 8 * NTCH - 2 + hidx
                            deps.append(step_silu[(s - 1, php // NTCH,
                                                  php % NTCH)])
                        ht = sch.op(
                            "tensor", lambda te, s=s, fm=fm, tch=tch,
                            hidx=hidx, WB=WB:
                            [te.matmul(
                                ps_h[hidx % 2][:],
                                env_w1b_slice(w1b, s, kd, fm, 2),
                                x1h[:, kd * THL + tch * CH:
                                    kd * THL + (tch + 1) * CH],
                                start=(kd == 0), stop=(kd == 7))
                             for kd in range(8)][-1],
                            deps=deps, inc=env.s_pe)
                        step_hgrp[(s, fm, tch)] = ht
                        sdeps = [ht]
                        if s >= 1:
                            sdeps.append(step_scale[(s - 1, fm, tch)])
                            sdeps.append(step_ygrp[(s - 1, 7, tch)])
                        stk = sch.op(
                            "scalar", lambda sc, fm=fm, tch=tch, hidx=hidx:
                            sc.activation(
                                out=hsb[:, fm * THL + tch * CH:
                                        fm * THL + (tch + 1) * CH],
                                in_=ps_h[hidx % 2][:], func=AF.Silu),
                            deps=sdeps, inc=env.s_ac)
                        step_silu[(s, fm, tch)] = stk
                        ptk = sch.op(
                            "gpsimd", lambda gp, fm=fm, tch=tch, e=e:
                            gp.tensor_mul(
                                out=hsb[:, fm * THL + tch * CH:
                                        fm * THL + (tch + 1) * CH],
                                in0=hsb[:, fm * THL + tch * CH:
                                        fm * THL + (tch + 1) * CH],
                                in1=cwb[:, ((e % 2) * NTCH + tch) * CH:
                                        ((e % 2) * NTCH + tch + 1) * CH]),
                            deps=[stk, cwb_ev[(e, tch)]], inc=env.s_pl)
                        step_scale[(s, fm, tch)] = ptk
                        hidx += 1
                w2last = cast_tok[(s, 'w2', 7)]
                yidx = 0
                for tch in range(NTCH):
                    scale_deps = [step_scale[(s, fm, tch)] for fm in range(8)]
                    for ym in range(8):
                        deps = [w2last] + scale_deps
                        if yidx >= 2:
                            deps.append(step_acc[(s, (yidx - 2) % 8,
                                                 (yidx - 2) // 8)])
                        elif s >= 1:
                            pyp = 8 * NTCH - 2 + yidx
                            deps.append(step_acc[(s - 1, pyp % 8, pyp // 8)])
                        yt = sch.op(
                            "tensor", lambda te, s=s, ym=ym, tch=tch,
                            yidx=yidx, WB=WB:
                            [te.matmul(
                                ps_y[yidx % 2][:],
                                env_w1b_slice(w2b, s, kf, ym, 1),
                                hsb[:, kf * THL + tch * CH:
                                    kf * THL + (tch + 1) * CH],
                                start=(kf == 0), stop=(kf == 7))
                             for kf in range(8)][-1],
                            deps=deps, inc=env.s_pe)
                        step_ygrp[(s, ym, tch)] = yt
                        adeps = [yt]
                        if s >= 1:
                            adeps.append(step_acc[(s - 1, ym, tch)])
                        else:
                            adeps.append(acc_clear[ym])
                        at = sch.op(
                            "vector", lambda ve, ym=ym, tch=tch, yidx=yidx:
                            ve.tensor_add(
                                out=accT[:, ym * THL + tch * CH:
                                         ym * THL + (tch + 1) * CH],
                                in0=accT[:, ym * THL + tch * CH:
                                         ym * THL + (tch + 1) * CH],
                                in1=ps_y[yidx % 2][:]),
                            deps=adeps, inc=env.s_dv)
                        step_acc[(s, ym, tch)] = at
                        yidx += 1
            for ym in range(8):
                for tch in range(NTCH):
                    final_acc[(ym, tch)] = step_acc[(15, ym, tch)]
            if getattr(env, 'dbg_moe', None) is not None:
                for m in range(8):
                    sch.op("gpsimd", lambda gp, m=m, th=th: gp.dma_start(
                        out=env.dbg_moe[m * 128:(m + 1) * 128,
                                        th * THL:(th + 1) * THL],
                        in_=accT[:, m * THL:(m + 1) * THL]),
                        deps=[step_acc[(15, m, tch)] for tch in range(NTCH)],
                        inc=env.s_dbg, inck=16)
            sch.drain(env.s_dbg)
            sch.run(f"moe{th}")

        _ln2(env, th, accT, final_acc)


def env_w1b_slice(wb, s, kd, fm, WB=1):
    base = ((s % WB) * 8 + kd) * 1024
    return wb[:, base + fm * 128: base + (fm + 1) * 128]


def _ln2(env, th, accT, final_acc):
    nc = env.nc
    THL, NTCH = env.THL, env.NTCH
    with nc.sbuf_tensor(f"xst2_{th}", [128, 16 * CH], F32) as xst2, \
         nc.sbuf_tensor(f"res2_{th}", [128, 8 * CH], F32) as res2, \
         nc.sbuf_tensor(f"rsq2_{th}", [128, 8 * CH], F32) as rsq2, \
         nc.sbuf_tensor(f"outf_{th}", [128, 8 * CH], F32) as outf, \
         nc.sbuf_tensor(f"ab2_{th}", [128, 2 * CH], F32) as ab2, \
         nc.sbuf_tensor(f"rows2_{th}", [1, 6 * CH], F32) as rows2, \
         nc.psum_tensor(f"ps2s_{th}", [1, CH], F32) as ps2s, \
         nc.psum_tensor(f"ps2q_{th}", [1, CH], F32) as ps2q, \
         nc.psum_tensor(f"ps2A_{th}", [128, CH], F32) as ps2A, \
         nc.psum_tensor(f"ps2B_{th}", [128, CH], F32) as ps2B:
        sch = Sched(nc)
        prev2 = {}
        for c in range(NTCH):
            st = {}
            chg = th * NTCH + c
            xtoks = []
            for m in range(8):
                deps = [final_acc[(m, c)]]
                if c >= 2:
                    deps.append(prev2[c - 2]['ra'][m])
                xtoks.append(sch.op(
                    "sync", lambda sy, m=m, c=c, chg=chg:
                    sy.dma_start(
                        out=xst2[:, ((c % 2) * 8 + m) * CH:
                                 ((c % 2) * 8 + m + 1) * CH],
                        in_=env.x1f[m * 128:(m + 1) * 128,
                                    chg * CH:(chg + 1) * CH]),
                    deps=deps, inc=env.s_xs[c % 2], inck=16))
            ra = [None] * 8
            sq = [None] * 8
            sums = [None] * 8
            ssqs = [None] * 8
            for m in range(8):
                rdeps = [xtoks[-1], final_acc[(m, c)]]
                if c >= 1:
                    rdeps.append(prev2[c - 1]['sum'][m])
                    rdeps.append(prev2[c - 1]['norm'][m])
                    if 'dbgres' in prev2[c - 1]:
                        rdeps.append(prev2[c - 1]['dbgres'][m])
                ra[m] = sch.op(
                    "vector", lambda ve, m=m, c=c:
                    ve.tensor_add(
                        out=res2[:, m * CH:(m + 1) * CH],
                        in0=accT[:, m * THL + c * CH: m * THL + (c + 1) * CH],
                        in1=xst2[:, ((c % 2) * 8 + m) * CH:
                                 ((c % 2) * 8 + m + 1) * CH]),
                    deps=rdeps, inc=env.s_dv)
                sdeps = [ra[m]]
                if c >= 1:
                    sdeps.append(prev2[c - 1]['ssq'][m])
                    sdeps.append(prev2[c - 1]['n1'][m])
                sq[m] = sch.op(
                    "scalar", lambda sc, m=m:
                    sc.activation(out=rsq2[:, m * CH:(m + 1) * CH],
                                  in_=res2[:, m * CH:(m + 1) * CH],
                                  func=AF.Square),
                    deps=sdeps, inc=env.s_ac)
            for m in range(8):
                sums[m] = sch.op(
                    "tensor", lambda te, m=m:
                    te.matmul(ps2s[:], env.ones_fc[:],
                              res2[:, m * CH:(m + 1) * CH],
                              start=(m == 0), stop=(m == 7)),
                    deps=[ra[m]] + ([prev2[c - 1]['rows0']]
                                    if (m == 0 and c >= 1) else []),
                    inc=env.s_pe)
                ssqs[m] = sch.op(
                    "tensor", lambda te, m=m:
                    te.matmul(ps2q[:], env.ones_fc[:],
                              rsq2[:, m * CH:(m + 1) * CH],
                              start=(m == 0), stop=(m == 7)),
                    deps=[sq[m]] + ([prev2[c - 1]['rows1']]
                                    if (m == 0 and c >= 1) else []),
                    inc=env.s_pe)
            c1 = sch.op("vector", lambda ve: ve.tensor_scalar_mul(
                out=rows2[0:1, 0:CH], in0=ps2s[:], scalar1=1.0 / D),
                deps=[sums[7]], inc=env.s_dv)
            c2 = sch.op("vector", lambda ve: ve.tensor_mul(
                out=rows2[0:1, CH:2 * CH], in0=rows2[0:1, 0:CH],
                in1=rows2[0:1, 0:CH]), deps=[c1], inc=env.s_dv)
            c3 = sch.op("vector", lambda ve: ve.tensor_scalar_mul(
                out=rows2[0:1, 4 * CH:5 * CH], in0=ps2q[:], scalar1=1.0 / D),
                deps=[ssqs[7]], inc=env.s_dv)
            c4 = sch.op("vector", lambda ve: ve.tensor_sub(
                out=rows2[0:1, CH:2 * CH], in0=rows2[0:1, 4 * CH:5 * CH],
                in1=rows2[0:1, CH:2 * CH]), deps=[c2, c3], inc=env.s_dv)
            c5 = sch.op("scalar", lambda sc: sc.activation(
                out=rows2[0:1, 5 * CH:6 * CH], in_=rows2[0:1, CH:2 * CH],
                func=AF.Sqrt, bias=env.eps_t[:]), deps=[c4], inc=env.s_ac)
            c6 = sch.op("vector", lambda ve: ve.reciprocal(
                out=rows2[0:1, 2 * CH:3 * CH], in_=rows2[0:1, 5 * CH:6 * CH]),
                deps=[c5], inc=env.s_dv)
            c7 = sch.op("vector", lambda ve: ve.tensor_mul(
                out=rows2[0:1, 3 * CH:4 * CH], in0=rows2[0:1, 0:CH],
                in1=rows2[0:1, 2 * CH:3 * CH]), deps=[c6], inc=env.s_dv)
            c8 = sch.op("vector", lambda ve: ve.tensor_scalar_mul(
                out=rows2[0:1, 3 * CH:4 * CH], in0=rows2[0:1, 3 * CH:4 * CH],
                scalar1=-1.0), deps=[c7], inc=env.s_dv)
            st['rows0'] = c1
            st['rows1'] = c3
            bdeps_extra = [prev2[c - 1]['abev']] if c >= 1 else []
            bA = sch.op("tensor", lambda te: te.matmul(
                ps2A[:], env.ones_f1[0:1, :], rows2[0:1, 2 * CH:3 * CH],
                start=True, stop=True), deps=[c6] + bdeps_extra, inc=env.s_pe)
            bB = sch.op("tensor", lambda te: te.matmul(
                ps2B[:], env.ones_f1[0:1, :], rows2[0:1, 3 * CH:4 * CH],
                start=True, stop=True), deps=[c8] + bdeps_extra, inc=env.s_pe)
            evdeps = [bA, bB]
            if c >= 1:
                evdeps.append(prev2[c - 1]['norm'][7])
                if 'dbgab' in prev2[c - 1]:
                    evdeps.append(prev2[c - 1]['dbgab'])
            abev = sch.op("vector", lambda ve: [
                ve.tensor_copy(out=ab2[:, 0:CH], in_=ps2A[:]),
                ve.tensor_copy(out=ab2[:, CH:2 * CH], in_=ps2B[:])][-1],
                deps=evdeps, inc=env.s_dv)
            st['abev'] = abev
            if getattr(env, 'dbg_ab', None) is not None:
                st['dbgab'] = sch.op(
                    "gpsimd", lambda gp, th=th, c=c: gp.dma_start(
                        out=env.dbg_ab[:, (th * env.NTCH + c) * 2 * CH:
                                       (th * env.NTCH + c + 1) * 2 * CH],
                        in_=ab2[:]), deps=[abev], inc=env.s_dbg, inck=16)
                st['dbgres'] = [None] * 8
                for m in range(8):
                    st['dbgres'][m] = sch.op(
                        "gpsimd", lambda gp, m=m, chg=chg: gp.dma_start(
                            out=env.dbg_res2[m * 128:(m + 1) * 128,
                                             chg * CH:(chg + 1) * CH],
                            in_=res2[:, m * CH:(m + 1) * CH]),
                        deps=[ra[m], abev], inc=env.s_dbg, inck=16)
            norm = [None] * 8
            odma = [None] * 8
            n1s = [None] * 8
            for m in range(8):
                n1 = sch.op("vector", lambda ve, m=m: ve.tensor_mul(
                    out=rsq2[:, m * CH:(m + 1) * CH],
                    in0=res2[:, m * CH:(m + 1) * CH], in1=ab2[:, 0:CH]),
                    deps=[abev, sq[m], sums[m], ssqs[m]], inc=env.s_dv)
                n1s[m] = n1
                ndeps = [n1]
                if c >= 1:
                    ndeps.append(prev2[c - 1]['odma'][7])
                n2 = sch.op("vector", lambda ve, m=m: ve.tensor_add(
                    out=outf[:, m * CH:(m + 1) * CH],
                    in0=rsq2[:, m * CH:(m + 1) * CH],
                    in1=ab2[:, CH:2 * CH]), deps=ndeps, inc=env.s_dv)
                norm[m] = n2
                odma[m] = sch.op(
                    "gpsimd", lambda gp, m=m, chg=chg:
                    gp.dma_start(out=env.outT[m * 128:(m + 1) * 128,
                                              chg * CH:(chg + 1) * CH],
                                 in_=outf[:, m * CH:(m + 1) * CH]),
                    deps=[n2], inc=env.s_od[c % 2], inck=16)
            st['norm'] = norm
            st['odma'] = odma
            st['ra'] = ra
            st['sum'] = sums
            st['ssq'] = ssqs
            st['n1'] = n1s
            prev2[c] = st
        sch.drain(env.s_od[0])
        sch.drain(env.s_od[1])
        sch.drain(env.s_xs[0])
        sch.drain(env.s_xs[1])
        sch.run(f"ln2_{th}")


def build(T=4096, debug=False, wsets=1):
    assert T % 1024 == 0
    env = Env()
    env.T = T
    env.wsets = wsets
    env.NCH = T // CH
    env.NKT = T // 128
    env.THL = T // 2
    env.NTCH = env.THL // CH

    nc = bass.Bass("TRN2", target_bir_lowering=False)
    env.nc = nc

    env.xT = nc.dram_tensor("xT", [D, T], F32, kind="ExternalInput").ap()
    env.wdkv = nc.dram_tensor("wdkv", [D, DL], F32, kind="ExternalInput").ap()
    env.wuk = nc.dram_tensor("wuk", [DL, D], F32, kind="ExternalInput").ap()
    env.wuv = nc.dram_tensor("wuv", [DL, D], F32, kind="ExternalInput").ap()
    env.wuq = nc.dram_tensor("wuq", [D, D], F32, kind="ExternalInput").ap()
    env.wo = nc.dram_tensor("wo", [D, D], F32, kind="ExternalInput").ap()
    env.rw = nc.dram_tensor("rw", [D, E], F32, kind="ExternalInput").ap()
    env.w1 = nc.dram_tensor("w1", [E, D, FF], F32, kind="ExternalInput").ap()
    env.w2 = nc.dram_tensor("w2", [E, FF, D], F32, kind="ExternalInput").ap()
    env.c_ident = nc.dram_tensor("c_ident", [128, 128], F32,
                                 kind="ExternalInput").ap()
    env.c_sel16 = nc.dram_tensor("c_sel16", [128, 128], BF16,
                                 kind="ExternalInput").ap()
    env.c_sel8h = nc.dram_tensor("c_sel8h", [128, 8], BF16,
                                 kind="ExternalInput").ap()
    env.c_sel8e = nc.dram_tensor("c_sel8e", [8, 1024], BF16,
                                 kind="ExternalInput").ap()

    env.outT = nc.dram_tensor("outT", [D, T], F32, kind="ExternalOutput").ap()
    if debug:
        env.x1f = nc.dram_tensor("x1f", [D, T], F32, kind="ExternalOutput").ap()
        env.x1b = nc.dram_tensor("x1b", [D, T], BF16, kind="ExternalOutput").ap()
    else:
        env.x1f = nc.dram_tensor("x1f", [D, T], F32).ap()
        env.x1b = nc.dram_tensor("x1b", [D, T], BF16).ap()
    if debug:
        env.dbg_ckv = nc.dram_tensor("dbg_ckv", [DL, T], BF16,
                                     kind="ExternalOutput").ap()
        env.dbg_q = nc.dram_tensor("dbg_q", [D, T], BF16,
                                   kind="ExternalOutput").ap()
        env.dbg_cw = nc.dram_tensor("dbg_cw", [8, T], BF16,
                                    kind="ExternalOutput").ap()
        env.dbg_moe = nc.dram_tensor("dbg_moe", [D, T], F32,
                                     kind="ExternalOutput").ap()
        env.dbg_res2 = nc.dram_tensor("dbg_res2", [D, T], F32,
                                      kind="ExternalOutput").ap()
        env.dbg_ab = nc.dram_tensor("dbg_ab", [128, 16 * CH], F32,
                                    kind="ExternalOutput").ap()

    ctx = ExitStack()
    EC = ctx.enter_context

    env.s_pe = Ctr(EC(nc.semaphore("s_pe")))
    env.s_ac = Ctr(EC(nc.semaphore("s_ac")))
    env.s_dv = Ctr(EC(nc.semaphore("s_dv")))
    env.s_pl = Ctr(EC(nc.semaphore("s_pl")))
    env.s_cst = Ctr(EC(nc.semaphore("s_cst")))
    env.s_stg = [Ctr(EC(nc.semaphore(f"s_stg{i}"))) for i in range(2)]
    env.s_xstg = [Ctr(EC(nc.semaphore(f"s_xstg{i}"))) for i in range(2)]
    env.s_cx = [Ctr(EC(nc.semaphore(f"s_cx{i}"))) for i in range(2)]
    env.s_xs = [Ctr(EC(nc.semaphore(f"s_xs{i}"))) for i in range(2)]
    env.s_xd = [Ctr(EC(nc.semaphore(f"s_xd{i}"))) for i in range(2)]
    env.s_h1 = Ctr(EC(nc.semaphore("s_h1")))
    env.s_od = [Ctr(EC(nc.semaphore(f"s_od{i}"))) for i in range(2)]
    env.s_cd = [Ctr(EC(nc.semaphore(f"s_cd{i}"))) for i in range(2)]
    env.s_msc = Ctr(EC(nc.semaphore("s_msc")))
    env.s_dbg = Ctr(EC(nc.semaphore("s_dbg")))
    env.all_dma_ctrs = ([env.s_cst, env.s_h1, env.s_msc, env.s_dbg]
                        + env.s_stg + env.s_xstg + env.s_cx + env.s_xs
                        + env.s_xd + env.s_od + env.s_cd)

    env.ones_f1 = EC(nc.sbuf_tensor("ones_f1", [1, 128], F32))
    env.ones_fc = EC(nc.sbuf_tensor("ones_fc", [128, 1], F32))
    env.eps_t = EC(nc.sbuf_tensor("eps_t", [1, 1], F32))
    env.ident = EC(nc.sbuf_tensor("ident", [128, 128], F32))
    env.sel16 = EC(nc.sbuf_tensor("sel16", [128, 128], BF16))
    env.sel8h = EC(nc.sbuf_tensor("sel8h", [128, 8], BF16))
    env.sel8e = EC(nc.sbuf_tensor("sel8e", [8, 1024], BF16))
    env.qmax16 = EC(nc.sbuf_tensor("qmax16", [16, 1], F32))
    env.qmaxG = EC(nc.sbuf_tensor("qmaxG", [4, 1], F32))
    env.nrm_tmp = EC(nc.sbuf_tensor("nrm_tmp", [16, 1], F32))
    env.cwT = EC(nc.sbuf_tensor("cwT", [8, T], BF16))
    env.ctxh = nc.dram_tensor("ctxh", [D, T], BF16,
                              kind="ExternalOutput" if debug else "Internal").ap()

    es_w = ExitStack()
    wuk_bf = es_w.enter_context(nc.sbuf_tensor("wuk_bf", [128, 2 * D], BF16))
    wuv_bf = es_w.enter_context(nc.sbuf_tensor("wuv_bf", [128, 2 * D], BF16))
    es_ckq = ExitStack()
    env.c_kvT = [es_ckq.enter_context(nc.sbuf_tensor(f"ckv{i}", [128, T], BF16))
                 for i in range(2)]
    env.qT = [es_ckq.enter_context(nc.sbuf_tensor(f"qT{i}", [128, T], BF16))
              for i in range(8)]

    es_x = ExitStack()
    wdkv_bf = es_x.enter_context(nc.sbuf_tensor("wdkv_bf", [128, 8 * DL], BF16))
    wuq_bf = es_x.enter_context(nc.sbuf_tensor("wuq_bf", [128, 8 * D], BF16))
    xT_bf = es_x.enter_context(nc.sbuf_tensor("xT_bf", [128, 8 * env.T], BF16))

    _p0(env, wdkv_bf, wuk_bf, wuv_bf, wuq_bf, xT_bf)

    groups_ckv = [(m, ch) for m in range(2) for ch in range(env.NCH)]
    _proj_phase(
        env, "ckv", groups_ckv, 8,
        lambda g, k: wdkv_bf[:, k * DL + g[0] * 128: k * DL + (g[0] + 1) * 128],
        lambda g, k: xT_bf[:, k * T + g[1] * CH: k * T + (g[1] + 1) * CH],
        lambda ve, g, ps: ve.tensor_copy(
            out=env.c_kvT[g[0]][:, g[1] * CH:(g[1] + 1) * CH], in_=ps[:]),
    )
    groups_q = [(m, ch) for m in range(8) for ch in range(env.NCH)]
    _proj_phase(
        env, "q", groups_q, 8,
        lambda g, k: wuq_bf[:, k * D + g[0] * 128: k * D + (g[0] + 1) * 128],
        lambda g, k: xT_bf[:, k * T + g[1] * CH: k * T + (g[1] + 1) * CH],
        lambda ve, g, ps: ve.tensor_copy(
            out=env.qT[g[0]][:, g[1] * CH:(g[1] + 1) * CH], in_=ps[:]),
    )
    _qnorms(env)

    if debug:
        sch = Sched(nc)
        for m in range(2):
            sch.op("gpsimd", lambda gp, m=m: gp.dma_start(
                out=env.dbg_ckv[m * 128:(m + 1) * 128, :],
                in_=env.c_kvT[m][:]), inc=env.s_dbg, inck=16)
        for m in range(8):
            sch.op("gpsimd", lambda gp, m=m: gp.dma_start(
                out=env.dbg_q[m * 128:(m + 1) * 128, :], in_=env.qT[m][:]),
                inc=env.s_dbg, inck=16)
        sch.drain(env.s_dbg)
        sch.run("dbgq")

    es_x.close()

    for g in range(NH // HG):
        _attn_group(env, g, wuk_bf, wuv_bf)

    es_ckq.close()
    es_w.close()

    es_p6 = ExitStack()
    wo_bf = es_p6.enter_context(nc.sbuf_tensor("wo_bf", [128, 8 * D], BF16))
    rw_bf = es_p6.enter_context(nc.sbuf_tensor("rw_bf", [128, 8 * E], BF16))
    with nc.sbuf_tensor("wstg0b", [128, D], F32) as wstg0b, \
         nc.sbuf_tensor("wstg1b", [128, D], F32) as wstg1b:
        wp2 = []
        for kd in range(8):
            wp2.append((wo_bf, kd * D, D, env.wo[kd * 128:(kd + 1) * 128, :]))
        for kd in range(8):
            wp2.append((rw_bf, kd * E, E, env.rw[kd * 128:(kd + 1) * 128, :]))
        _stream_pieces(env, wp2, [wstg0b, wstg1b])

    _p6(env, wo_bf, rw_bf)
    es_p6.close()

    for th in range(2):
        _moe_pass(env, th)

    if debug:
        sch = Sched(nc)
        sch.op("gpsimd", lambda gp: gp.dma_start(out=env.dbg_cw[:],
                                                 in_=env.cwT[:]),
               inc=env.s_dbg, inck=16)
        sch.drain(env.s_dbg)
        sch.run("dbgcw")

    with nc.Block("fin") as blk:
        def fin(gp):
            for ctr in env.all_dma_ctrs:
                if ctr.n > 0:
                    gp.wait_ge(ctr.h, ctr.n)
        blk.gpsimd(fin)

    ctx.close()
    return nc


def _consts():
    ident = np.eye(128, dtype=np.float32)
    sel16 = np.zeros((128, 128), dtype=ml_dtypes.bfloat16)
    for kd in range(8):
        sel16[0:64, 16 * kd + 2 * kd] = 1
        sel16[64:128, 16 * kd + 2 * kd + 1] = 1
    sel8h = np.zeros((128, 8), dtype=ml_dtypes.bfloat16)
    for p in range(2):
        sel8h[0:64, 4 * p + 2 * p] = 1
        sel8h[64:128, 4 * p + 2 * p + 1] = 1
    sel8e = np.zeros((8, 1024), dtype=ml_dtypes.bfloat16)
    for e in range(8):
        sel8e[e, e * 128:(e + 1) * 128] = 1
    return {"c_ident": ident, "c_sel16": sel16, "c_sel8h": sel8h,
            "c_sel8e": sel8e}


def kernel(L0_attn_x, L0_attn_W_DKV, L0_attn_W_UK, L0_attn_W_UV, L0_attn_W_UQ,
           L0_attn_W_O, L0_ffn_router_w, L0_ffn_w1, L0_ffn_w2):
    x = np.ascontiguousarray(np.asarray(L0_attn_x, dtype=np.float32))
    assert x.shape == (8, 4096, 1024)
    shared = {
        "wdkv": np.ascontiguousarray(np.asarray(L0_attn_W_DKV, np.float32)),
        "wuk": np.ascontiguousarray(np.asarray(L0_attn_W_UK, np.float32)),
        "wuv": np.ascontiguousarray(np.asarray(L0_attn_W_UV, np.float32)),
        "wuq": np.ascontiguousarray(np.asarray(L0_attn_W_UQ, np.float32)),
        "wo": np.ascontiguousarray(np.asarray(L0_attn_W_O, np.float32)),
        "rw": np.ascontiguousarray(np.asarray(L0_ffn_router_w, np.float32)),
        "w1": np.ascontiguousarray(np.asarray(L0_ffn_w1, np.float32)),
        "w2": np.ascontiguousarray(np.asarray(L0_ffn_w2, np.float32)),
    }
    shared.update(_consts())
    nc = build(T=4096, debug=False)
    in_maps = []
    for b in range(8):
        m = dict(shared)
        m["xT"] = np.ascontiguousarray(x[b].T)
        in_maps.append(m)
    res = run_bass_kernel_spmd(nc, in_maps, core_ids=list(range(8)))
    out = np.stack([np.asarray(r["outT"], dtype=np.float32).T
                    for r in res.results], axis=0)
    return out


# version 07
# speedup vs baseline: 1.1219x; 1.0218x over previous
import numpy as np
import ml_dtypes
from contextlib import ExitStack

import concourse.bass as bass
import concourse.mybir as mybir
from concourse.bass_utils import run_bass_kernel_spmd

AF = mybir.ActivationFunctionType
ALU = mybir.AluOpType
AXX = mybir.AxisListType.X
F32 = mybir.dt.float32
BF16 = mybir.dt.bfloat16

D = 1024
DL = 256
NH = 16
HD = 64
E = 8
FF = 2048
CH = 512
HG = 4
EPS_LN = 1e-5
ENGINES = ("sync", "tensor", "scalar", "vector", "gpsimd")


class Ctr:
    def __init__(self, h):
        self.h = h
        self.n = 0


class Tok:
    __slots__ = ("ctr", "val", "k")

    def __init__(self, ctr, k, val):
        self.ctr = ctr
        self.k = k
        self.val = val


class Sched:

    def __init__(self, nc):
        self.nc = nc
        self.items = []

    def op(self, eng, fn, deps=(), inc=None, inck=1):
        deps = tuple(d for d in deps if d is not None)
        tok = None
        if inc is not None:
            inc.n += inck
            tok = Tok(inc, inck, inc.n)
        self.items.append((eng, fn, deps, tok))
        return tok

    def drain(self, ctr, eng="gpsimd"):
        v = ctr.n
        if v > 0:
            self.op(eng, lambda e, h=ctr.h, v=v: e.wait_ge(h, v))

    def run(self, name=None):
        with self.nc.Block(name) as blk:
            for eng in ENGINES:
                items = [it for it in self.items if it[0] == eng]
                if not items:
                    continue

                def body(e, items=items):
                    last = {}
                    for (_, fn, deps, tok) in items:
                        for d in deps:
                            if last.get(id(d.ctr), -1) < d.val:
                                e.wait_ge(d.ctr.h, d.val)
                                last[id(d.ctr)] = d.val
                        inst = fn(e)
                        if tok is not None:
                            inst.then_inc(tok.ctr.h, tok.k)
                getattr(blk, eng)(body)
        self.items = []


class Env:
    pass


def _stream_pieces(env, pieces, stg, dma_ctr=None, cast_eng="vector"):
    nc = env.nc
    sch = Sched(nc)
    toks = []
    cast_ctr = env.s_dv if cast_eng == "vector" else env.s_pl
    for i, (dst, c0, ncol, src) in enumerate(pieces):
        deps = [toks[i - 2]] if i >= 2 else []
        dt_ = sch.op("sync", lambda sy, i=i, ncol=ncol, src=src:
                     sy.dma_start(out=stg[i % 2][:, 0:ncol], in_=src),
                     deps=deps, inc=env.s_stg[i % 2], inck=16)
        ct = sch.op(cast_eng, lambda e, i=i, dst=dst, c0=c0, ncol=ncol:
                    e.tensor_copy(out=dst[:, c0:c0 + ncol],
                                  in_=stg[i % 2][:, 0:ncol]),
                    deps=[dt_], inc=cast_ctr)
        toks.append(ct)
    sch.run("stream")
    return toks


def _proj_phase(env, name, groups, nk, lhs_fn, rhs_fn, evac_fn, psum_n=CH):
    nc = env.nc
    toks = {}
    with nc.psum_tensor(f"{name}p0", [128, psum_n], F32) as p0, \
         nc.psum_tensor(f"{name}p1", [128, psum_n], F32) as p1:
        ps = [p0, p1]
        sch = Sched(nc)
        for gi, g in enumerate(groups):
            deps = [toks[groups[gi - 2]]] if gi >= 2 else []

            def mk_mm(g=g, gi=gi):
                def f(te):
                    for k in range(nk):
                        inst = te.matmul(ps[gi % 2][:], lhs_fn(g, k),
                                         rhs_fn(g, k),
                                         start=(k == 0), stop=(k == nk - 1))
                    return inst
                return f
            mt = sch.op("tensor", mk_mm(), deps=deps, inc=env.s_pe)

            def mk_ev(g=g, gi=gi):
                def f(ve):
                    return evac_fn(ve, g, ps[gi % 2])
                return f
            toks[g] = sch.op("vector", mk_ev(), deps=[mt], inc=env.s_dv)
        sch.run(name)
    return toks


def _p0(env, wdkv_bf, wuk_bf, wuv_bf, wuq_bf, xT_bf):
    nc = env.nc
    T = env.T
    with nc.sbuf_tensor("wstg0", [128, D], F32) as wstg0, \
         nc.sbuf_tensor("wstg1", [128, D], F32) as wstg1, \
         nc.sbuf_tensor("xstg0", [128, 1024], F32) as xstg0, \
         nc.sbuf_tensor("xstg1", [128, 1024], F32) as xstg1:
        wstg = [wstg0, wstg1]
        xstg = [xstg0, xstg1]
        sch = Sched(nc)
        ctok = None
        for dst, src in ((env.ident, env.c_ident), (env.sel16, env.c_sel16),
                         (env.sel8h, env.c_sel8h), (env.sel8e, env.c_sel8e)):
            ctok = sch.op("sync", lambda sy, dst=dst, src=src:
                          sy.dma_start(out=dst[:], in_=src),
                          inc=env.s_cst, inck=16)
        env.consts_tok = ctok
        sch.op("vector", lambda ve: ve.memset(env.ones_f1[:], 1.0))
        sch.op("vector", lambda ve: ve.memset(env.ones_fc[:], 1.0))
        sch.op("vector", lambda ve: ve.memset(env.eps_t[:], EPS_LN))
        sch.op("vector", lambda ve: ve.memset(env.qmax16[:], 0.0))
        wpieces = []
        for kd in range(8):
            wpieces.append((wdkv_bf, kd * DL, DL,
                            env.wdkv[kd * 128:(kd + 1) * 128, :]))
        for kl in range(2):
            wpieces.append((wuk_bf, kl * D, D,
                            env.wuk[kl * 128:(kl + 1) * 128, :]))
        for kl in range(2):
            wpieces.append((wuv_bf, kl * D, D,
                            env.wuv[kl * 128:(kl + 1) * 128, :]))
        for kd in range(8):
            wpieces.append((wuq_bf, kd * D, D,
                            env.wuq[kd * 128:(kd + 1) * 128, :]))
        wtoks = []
        for i, (dst, c0, ncol, src) in enumerate(wpieces):
            deps = [wtoks[i - 2]] if i >= 2 else []
            dt_ = sch.op("sync", lambda sy, i=i, ncol=ncol, src=src:
                         sy.dma_start(out=wstg[i % 2][:, 0:ncol], in_=src),
                         deps=deps, inc=env.s_stg[i % 2], inck=16)
            ct = sch.op("vector", lambda ve, i=i, dst=dst, c0=c0, ncol=ncol:
                        ve.tensor_copy(out=dst[:, c0:c0 + ncol],
                                       in_=wstg[i % 2][:, 0:ncol]),
                        deps=[dt_], inc=env.s_dv)
            wtoks.append(ct)
        xtoks = []
        xpieces = [(kd, cp) for kd in range(8) for cp in range(T // 1024)]
        for i, (kd, cp) in enumerate(xpieces):
            deps = [xtoks[i - 2]] if i >= 2 else []
            dt_ = sch.op("sync", lambda sy, i=i, kd=kd, cp=cp:
                         sy.dma_start(out=xstg[i % 2][:],
                                      in_=env.xT[kd * 128:(kd + 1) * 128,
                                                 cp * 1024:(cp + 1) * 1024]),
                         deps=deps, inc=env.s_xstg[i % 2], inck=16)
            ct = sch.op("gpsimd", lambda gp, i=i, kd=kd, cp=cp:
                        gp.tensor_copy(
                            out=xT_bf[:, kd * T + cp * 1024:
                                      kd * T + (cp + 1) * 1024],
                            in_=xstg[i % 2][:]),
                        deps=[dt_], inc=env.s_pl)
            xtoks.append(ct)
        sch.run("p0")


def _qnorms(env):
    nc = env.nc
    NCH = env.NCH
    with nc.sbuf_tensor("qsq0", [128, CH], BF16) as qsq0, \
         nc.sbuf_tensor("qsq1", [128, CH], BF16) as qsq1, \
         nc.psum_tensor("psn16", [16, CH], F32) as psn16:
        qsq = [qsq0, qsq1]
        sch = Sched(nc)
        sq_toks = []
        prev_red = None
        prev_mx = None
        i = 0
        for chq in range(NCH):
            grp_tok = None
            for kd in range(8):
                deps = [sq_toks[i - 2][1]] if i >= 2 else []
                st_ = sch.op("scalar", lambda sc, i=i, kd=kd, chq=chq:
                             sc.activation(
                                 out=qsq[i % 2][:],
                                 in_=env.qT[kd][:, chq * CH:(chq + 1) * CH],
                                 func=AF.Square),
                             deps=deps, inc=env.s_ac)
                mdeps = [st_, env.consts_tok]
                if kd == 0 and prev_red is not None:
                    mdeps.append(prev_red)
                mt = sch.op("tensor", lambda te, i=i, kd=kd:
                            te.matmul(psn16[:],
                                      env.sel16[:, kd * 16:(kd + 1) * 16],
                                      qsq[i % 2][:],
                                      start=(kd == 0), stop=(kd == 7)),
                            deps=mdeps, inc=env.s_pe)
                sq_toks.append((st_, mt))
                grp_tok = mt
                i += 1
            rd = sch.op("vector", lambda ve: ve.reduce_max(
                out=env.nrm_tmp[:], in_=psn16[:], axis=AXX),
                deps=[grp_tok] + ([prev_mx] if prev_mx else []), inc=env.s_dv)
            mx = sch.op("vector", lambda ve: ve.tensor_max(
                out=env.qmax16[:], in0=env.qmax16[:], in1=env.nrm_tmp[:]),
                deps=[rd], inc=env.s_dv)
            prev_red = rd
            prev_mx = mx
        sch.run("p2n")


def _attn_group(env, g, wuk_bf, wuv_bf):
    nc = env.nc
    T, NCH, NKT = env.T, env.NCH, env.NKT
    VW = HG * 65
    with nc.sbuf_tensor(f"kTg{g}", [128, 2 * T], BF16) as kTg, \
         nc.sbuf_tensor(f"vaug{g}", [128, NKT * VW], BF16) as vaug, \
         nc.sbuf_tensor(f"bias4_{g}", [128, 4], F32) as bias4, \
         nc.sbuf_tensor(f"kmax4_{g}", [4, 1], F32) as kmax4, \
         nc.sbuf_tensor(f"ktmp4_{g}", [4, 1], F32) as ktmp4, \
         nc.sbuf_tensor(f"brow_{g}", [1, 4], F32) as brow, \
         nc.sbuf_tensor(f"probT_{g}", [128, 6 * CH], BF16) as probT, \
         nc.sbuf_tensor(f"probM_{g}", [128, 4 * CH], BF16) as probM, \
         nc.sbuf_tensor(f"recipS_{g}", [1, 2 * CH], F32) as recipS, \
         nc.sbuf_tensor(f"rbS_{g}", [64, 2 * CH], F32) as rbS, \
         nc.sbuf_tensor(f"ksq0_{g}", [128, CH], BF16) as ksq0, \
         nc.sbuf_tensor(f"ksq1_{g}", [128, CH], BF16) as ksq1, \
         nc.sbuf_tensor(f"ctxs_{g}", [64, 2 * CH], BF16) as ctxs:

        groups_k = [(p, ch) for p in range(2) for ch in range(NCH)]
        ktoks = _proj_phase(
            env, f"kT{g}", groups_k, 2,
            lambda gk, k: wuk_bf[:, k * D + g * 256 + gk[0] * 128:
                                 k * D + g * 256 + (gk[0] + 1) * 128],
            lambda gk, k: env.c_kvT[k][:, gk[1] * CH:(gk[1] + 1) * CH],
            lambda ve, gk, ps: ve.tensor_copy(
                out=kTg[:, gk[0] * T + gk[1] * CH:
                        gk[0] * T + (gk[1] + 1) * CH],
                in_=ps[:]),
        )

        sch = Sched(nc)

        def ones_cols(gp):
            inst = None
            for kt in range(NKT):
                for hh in range(HG):
                    inst = gp.memset(vaug[:, kt * VW + hh * 65 + 64:
                                          kt * VW + hh * 65 + 65], 1.0)
            return inst
        vones_tok = sch.op("gpsimd", ones_cols, inc=env.s_pl)
        sch.run(f"vones{g}")

        def v_evac(ve, kt, ps):
            inst = None
            for hh in range(HG):
                inst = ve.tensor_copy(
                    out=vaug[:, kt * VW + hh * 65: kt * VW + hh * 65 + 64],
                    in_=ps[:, hh * 64:(hh + 1) * 64])
            return inst
        vtoks = _proj_phase(
            env, f"v{g}", list(range(NKT)), 2,
            lambda kt, k: env.c_kvT[k][:, kt * 128:(kt + 1) * 128],
            lambda kt, k: wuv_bf[:, k * D + g * 256: k * D + (g + 1) * 256],
            v_evac, psum_n=HG * 64,
        )

        bias_tok = None
        with nc.psum_tensor(f"psn4_{g}", [4, CH], F32) as psn4, \
             nc.psum_tensor(f"psb4_{g}", [128, 4], F32) as psb4:
            ksq = [ksq0, ksq1]
            sch = Sched(nc)
            km0 = sch.op("vector", lambda ve: ve.memset(kmax4[:], 0.0),
                         inc=env.s_dv)
            i = 0
            prev_red = None
            prev_mx = km0
            sq_list = []
            for chq in range(NCH):
                grp_tok = None
                for p in range(2):
                    deps = [sq_list[i - 2][1]] if i >= 2 else []
                    st_ = sch.op(
                        "scalar", lambda sc, i=i, p=p, chq=chq:
                        sc.activation(
                            out=ksq[i % 2][:],
                            in_=kTg[:, p * T + chq * CH:
                                    p * T + (chq + 1) * CH],
                            func=AF.Square),
                        deps=deps + [ktoks[(p, chq)]], inc=env.s_ac)
                    mdeps = [st_, env.consts_tok]
                    if p == 0 and prev_red is not None:
                        mdeps.append(prev_red)
                    mt = sch.op("tensor", lambda te, i=i, p=p:
                                te.matmul(psn4[:],
                                          env.sel8h[:, p * 4:(p + 1) * 4],
                                          ksq[i % 2][:],
                                          start=(p == 0), stop=(p == 1)),
                                deps=mdeps, inc=env.s_pe)
                    sq_list.append((st_, mt))
                    grp_tok = mt
                    i += 1
                rd = sch.op("vector", lambda ve: ve.reduce_max(
                    out=ktmp4[:], in_=psn4[:], axis=AXX),
                    deps=[grp_tok, prev_mx], inc=env.s_dv)
                mx = sch.op("vector", lambda ve: ve.tensor_max(
                    out=kmax4[:], in0=kmax4[:], in1=ktmp4[:]),
                    deps=[rd], inc=env.s_dv)
                prev_red = rd
                prev_mx = mx
            qg = sch.op("sync", lambda sy: sy.dma_start(
                out=env.qmaxG[:], in_=env.qmax16[g * 4:(g + 1) * 4, :],
                allow_slow_non_contiguous=True), inc=env.s_msc, inck=16)
            t1 = sch.op("vector", lambda ve: ve.tensor_mul(
                out=ktmp4[:], in0=kmax4[:], in1=env.qmaxG[:]),
                deps=[prev_mx, prev_red, qg], inc=env.s_dv)
            t2 = sch.op("scalar", lambda sc: sc.activation(
                out=ktmp4[:], in_=ktmp4[:], func=AF.Sqrt),
                deps=[t1], inc=env.s_ac)
            t3 = sch.op("vector", lambda ve: ve.tensor_scalar_mul(
                out=ktmp4[:], in0=ktmp4[:], scalar1=-0.125),
                deps=[t2], inc=env.s_dv)
            t4 = sch.op("sync", lambda sy: sy.dma_start(
                out=brow[:], in_=ktmp4[:], allow_slow_non_contiguous=True),
                deps=[t3], inc=env.s_msc, inck=16)
            t5 = sch.op("tensor", lambda te: te.matmul(
                psb4[:], env.ones_f1[0:1, :], brow[:], start=True, stop=True),
                deps=[t4], inc=env.s_pe)
            bias_tok = sch.op("vector", lambda ve: ve.tensor_copy(
                out=bias4[:], in_=psb4[:]), deps=[t5], inc=env.s_dv)
            sch.run(f"knrm{g}")

        _attn_loop(env, g, kTg, vaug, bias4, probT, probM, recipS, rbS,
                   ctxs, ktoks, vtoks, vones_tok, bias_tok)


def _attn_loop(env, g, kTg, vaug, bias4, probT, probM, recipS, rbS,
               ctxs, ktoks, vtoks, vones_tok, bias_tok):
    nc = env.nc
    T, NCH, NKT = env.T, env.NCH, env.NKT
    VW = HG * 65
    PW = 2 * CH
    jobs = [(hh, qc) for qc in range(NCH - 1, -1, -1) for hh in range(HG)]
    with nc.psum_tensor(f"ps_s0_{g}", [128, PW], F32) as ps_s0, \
         nc.psum_tensor(f"ps_s1_{g}", [128, PW], F32) as ps_s1, \
         nc.psum_tensor(f"ps_s2_{g}", [128, PW], F32) as ps_s2, \
         nc.psum_tensor(f"ps_c0_{g}", [65, CH], F32) as ps_c0, \
         nc.psum_tensor(f"ps_c1_{g}", [65, CH], F32) as ps_c1:
        ps_s = [ps_s0, ps_s1, ps_s2]
        ps_c = [ps_c0, ps_c1]
        sch = Sched(nc)
        pslot_free = [[], [], []]
        borrowed = [None, None, None]
        mslot_free = [None, None, None, None]
        exp_toks = {}
        mask_toks = {}
        mask_slot = {}
        pv_toks = {}
        job_dve = {}
        job_bc = {}
        gp_ = 0
        gt = 0
        d = 0
        for j, (hh, qc) in enumerate(jobs):
            nkt = min(NKT, 4 * (qc + 1))
            npair = nkt // 2
            h_glob = g * HG + hh
            pt = hh // 2
            rb = 64 * (hh % 2)
            qtile = h_glob // 2
            qb = 64 * (h_glob % 2)
            base_gt = gt
            base_gp = gp_
            for i in range(npair + 2):
                if i < npair:
                    pr = base_gp + i
                    sl = pr % 3
                    sct = None
                    for half in range(2):
                        kt = 2 * i + half
                        deps = [exp_toks.get(pr - 3), ktoks[(pt, kt // 4)],
                                borrowed[sl]]
                        sct = sch.op(
                            "tensor",
                            lambda te, pt=pt, rb=rb, kt=kt, qc=qc, qtile=qtile,
                            qb=qb, sl=sl, half=half:
                            te.matmul(
                                ps_s[sl][:, half * CH:(half + 1) * CH],
                                kTg[rb:rb + 64,
                                    pt * T + kt * 128: pt * T + (kt + 1) * 128],
                                env.qT[qtile][qb:qb + 64, qc * CH:(qc + 1) * CH],
                                start=True, stop=True),
                            deps=deps, inc=env.s_pe)
                    borrowed[sl] = None
                    edeps = [sct, bias_tok] + pslot_free[sl]
                    et = sch.op(
                        "scalar", lambda sc, sl=sl, hh=hh:
                        sc.activation(
                            out=probT[:, sl * PW:(sl + 1) * PW],
                            in_=ps_s[sl][:], func=AF.Exp,
                            scale=0.125, bias=bias4[:, hh:hh + 1]),
                        deps=edeps, inc=env.s_ac)
                    exp_toks[pr] = et
                    pslot_free[sl] = []
                    for half in range(2):
                        kt = 2 * i + half
                        t_this = base_gt + kt
                        if kt >= nkt - 4:
                            mdeps = [et, mslot_free[d % 4]]
                            mt = sch.op(
                                "gpsimd",
                                lambda gp, sl=sl, half=half, d=d, qc=qc, kt=kt:
                                gp.affine_select(
                                    out=probM[:, (d % 4) * CH:(d % 4 + 1) * CH],
                                    in_=probT[:, sl * PW + half * CH:
                                              sl * PW + (half + 1) * CH],
                                    pattern=[[1, CH]], compare_op=ALU.is_ge,
                                    fill=0.0, base=qc * CH - kt * 128,
                                    channel_multiplier=-1),
                                deps=mdeps, inc=env.s_pl)
                            mask_toks[t_this] = mt
                            mask_slot[t_this] = d % 4
                            pslot_free[sl].append(mt)
                            d += 1
                if i >= 2:
                    ip = i - 2
                    pr = base_gp + ip
                    for half in range(2):
                        ii = 2 * ip + half
                        t_pv = base_gt + ii
                        diag = ii >= nkt - 4
                        deps = [vtoks[ii], vones_tok]
                        if diag:
                            deps.append(mask_toks[t_pv])
                            rhs_slot = ('m', mask_slot[t_pv])
                        else:
                            deps.append(exp_toks[pr])
                            rhs_slot = ('t', (pr % 3, half))
                        if ii == 0 and j >= 2:
                            deps.append(job_dve[j - 2][2])

                        def mk_pv(ii=ii, j=j, hh=hh, nkt=nkt, rhs_slot=rhs_slot):
                            def f(te):
                                if rhs_slot[0] == 'm':
                                    rhs = probM[:, rhs_slot[1] * CH:
                                                (rhs_slot[1] + 1) * CH]
                                else:
                                    sl, hf = rhs_slot[1]
                                    rhs = probT[:, sl * PW + hf * CH:
                                                sl * PW + (hf + 1) * CH]
                                return te.matmul(
                                    ps_c[j % 2][:],
                                    vaug[:, ii * VW + hh * 65:
                                         ii * VW + hh * 65 + 65],
                                    rhs, start=(ii == 0), stop=(ii == nkt - 1))
                            return f
                        pt_ = sch.op("tensor", mk_pv(), deps=deps, inc=env.s_pe)
                        pv_toks[t_pv] = pt_
                        if diag:
                            mslot_free[mask_slot[t_pv]] = pt_
                        else:
                            pslot_free[pr % 3].append(pt_)
            gt = base_gt + nkt
            gp_ = base_gp + npair
            bsl = (gp_ - 1) % 3
            last_pv = pv_toks[gt - 1]
            rdeps = [last_pv]
            if j >= 2:
                rdeps.append(job_bc[j - 2])
            r1 = sch.op("vector", lambda ve, j=j:
                        ve.reciprocal(
                            out=recipS[0:1, (j % 2) * CH:(j % 2 + 1) * CH],
                            in_=ps_c[j % 2][64:65, :]),
                        deps=rdeps, inc=env.s_dv)
            bdeps = [r1, exp_toks[gp_ - 1]]
            b1 = sch.op("tensor", lambda te, j=j, bsl=bsl:
                        te.matmul(ps_s[bsl][0:64, 0:CH],
                                  env.ones_f1[0:1, 0:64],
                                  recipS[0:1, (j % 2) * CH:(j % 2 + 1) * CH],
                                  start=True, stop=True),
                        deps=bdeps, inc=env.s_pe)
            job_bc[j] = b1
            e1 = sch.op("vector", lambda ve, j=j, bsl=bsl:
                        ve.tensor_copy(
                            out=rbS[:, (j % 2) * CH:(j % 2 + 1) * CH],
                            in_=ps_s[bsl][0:64, 0:CH]),
                        deps=[b1], inc=env.s_dv)
            borrowed[bsl] = e1
            cdeps = [e1]
            if j >= 2:
                cdeps.append(job_dve[j - 2][3])
            c1 = sch.op("vector",
                        lambda ve, j=j:
                        ve.tensor_mul(
                            out=ctxs[:, (j % 2) * CH:(j % 2 + 1) * CH],
                            in0=ps_c[j % 2][0:64, :],
                            in1=rbS[:, (j % 2) * CH:(j % 2 + 1) * CH]),
                        deps=cdeps, inc=env.s_dv)
            cd = sch.op("gpsimd",
                        lambda gp, j=j, h_glob=h_glob, qc=qc:
                        gp.dma_start(
                            out=env.ctxh[h_glob * 64:(h_glob + 1) * 64,
                                         qc * CH:(qc + 1) * CH],
                            in_=ctxs[:, (j % 2) * CH:(j % 2 + 1) * CH]),
                        deps=[c1], inc=env.s_cd[j % 2], inck=16)
            job_dve[j] = (r1, e1, c1, cd)
        sch.drain(env.s_cd[0])
        sch.drain(env.s_cd[1])
        sch.run(f"attn{g}")


def _p6(env, wo_bf, rw_bf):
    nc = env.nc
    T, NCH = env.T, env.NCH
    with nc.sbuf_tensor("ctxr", [128, 16 * CH], BF16) as ctxr, \
         nc.sbuf_tensor("xst", [128, 16 * CH], F32) as xst, \
         nc.sbuf_tensor("resid", [128, 8 * CH], F32) as resid, \
         nc.sbuf_tensor("rsq", [128, 8 * CH], F32) as rsq, \
         nc.sbuf_tensor("x1bs", [128, 8 * CH], BF16) as x1bs, \
         nc.sbuf_tensor("ab_sb", [128, 2 * CH], F32) as ab_sb, \
         nc.sbuf_tensor("rows", [1, 6 * CH], F32) as rows, \
         nc.sbuf_tensor("cwtile", [128, E], F32) as cwtile, \
         nc.sbuf_tensor("cwmask", [128, E], F32) as cwmask, \
         nc.sbuf_tensor("cwrow", [128, 5], F32) as cwrow, \
         nc.psum_tensor("ps_o0", [128, CH], F32) as ps_o0, \
         nc.psum_tensor("ps_o1", [128, CH], F32) as ps_o1, \
         nc.psum_tensor("ps_sum", [1, CH], F32) as ps_sum, \
         nc.psum_tensor("ps_ssq", [1, CH], F32) as ps_ssq, \
         nc.psum_tensor("ps_A", [128, CH], F32) as ps_A, \
         nc.psum_tensor("ps_B", [128, CH], F32) as ps_B, \
         nc.psum_tensor("ps_lg", [128, E], F32) as ps_lg, \
         nc.psum_tensor("ps_ct", [8, 128], F32) as ps_ct:
        ps_o = [ps_o0, ps_o1]
        sch = Sched(nc)
        prev = {}
        for chq in range(NCH):
            st = {}
            ctoks = []
            for kd in range(8):
                deps = []
                if chq >= 2:
                    deps.append(prev[chq - 2]['wo'][7])
                ctoks.append(sch.op(
                    "sync", lambda sy, kd=kd, chq=chq:
                    sy.dma_start(
                        out=ctxr[:, ((chq % 2) * 8 + kd) * CH:
                                 ((chq % 2) * 8 + kd + 1) * CH],
                        in_=env.ctxh[kd * 128:(kd + 1) * 128,
                                     chq * CH:(chq + 1) * CH]),
                    deps=deps, inc=env.s_cx[chq % 2], inck=16))
            xtoks = []
            for m in range(8):
                deps = []
                if chq >= 2:
                    deps.append(prev[chq - 2]['ra'][m])
                xtoks.append(sch.op(
                    "sync", lambda sy, m=m, chq=chq:
                    sy.dma_start(
                        out=xst[:, ((chq % 2) * 8 + m) * CH:
                                ((chq % 2) * 8 + m + 1) * CH],
                        in_=env.xT[m * 128:(m + 1) * 128,
                                   chq * CH:(chq + 1) * CH]),
                    deps=deps, inc=env.s_xs[chq % 2], inck=16))
            ra = [None] * 8
            sq = [None] * 8
            sums = [None] * 8
            ssqs = [None] * 8
            wo_toks = [None] * 8
            for m in range(8):
                deps = [ctoks[-1]]
                if chq * 8 + m >= 2:
                    pm = chq * 8 + m - 2
                    deps.append(ra[pm % 8] if pm // 8 == chq
                                else prev[pm // 8]['ra'][pm % 8])
                wo_toks[m] = sch.op(
                    "tensor", lambda te, m=m, chq=chq:
                    [te.matmul(ps_o[m % 2][:],
                               wo_bf[:, kd * D + m * 128:
                                     kd * D + (m + 1) * 128],
                               ctxr[:, ((chq % 2) * 8 + kd) * CH:
                                    ((chq % 2) * 8 + kd + 1) * CH],
                               start=(kd == 0), stop=(kd == 7))
                     for kd in range(8)][-1],
                    deps=deps, inc=env.s_pe)
                rdeps = [wo_toks[m], xtoks[-1]]
                if chq >= 1:
                    rdeps.append(prev[chq - 1]['x1bdma'][7])
                    rdeps.append(prev[chq - 1]['sum'][m])
                    rdeps.append(prev[chq - 1]['lg_last'])
                ra[m] = sch.op(
                    "vector", lambda ve, m=m, chq=chq:
                    ve.tensor_add(
                        out=resid[:, m * CH:(m + 1) * CH],
                        in0=ps_o[m % 2][:],
                        in1=xst[:, ((chq % 2) * 8 + m) * CH:
                                ((chq % 2) * 8 + m + 1) * CH]),
                    deps=rdeps, inc=env.s_dv)
                sdeps = [ra[m]]
                if chq >= 1:
                    sdeps.append(prev[chq - 1]['ssq'][m])
                    sdeps.append(prev[chq - 1]['n1'][m])
                sq[m] = sch.op(
                    "scalar", lambda sc, m=m:
                    sc.activation(out=rsq[:, m * CH:(m + 1) * CH],
                                  in_=resid[:, m * CH:(m + 1) * CH],
                                  func=AF.Square),
                    deps=sdeps, inc=env.s_ac)
            for m in range(8):
                sums[m] = sch.op(
                    "tensor", lambda te, m=m:
                    te.matmul(ps_sum[:], env.ones_fc[:],
                              resid[:, m * CH:(m + 1) * CH],
                              start=(m == 0), stop=(m == 7)),
                    deps=[ra[m]] + ([prev[chq - 1]['rows0']]
                                    if (m == 0 and chq >= 1) else []),
                    inc=env.s_pe)
                ssqs[m] = sch.op(
                    "tensor", lambda te, m=m:
                    te.matmul(ps_ssq[:], env.ones_fc[:],
                              rsq[:, m * CH:(m + 1) * CH],
                              start=(m == 0), stop=(m == 7)),
                    deps=[sq[m]] + ([prev[chq - 1]['rows1']]
                                    if (m == 0 and chq >= 1) else []),
                    inc=env.s_pe)
            c1 = sch.op("vector", lambda ve: ve.tensor_scalar_mul(
                out=rows[0:1, 0:CH], in0=ps_sum[:], scalar1=1.0 / D),
                deps=[sums[7]], inc=env.s_dv)
            c2 = sch.op("vector", lambda ve: ve.tensor_mul(
                out=rows[0:1, CH:2 * CH], in0=rows[0:1, 0:CH],
                in1=rows[0:1, 0:CH]), deps=[c1], inc=env.s_dv)
            c3 = sch.op("vector", lambda ve: ve.tensor_scalar_mul(
                out=rows[0:1, 4 * CH:5 * CH], in0=ps_ssq[:],
                scalar1=1.0 / D), deps=[ssqs[7]], inc=env.s_dv)
            c4 = sch.op("vector", lambda ve: ve.tensor_sub(
                out=rows[0:1, CH:2 * CH], in0=rows[0:1, 4 * CH:5 * CH],
                in1=rows[0:1, CH:2 * CH]), deps=[c2, c3], inc=env.s_dv)
            c5 = sch.op("scalar", lambda sc: sc.activation(
                out=rows[0:1, 5 * CH:6 * CH], in_=rows[0:1, CH:2 * CH],
                func=AF.Sqrt, bias=env.eps_t[:]), deps=[c4], inc=env.s_ac)
            c6 = sch.op("vector", lambda ve: ve.reciprocal(
                out=rows[0:1, 2 * CH:3 * CH], in_=rows[0:1, 5 * CH:6 * CH]),
                deps=[c5], inc=env.s_dv)
            c7 = sch.op("vector", lambda ve: ve.tensor_mul(
                out=rows[0:1, 3 * CH:4 * CH], in0=rows[0:1, 0:CH],
                in1=rows[0:1, 2 * CH:3 * CH]), deps=[c6], inc=env.s_dv)
            c8 = sch.op("vector", lambda ve: ve.tensor_scalar_mul(
                out=rows[0:1, 3 * CH:4 * CH], in0=rows[0:1, 3 * CH:4 * CH],
                scalar1=-1.0), deps=[c7], inc=env.s_dv)
            st['rows0'] = c1
            st['rows1'] = c3
            bdeps_extra = [prev[chq - 1]['abev']] if chq >= 1 else []
            bA = sch.op("tensor", lambda te: te.matmul(
                ps_A[:], env.ones_f1[0:1, :], rows[0:1, 2 * CH:3 * CH],
                start=True, stop=True), deps=[c6] + bdeps_extra, inc=env.s_pe)
            bB = sch.op("tensor", lambda te: te.matmul(
                ps_B[:], env.ones_f1[0:1, :], rows[0:1, 3 * CH:4 * CH],
                start=True, stop=True), deps=[c8] + bdeps_extra, inc=env.s_pe)
            evdeps = [bA, bB]
            if chq >= 1:
                evdeps.append(prev[chq - 1]['norm_last'])
            abev = sch.op("vector", lambda ve: [
                ve.tensor_copy(out=ab_sb[:, 0:CH], in_=ps_A[:]),
                ve.tensor_copy(out=ab_sb[:, CH:2 * CH], in_=ps_B[:])][-1],
                deps=evdeps, inc=env.s_dv)
            st['abev'] = abev
            norm_toks = [None] * 8
            cast2 = [None] * 8
            n1s = [None] * 8
            for m in range(8):
                n1 = sch.op("vector", lambda ve, m=m: ve.tensor_mul(
                    out=rsq[:, m * CH:(m + 1) * CH],
                    in0=resid[:, m * CH:(m + 1) * CH], in1=ab_sb[:, 0:CH]),
                    deps=[abev, sq[m], sums[m], ssqs[m]], inc=env.s_dv)
                n1s[m] = n1
                n2 = sch.op("vector", lambda ve, m=m: ve.tensor_add(
                    out=resid[:, m * CH:(m + 1) * CH],
                    in0=rsq[:, m * CH:(m + 1) * CH],
                    in1=ab_sb[:, CH:2 * CH]), deps=[n1], inc=env.s_dv)
                n3 = sch.op("vector", lambda ve, m=m: ve.tensor_copy(
                    out=x1bs[:, m * CH:(m + 1) * CH],
                    in_=resid[:, m * CH:(m + 1) * CH]), deps=[n2],
                    inc=env.s_dv)
                norm_toks[m] = n2
                cast2[m] = n3
            st['norm_last'] = cast2[7]
            st['ra'] = ra
            st['sum'] = sums
            st['ssq'] = ssqs
            st['n1'] = n1s
            x1fdma = [None] * 8
            x1bdma = [None] * 8
            for m in range(8):
                x1fdma[m] = sch.op(
                    "gpsimd", lambda gp, m=m, chq=chq:
                    gp.dma_start(out=env.x1f[m * 128:(m + 1) * 128,
                                             chq * CH:(chq + 1) * CH],
                                 in_=resid[:, m * CH:(m + 1) * CH]),
                    deps=[norm_toks[m]], inc=env.s_xd[chq % 2], inck=16)
                x1bdma[m] = sch.op(
                    "gpsimd", lambda gp, m=m, chq=chq:
                    gp.dma_start(out=env.x1b[m * 128:(m + 1) * 128,
                                             chq * CH:(chq + 1) * CH],
                                 in_=x1bs[:, m * CH:(m + 1) * CH]),
                    deps=[cast2[m]], inc=env.s_xd[chq % 2], inck=16)
            st['x1fdma'] = x1fdma
            st['x1bdma'] = x1bdma
            st['wo'] = wo_toks
            lg_tok = None
            for tt in range(4):
                ldeps = [cast2[kd] for kd in range(8)]
                if tt >= 1:
                    ldeps += [st[f'exp{tt-1}'], st[f'm1red{tt-1}']]
                elif chq >= 1:
                    ldeps += [prev[chq - 1]['exp3'], prev[chq - 1]['m1red3']]
                lg_tok = sch.op(
                    "tensor", lambda te, tt=tt:
                    [te.matmul(ps_lg[:],
                               x1bs[:, kd * CH + tt * 128:
                                    kd * CH + (tt + 1) * 128],
                               rw_bf[:, kd * E:(kd + 1) * E],
                               start=(kd == 0), stop=(kd == 7))
                     for kd in range(8)][-1],
                    deps=ldeps, inc=env.s_pe)
                m1 = sch.op("vector", lambda ve: ve.reduce_max(
                    out=cwrow[:, 0:1], in_=ps_lg[:], axis=AXX),
                    deps=[lg_tok], inc=env.s_dv)
                m1n = sch.op("vector", lambda ve: ve.tensor_scalar_mul(
                    out=cwrow[:, 0:1], in0=cwrow[:, 0:1], scalar1=-1.0),
                    deps=[m1], inc=env.s_dv)
                st[f'm1red{tt}'] = m1n
                exdeps = [m1n]
                if tt >= 1:
                    exdeps.append(st[f'cwev{tt-1}'])
                elif chq >= 1:
                    exdeps.append(prev[chq - 1]['cwev3'])
                ex = sch.op("scalar", lambda sc: sc.activation(
                    out=cwtile[:], in_=ps_lg[:], func=AF.Exp,
                    bias=cwrow[:, 0:1]), deps=exdeps, inc=env.s_ac)
                st[f'exp{tt}'] = ex
                k1 = sch.op("vector", lambda ve: ve.reduce_max(
                    out=cwrow[:, 1:2], in_=cwtile[:], axis=AXX),
                    deps=[ex], inc=env.s_dv)
                k2 = sch.op("vector", lambda ve: ve.tensor_scalar(
                    out=cwmask[:], in0=cwtile[:], scalar1=cwrow[:, 1:2],
                    scalar2=-4.0, op0=ALU.is_ge, op1=ALU.mult),
                    deps=[k1], inc=env.s_dv)
                k3 = sch.op("vector", lambda ve: ve.tensor_add(
                    out=cwmask[:], in0=cwmask[:], in1=cwtile[:]),
                    deps=[k2], inc=env.s_dv)
                k4 = sch.op("vector", lambda ve: ve.reduce_max(
                    out=cwrow[:, 2:3], in_=cwmask[:], axis=AXX),
                    deps=[k3], inc=env.s_dv)
                k5 = sch.op("vector", lambda ve: ve.tensor_tensor(
                    out=cwrow[:, 3:4], in0=cwrow[:, 1:2], in1=cwrow[:, 2:3],
                    op=ALU.add), deps=[k4], inc=env.s_dv)
                k6 = sch.op("vector", lambda ve: ve.reciprocal(
                    out=cwrow[:, 4:5], in_=cwrow[:, 3:4]),
                    deps=[k5], inc=env.s_dv)
                k7 = sch.op("vector", lambda ve: ve.tensor_scalar(
                    out=cwmask[:], in0=cwtile[:], scalar1=cwrow[:, 2:3],
                    scalar2=None, op0=ALU.is_ge), deps=[k6, k4], inc=env.s_dv)
                k8 = sch.op("vector", lambda ve: ve.tensor_mul(
                    out=cwtile[:], in0=cwtile[:], in1=cwmask[:]),
                    deps=[k7], inc=env.s_dv)
                k9 = sch.op("vector", lambda ve: ve.tensor_scalar_mul(
                    out=cwtile[:], in0=cwtile[:], scalar1=cwrow[:, 4:5]),
                    deps=[k8], inc=env.s_dv)
                tdeps = [k9, env.consts_tok]
                if tt >= 1:
                    tdeps.append(st[f'cwev{tt-1}'])
                elif chq >= 1:
                    tdeps.append(prev[chq - 1]['cwev3'])
                tr = sch.op("tensor", lambda te: te.matmul(
                    ps_ct[:], cwtile[:], env.ident[:],
                    start=True, stop=True), deps=tdeps, inc=env.s_pe)
                cwev = sch.op(
                    "vector", lambda ve, chq=chq, tt=tt:
                    ve.tensor_copy(out=env.cwT[:, chq * CH + tt * 128:
                                               chq * CH + (tt + 1) * 128],
                                   in_=ps_ct[:]), deps=[tr], inc=env.s_dv)
                st[f'cwev{tt}'] = cwev
            st['lg_last'] = lg_tok
            prev[chq] = st
        sch.drain(env.s_xd[0])
        sch.drain(env.s_xd[1])
        sch.run("p6")


def _moe_pass(env, th):
    nc = env.nc
    T, THL, NTCH = env.T, env.THL, env.NTCH
    WB = env.wsets
    final_acc = {}
    with nc.sbuf_tensor(f"acc_{th}", [128, 8 * THL], F32) as accT, \
         nc.sbuf_tensor(f"x1h_{th}", [128, 8 * THL], BF16) as x1h:
        with nc.sbuf_tensor(f"w1b_{th}", [128, 2 * 8 * 1024], BF16) as w1b, \
             nc.sbuf_tensor(f"w2b_{th}", [128, 8 * 1024], BF16) as w2b, \
             nc.sbuf_tensor(f"hsb_{th}", [128, 8 * THL], BF16) as hsb, \
             nc.sbuf_tensor(f"stgw0_{th}", [128, 1024], F32) as stgw0, \
             nc.sbuf_tensor(f"stgw1_{th}", [128, 1024], F32) as stgw1, \
             nc.sbuf_tensor(f"cwb_{th}", [128, 2 * NTCH * CH], BF16) as cwb, \
             nc.psum_tensor(f"ph0_{th}", [128, CH], F32) as ph0, \
             nc.psum_tensor(f"ph1_{th}", [128, CH], F32) as ph1, \
             nc.psum_tensor(f"py0_{th}", [128, CH], F32) as py0, \
             nc.psum_tensor(f"py1_{th}", [128, CH], F32) as py1, \
             nc.psum_tensor(f"pcw0_{th}", [128, CH], F32) as pcw0, \
             nc.psum_tensor(f"pcw1_{th}", [128, CH], F32) as pcw1:
            stgw = [stgw0, stgw1]
            ps_h = [ph0, ph1]
            ps_y = [py0, py1]
            pcw = [pcw0, pcw1]
            sch = Sched(nc)
            x1toks = []
            for m in range(8):
                x1toks.append(sch.op(
                    "sync", lambda sy, m=m, th=th:
                    sy.dma_start(out=x1h[:, m * THL:(m + 1) * THL],
                                 in_=env.x1b[m * 128:(m + 1) * 128,
                                             th * THL:(th + 1) * THL]),
                    inc=env.s_h1, inck=16))
            x1last = x1toks[-1]
            acc_clear = []
            for m in range(8):
                acc_clear.append(sch.op(
                    "gpsimd", lambda gp, m=m:
                    gp.memset(accT[:, m * THL:(m + 1) * THL], 0.0),
                    inc=env.s_pl))

            cast_tok = {}
            step_hgrp = {}
            step_silu = {}
            step_scale = {}
            step_ygrp = {}
            step_acc = {}
            cwb_ev = {}
            dma_i = 0
            cast_list = []
            for s in range(16):
                e, fh = s // 2, s % 2
                for widx in range(16):
                    isw1 = widx < 8
                    kk = widx % 8
                    if isw1:
                        src = env.w1[e, kk * 128:(kk + 1) * 128,
                                     fh * 1024:(fh + 1) * 1024]
                    else:
                        src = env.w2[e, fh * 1024 + kk * 128:
                                     fh * 1024 + (kk + 1) * 128, :]
                    deps = [cast_list[dma_i - 2]] if dma_i >= 2 else []
                    dt_ = sch.op("sync", lambda sy, src=src, dma_i=dma_i:
                                 sy.dma_start(out=stgw[dma_i % 2][:], in_=src),
                                 deps=deps, inc=env.s_stg[dma_i % 2], inck=16)
                    cdeps = [dt_]
                    if isw1:
                        if s >= 2:
                            cdeps.append(step_hgrp[(s - 2, 7, NTCH - 1)])
                        ws = (s % 2) * 8
                    else:
                        if s >= 1:
                            cdeps.append(step_ygrp[(s - 1, 7, NTCH - 1)])
                        ws = 0
                    dst = w1b if isw1 else w2b
                    ct = sch.op(
                        "vector", lambda ve, dst=dst, kk=kk, dma_i=dma_i, ws=ws:
                        ve.tensor_copy(
                            out=dst[:, (ws + kk) * 1024:(ws + kk + 1) * 1024],
                            in_=stgw[dma_i % 2][:]),
                        deps=cdeps, inc=env.s_dv)
                    cast_tok[(s, 'w1' if isw1 else 'w2', kk)] = ct
                    cast_list.append(ct)
                    dma_i += 1
                if fh == 0:
                    for tch in range(NTCH):
                        deps = [env.consts_tok]
                        if tch >= 2:
                            deps.append(cwb_ev[(e, tch - 2)])
                        if e >= 2:
                            deps += [step_scale[(2 * e - 3, fm, tch)]
                                     for fm in range(8)]
                            deps.append(cwb_ev[(e - 2, NTCH - 1)])
                        bc = sch.op(
                            "tensor", lambda te, e=e, tch=tch, th=th:
                            te.matmul(pcw[tch % 2][:],
                                      env.sel8e[:, e * 128:(e + 1) * 128],
                                      env.cwT[0:8, th * THL + tch * CH:
                                              th * THL + (tch + 1) * CH],
                                      start=True, stop=True),
                            deps=deps, inc=env.s_pe)
                        cwb_ev[(e, tch)] = sch.op(
                            "scalar", lambda sc, e=e, tch=tch:
                            sc.copy(out=cwb[:, ((e % 2) * NTCH + tch) * CH:
                                            ((e % 2) * NTCH + tch + 1) * CH],
                                    in_=pcw[tch % 2][:]),
                            deps=[bc], inc=env.s_ac)
                w1last = cast_tok[(s, 'w1', 7)]
                hidx = 0
                for fm in range(8):
                    for tch in range(NTCH):
                        deps = [w1last, x1last]
                        if hidx >= 2:
                            deps.append(step_silu[(s, (hidx - 2) // NTCH,
                                                  (hidx - 2) % NTCH)])
                        elif s >= 1:
                            php = 8 * NTCH - 2 + hidx
                            deps.append(step_silu[(s - 1, php // NTCH,
                                                  php % NTCH)])
                        ht = sch.op(
                            "tensor", lambda te, s=s, fm=fm, tch=tch,
                            hidx=hidx, WB=WB:
                            [te.matmul(
                                ps_h[hidx % 2][:],
                                env_w1b_slice(w1b, s, kd, fm, 2),
                                x1h[:, kd * THL + tch * CH:
                                    kd * THL + (tch + 1) * CH],
                                start=(kd == 0), stop=(kd == 7))
                             for kd in range(8)][-1],
                            deps=deps, inc=env.s_pe)
                        step_hgrp[(s, fm, tch)] = ht
                        sdeps = [ht]
                        if s >= 1:
                            sdeps.append(step_scale[(s - 1, fm, tch)])
                            sdeps.append(step_ygrp[(s - 1, 7, tch)])
                        stk = sch.op(
                            "scalar", lambda sc, fm=fm, tch=tch, hidx=hidx:
                            sc.activation(
                                out=hsb[:, fm * THL + tch * CH:
                                        fm * THL + (tch + 1) * CH],
                                in_=ps_h[hidx % 2][:], func=AF.Silu),
                            deps=sdeps, inc=env.s_ac)
                        step_silu[(s, fm, tch)] = stk
                        ptk = sch.op(
                            "gpsimd", lambda gp, fm=fm, tch=tch, e=e:
                            gp.tensor_mul(
                                out=hsb[:, fm * THL + tch * CH:
                                        fm * THL + (tch + 1) * CH],
                                in0=hsb[:, fm * THL + tch * CH:
                                        fm * THL + (tch + 1) * CH],
                                in1=cwb[:, ((e % 2) * NTCH + tch) * CH:
                                        ((e % 2) * NTCH + tch + 1) * CH]),
                            deps=[stk, cwb_ev[(e, tch)]], inc=env.s_pl)
                        step_scale[(s, fm, tch)] = ptk
                        hidx += 1
                w2last = cast_tok[(s, 'w2', 7)]
                yidx = 0
                for tch in range(NTCH):
                    scale_deps = [step_scale[(s, fm, tch)] for fm in range(8)]
                    for ym in range(8):
                        deps = [w2last] + scale_deps
                        if yidx >= 2:
                            deps.append(step_acc[(s, (yidx - 2) % 8,
                                                 (yidx - 2) // 8)])
                        elif s >= 1:
                            pyp = 8 * NTCH - 2 + yidx
                            deps.append(step_acc[(s - 1, pyp % 8, pyp // 8)])
                        yt = sch.op(
                            "tensor", lambda te, s=s, ym=ym, tch=tch,
                            yidx=yidx, WB=WB:
                            [te.matmul(
                                ps_y[yidx % 2][:],
                                env_w1b_slice(w2b, s, kf, ym, 1),
                                hsb[:, kf * THL + tch * CH:
                                    kf * THL + (tch + 1) * CH],
                                start=(kf == 0), stop=(kf == 7))
                             for kf in range(8)][-1],
                            deps=deps, inc=env.s_pe)
                        step_ygrp[(s, ym, tch)] = yt
                        adeps = [yt]
                        if s >= 1:
                            adeps.append(step_acc[(s - 1, ym, tch)])
                        else:
                            adeps.append(acc_clear[ym])
                        at = sch.op(
                            "vector", lambda ve, ym=ym, tch=tch, yidx=yidx:
                            ve.tensor_add(
                                out=accT[:, ym * THL + tch * CH:
                                         ym * THL + (tch + 1) * CH],
                                in0=accT[:, ym * THL + tch * CH:
                                         ym * THL + (tch + 1) * CH],
                                in1=ps_y[yidx % 2][:]),
                            deps=adeps, inc=env.s_dv)
                        step_acc[(s, ym, tch)] = at
                        yidx += 1
            for ym in range(8):
                for tch in range(NTCH):
                    final_acc[(ym, tch)] = step_acc[(15, ym, tch)]
            if getattr(env, 'dbg_moe', None) is not None:
                for m in range(8):
                    sch.op("gpsimd", lambda gp, m=m, th=th: gp.dma_start(
                        out=env.dbg_moe[m * 128:(m + 1) * 128,
                                        th * THL:(th + 1) * THL],
                        in_=accT[:, m * THL:(m + 1) * THL]),
                        deps=[step_acc[(15, m, tch)] for tch in range(NTCH)],
                        inc=env.s_dbg, inck=16)
            sch.drain(env.s_dbg)
            sch.run(f"moe{th}")

        _ln2(env, th, accT, final_acc)


def env_w1b_slice(wb, s, kd, fm, WB=1):
    base = ((s % WB) * 8 + kd) * 1024
    return wb[:, base + fm * 128: base + (fm + 1) * 128]


def _ln2(env, th, accT, final_acc):
    nc = env.nc
    THL, NTCH = env.THL, env.NTCH
    with nc.sbuf_tensor(f"xst2_{th}", [128, 16 * CH], F32) as xst2, \
         nc.sbuf_tensor(f"res2_{th}", [128, 8 * CH], F32) as res2, \
         nc.sbuf_tensor(f"rsq2_{th}", [128, 8 * CH], F32) as rsq2, \
         nc.sbuf_tensor(f"outf_{th}", [128, 8 * CH], F32) as outf, \
         nc.sbuf_tensor(f"ab2_{th}", [128, 2 * CH], F32) as ab2, \
         nc.sbuf_tensor(f"rows2_{th}", [1, 6 * CH], F32) as rows2, \
         nc.psum_tensor(f"ps2s_{th}", [1, CH], F32) as ps2s, \
         nc.psum_tensor(f"ps2q_{th}", [1, CH], F32) as ps2q, \
         nc.psum_tensor(f"ps2A_{th}", [128, CH], F32) as ps2A, \
         nc.psum_tensor(f"ps2B_{th}", [128, CH], F32) as ps2B:
        sch = Sched(nc)
        prev2 = {}
        for c in range(NTCH):
            st = {}
            chg = th * NTCH + c
            xtoks = []
            for m in range(8):
                deps = [final_acc[(m, c)]]
                if c >= 2:
                    deps.append(prev2[c - 2]['ra'][m])
                xtoks.append(sch.op(
                    "sync", lambda sy, m=m, c=c, chg=chg:
                    sy.dma_start(
                        out=xst2[:, ((c % 2) * 8 + m) * CH:
                                 ((c % 2) * 8 + m + 1) * CH],
                        in_=env.x1f[m * 128:(m + 1) * 128,
                                    chg * CH:(chg + 1) * CH]),
                    deps=deps, inc=env.s_xs[c % 2], inck=16))
            ra = [None] * 8
            sq = [None] * 8
            sums = [None] * 8
            ssqs = [None] * 8
            for m in range(8):
                rdeps = [xtoks[-1], final_acc[(m, c)]]
                if c >= 1:
                    rdeps.append(prev2[c - 1]['sum'][m])
                    rdeps.append(prev2[c - 1]['norm'][m])
                    if 'dbgres' in prev2[c - 1]:
                        rdeps.append(prev2[c - 1]['dbgres'][m])
                ra[m] = sch.op(
                    "vector", lambda ve, m=m, c=c:
                    ve.tensor_add(
                        out=res2[:, m * CH:(m + 1) * CH],
                        in0=accT[:, m * THL + c * CH: m * THL + (c + 1) * CH],
                        in1=xst2[:, ((c % 2) * 8 + m) * CH:
                                 ((c % 2) * 8 + m + 1) * CH]),
                    deps=rdeps, inc=env.s_dv)
                sdeps = [ra[m]]
                if c >= 1:
                    sdeps.append(prev2[c - 1]['ssq'][m])
                    sdeps.append(prev2[c - 1]['n1'][m])
                sq[m] = sch.op(
                    "scalar", lambda sc, m=m:
                    sc.activation(out=rsq2[:, m * CH:(m + 1) * CH],
                                  in_=res2[:, m * CH:(m + 1) * CH],
                                  func=AF.Square),
                    deps=sdeps, inc=env.s_ac)
            for m in range(8):
                sums[m] = sch.op(
                    "tensor", lambda te, m=m:
                    te.matmul(ps2s[:], env.ones_fc[:],
                              res2[:, m * CH:(m + 1) * CH],
                              start=(m == 0), stop=(m == 7)),
                    deps=[ra[m]] + ([prev2[c - 1]['rows0']]
                                    if (m == 0 and c >= 1) else []),
                    inc=env.s_pe)
                ssqs[m] = sch.op(
                    "tensor", lambda te, m=m:
                    te.matmul(ps2q[:], env.ones_fc[:],
                              rsq2[:, m * CH:(m + 1) * CH],
                              start=(m == 0), stop=(m == 7)),
                    deps=[sq[m]] + ([prev2[c - 1]['rows1']]
                                    if (m == 0 and c >= 1) else []),
                    inc=env.s_pe)
            c1 = sch.op("vector", lambda ve: ve.tensor_scalar_mul(
                out=rows2[0:1, 0:CH], in0=ps2s[:], scalar1=1.0 / D),
                deps=[sums[7]], inc=env.s_dv)
            c2 = sch.op("vector", lambda ve: ve.tensor_mul(
                out=rows2[0:1, CH:2 * CH], in0=rows2[0:1, 0:CH],
                in1=rows2[0:1, 0:CH]), deps=[c1], inc=env.s_dv)
            c3 = sch.op("vector", lambda ve: ve.tensor_scalar_mul(
                out=rows2[0:1, 4 * CH:5 * CH], in0=ps2q[:], scalar1=1.0 / D),
                deps=[ssqs[7]], inc=env.s_dv)
            c4 = sch.op("vector", lambda ve: ve.tensor_sub(
                out=rows2[0:1, CH:2 * CH], in0=rows2[0:1, 4 * CH:5 * CH],
                in1=rows2[0:1, CH:2 * CH]), deps=[c2, c3], inc=env.s_dv)
            c5 = sch.op("scalar", lambda sc: sc.activation(
                out=rows2[0:1, 5 * CH:6 * CH], in_=rows2[0:1, CH:2 * CH],
                func=AF.Sqrt, bias=env.eps_t[:]), deps=[c4], inc=env.s_ac)
            c6 = sch.op("vector", lambda ve: ve.reciprocal(
                out=rows2[0:1, 2 * CH:3 * CH], in_=rows2[0:1, 5 * CH:6 * CH]),
                deps=[c5], inc=env.s_dv)
            c7 = sch.op("vector", lambda ve: ve.tensor_mul(
                out=rows2[0:1, 3 * CH:4 * CH], in0=rows2[0:1, 0:CH],
                in1=rows2[0:1, 2 * CH:3 * CH]), deps=[c6], inc=env.s_dv)
            c8 = sch.op("vector", lambda ve: ve.tensor_scalar_mul(
                out=rows2[0:1, 3 * CH:4 * CH], in0=rows2[0:1, 3 * CH:4 * CH],
                scalar1=-1.0), deps=[c7], inc=env.s_dv)
            st['rows0'] = c1
            st['rows1'] = c3
            bdeps_extra = [prev2[c - 1]['abev']] if c >= 1 else []
            bA = sch.op("tensor", lambda te: te.matmul(
                ps2A[:], env.ones_f1[0:1, :], rows2[0:1, 2 * CH:3 * CH],
                start=True, stop=True), deps=[c6] + bdeps_extra, inc=env.s_pe)
            bB = sch.op("tensor", lambda te: te.matmul(
                ps2B[:], env.ones_f1[0:1, :], rows2[0:1, 3 * CH:4 * CH],
                start=True, stop=True), deps=[c8] + bdeps_extra, inc=env.s_pe)
            evdeps = [bA, bB]
            if c >= 1:
                evdeps.append(prev2[c - 1]['norm'][7])
                if 'dbgab' in prev2[c - 1]:
                    evdeps.append(prev2[c - 1]['dbgab'])
            abev = sch.op("vector", lambda ve: [
                ve.tensor_copy(out=ab2[:, 0:CH], in_=ps2A[:]),
                ve.tensor_copy(out=ab2[:, CH:2 * CH], in_=ps2B[:])][-1],
                deps=evdeps, inc=env.s_dv)
            st['abev'] = abev
            if getattr(env, 'dbg_ab', None) is not None:
                st['dbgab'] = sch.op(
                    "gpsimd", lambda gp, th=th, c=c: gp.dma_start(
                        out=env.dbg_ab[:, (th * env.NTCH + c) * 2 * CH:
                                       (th * env.NTCH + c + 1) * 2 * CH],
                        in_=ab2[:]), deps=[abev], inc=env.s_dbg, inck=16)
                st['dbgres'] = [None] * 8
                for m in range(8):
                    st['dbgres'][m] = sch.op(
                        "gpsimd", lambda gp, m=m, chg=chg: gp.dma_start(
                            out=env.dbg_res2[m * 128:(m + 1) * 128,
                                             chg * CH:(chg + 1) * CH],
                            in_=res2[:, m * CH:(m + 1) * CH]),
                        deps=[ra[m], abev], inc=env.s_dbg, inck=16)
            norm = [None] * 8
            odma = [None] * 8
            n1s = [None] * 8
            for m in range(8):
                n1 = sch.op("vector", lambda ve, m=m: ve.tensor_mul(
                    out=rsq2[:, m * CH:(m + 1) * CH],
                    in0=res2[:, m * CH:(m + 1) * CH], in1=ab2[:, 0:CH]),
                    deps=[abev, sq[m], sums[m], ssqs[m]], inc=env.s_dv)
                n1s[m] = n1
                ndeps = [n1]
                if c >= 1:
                    ndeps.append(prev2[c - 1]['odma'][7])
                n2 = sch.op("vector", lambda ve, m=m: ve.tensor_add(
                    out=outf[:, m * CH:(m + 1) * CH],
                    in0=rsq2[:, m * CH:(m + 1) * CH],
                    in1=ab2[:, CH:2 * CH]), deps=ndeps, inc=env.s_dv)
                norm[m] = n2
                odma[m] = sch.op(
                    "gpsimd", lambda gp, m=m, chg=chg:
                    gp.dma_start(out=env.outT[m * 128:(m + 1) * 128,
                                              chg * CH:(chg + 1) * CH],
                                 in_=outf[:, m * CH:(m + 1) * CH]),
                    deps=[n2], inc=env.s_od[c % 2], inck=16)
            st['norm'] = norm
            st['odma'] = odma
            st['ra'] = ra
            st['sum'] = sums
            st['ssq'] = ssqs
            st['n1'] = n1s
            prev2[c] = st
        sch.drain(env.s_od[0])
        sch.drain(env.s_od[1])
        sch.drain(env.s_xs[0])
        sch.drain(env.s_xs[1])
        sch.run(f"ln2_{th}")


def build(T=4096, debug=False, wsets=1):
    assert T % 1024 == 0
    env = Env()
    env.T = T
    env.wsets = wsets
    env.NCH = T // CH
    env.NKT = T // 128
    env.THL = T // 2
    env.NTCH = env.THL // CH

    nc = bass.Bass("TRN2", target_bir_lowering=False)
    env.nc = nc

    env.xT = nc.dram_tensor("xT", [D, T], F32, kind="ExternalInput").ap()
    env.wdkv = nc.dram_tensor("wdkv", [D, DL], F32, kind="ExternalInput").ap()
    env.wuk = nc.dram_tensor("wuk", [DL, D], F32, kind="ExternalInput").ap()
    env.wuv = nc.dram_tensor("wuv", [DL, D], F32, kind="ExternalInput").ap()
    env.wuq = nc.dram_tensor("wuq", [D, D], F32, kind="ExternalInput").ap()
    env.wo = nc.dram_tensor("wo", [D, D], F32, kind="ExternalInput").ap()
    env.rw = nc.dram_tensor("rw", [D, E], F32, kind="ExternalInput").ap()
    env.w1 = nc.dram_tensor("w1", [E, D, FF], F32, kind="ExternalInput").ap()
    env.w2 = nc.dram_tensor("w2", [E, FF, D], F32, kind="ExternalInput").ap()
    env.c_ident = nc.dram_tensor("c_ident", [128, 128], F32,
                                 kind="ExternalInput").ap()
    env.c_sel16 = nc.dram_tensor("c_sel16", [128, 128], BF16,
                                 kind="ExternalInput").ap()
    env.c_sel8h = nc.dram_tensor("c_sel8h", [128, 8], BF16,
                                 kind="ExternalInput").ap()
    env.c_sel8e = nc.dram_tensor("c_sel8e", [8, 1024], BF16,
                                 kind="ExternalInput").ap()

    env.outT = nc.dram_tensor("outT", [D, T], F32, kind="ExternalOutput").ap()
    if debug:
        env.x1f = nc.dram_tensor("x1f", [D, T], F32, kind="ExternalOutput").ap()
        env.x1b = nc.dram_tensor("x1b", [D, T], BF16, kind="ExternalOutput").ap()
    else:
        env.x1f = nc.dram_tensor("x1f", [D, T], F32).ap()
        env.x1b = nc.dram_tensor("x1b", [D, T], BF16).ap()
    if debug:
        env.dbg_ckv = nc.dram_tensor("dbg_ckv", [DL, T], BF16,
                                     kind="ExternalOutput").ap()
        env.dbg_q = nc.dram_tensor("dbg_q", [D, T], BF16,
                                   kind="ExternalOutput").ap()
        env.dbg_cw = nc.dram_tensor("dbg_cw", [8, T], BF16,
                                    kind="ExternalOutput").ap()
        env.dbg_moe = nc.dram_tensor("dbg_moe", [D, T], F32,
                                     kind="ExternalOutput").ap()
        env.dbg_res2 = nc.dram_tensor("dbg_res2", [D, T], F32,
                                      kind="ExternalOutput").ap()
        env.dbg_ab = nc.dram_tensor("dbg_ab", [128, 16 * CH], F32,
                                    kind="ExternalOutput").ap()

    ctx = ExitStack()
    EC = ctx.enter_context

    env.s_pe = Ctr(EC(nc.semaphore("s_pe")))
    env.s_ac = Ctr(EC(nc.semaphore("s_ac")))
    env.s_dv = Ctr(EC(nc.semaphore("s_dv")))
    env.s_pl = Ctr(EC(nc.semaphore("s_pl")))
    env.s_cst = Ctr(EC(nc.semaphore("s_cst")))
    env.s_stg = [Ctr(EC(nc.semaphore(f"s_stg{i}"))) for i in range(2)]
    env.s_xstg = [Ctr(EC(nc.semaphore(f"s_xstg{i}"))) for i in range(2)]
    env.s_cx = [Ctr(EC(nc.semaphore(f"s_cx{i}"))) for i in range(2)]
    env.s_xs = [Ctr(EC(nc.semaphore(f"s_xs{i}"))) for i in range(2)]
    env.s_xd = [Ctr(EC(nc.semaphore(f"s_xd{i}"))) for i in range(2)]
    env.s_h1 = Ctr(EC(nc.semaphore("s_h1")))
    env.s_od = [Ctr(EC(nc.semaphore(f"s_od{i}"))) for i in range(2)]
    env.s_cd = [Ctr(EC(nc.semaphore(f"s_cd{i}"))) for i in range(2)]
    env.s_msc = Ctr(EC(nc.semaphore("s_msc")))
    env.s_dbg = Ctr(EC(nc.semaphore("s_dbg")))
    env.all_dma_ctrs = ([env.s_cst, env.s_h1, env.s_msc, env.s_dbg]
                        + env.s_stg + env.s_xstg + env.s_cx + env.s_xs
                        + env.s_xd + env.s_od + env.s_cd)

    env.ones_f1 = EC(nc.sbuf_tensor("ones_f1", [1, 128], F32))
    env.ones_fc = EC(nc.sbuf_tensor("ones_fc", [128, 1], F32))
    env.eps_t = EC(nc.sbuf_tensor("eps_t", [1, 1], F32))
    env.ident = EC(nc.sbuf_tensor("ident", [128, 128], F32))
    env.sel16 = EC(nc.sbuf_tensor("sel16", [128, 128], BF16))
    env.sel8h = EC(nc.sbuf_tensor("sel8h", [128, 8], BF16))
    env.sel8e = EC(nc.sbuf_tensor("sel8e", [8, 1024], BF16))
    env.qmax16 = EC(nc.sbuf_tensor("qmax16", [16, 1], F32))
    env.qmaxG = EC(nc.sbuf_tensor("qmaxG", [4, 1], F32))
    env.nrm_tmp = EC(nc.sbuf_tensor("nrm_tmp", [16, 1], F32))
    env.cwT = EC(nc.sbuf_tensor("cwT", [8, T], BF16))
    env.ctxh = nc.dram_tensor("ctxh", [D, T], BF16,
                              kind="ExternalOutput" if debug else "Internal").ap()

    es_w = ExitStack()
    wuk_bf = es_w.enter_context(nc.sbuf_tensor("wuk_bf", [128, 2 * D], BF16))
    wuv_bf = es_w.enter_context(nc.sbuf_tensor("wuv_bf", [128, 2 * D], BF16))
    es_ckq = ExitStack()
    env.c_kvT = [es_ckq.enter_context(nc.sbuf_tensor(f"ckv{i}", [128, T], BF16))
                 for i in range(2)]
    env.qT = [es_ckq.enter_context(nc.sbuf_tensor(f"qT{i}", [128, T], BF16))
              for i in range(8)]

    es_x = ExitStack()
    wdkv_bf = es_x.enter_context(nc.sbuf_tensor("wdkv_bf", [128, 8 * DL], BF16))
    wuq_bf = es_x.enter_context(nc.sbuf_tensor("wuq_bf", [128, 8 * D], BF16))
    xT_bf = es_x.enter_context(nc.sbuf_tensor("xT_bf", [128, 8 * env.T], BF16))

    _p0(env, wdkv_bf, wuk_bf, wuv_bf, wuq_bf, xT_bf)

    groups_ckv = [(m, ch) for m in range(2) for ch in range(env.NCH)]
    _proj_phase(
        env, "ckv", groups_ckv, 8,
        lambda g, k: wdkv_bf[:, k * DL + g[0] * 128: k * DL + (g[0] + 1) * 128],
        lambda g, k: xT_bf[:, k * T + g[1] * CH: k * T + (g[1] + 1) * CH],
        lambda ve, g, ps: ve.tensor_copy(
            out=env.c_kvT[g[0]][:, g[1] * CH:(g[1] + 1) * CH], in_=ps[:]),
    )
    groups_q = [(m, ch) for m in range(8) for ch in range(env.NCH)]
    _proj_phase(
        env, "q", groups_q, 8,
        lambda g, k: wuq_bf[:, k * D + g[0] * 128: k * D + (g[0] + 1) * 128],
        lambda g, k: xT_bf[:, k * T + g[1] * CH: k * T + (g[1] + 1) * CH],
        lambda ve, g, ps: ve.tensor_copy(
            out=env.qT[g[0]][:, g[1] * CH:(g[1] + 1) * CH], in_=ps[:]),
    )
    _qnorms(env)

    if debug:
        sch = Sched(nc)
        for m in range(2):
            sch.op("gpsimd", lambda gp, m=m: gp.dma_start(
                out=env.dbg_ckv[m * 128:(m + 1) * 128, :],
                in_=env.c_kvT[m][:]), inc=env.s_dbg, inck=16)
        for m in range(8):
            sch.op("gpsimd", lambda gp, m=m: gp.dma_start(
                out=env.dbg_q[m * 128:(m + 1) * 128, :], in_=env.qT[m][:]),
                inc=env.s_dbg, inck=16)
        sch.drain(env.s_dbg)
        sch.run("dbgq")

    es_x.close()

    for g in range(NH // HG):
        _attn_group(env, g, wuk_bf, wuv_bf)

    es_ckq.close()
    es_w.close()

    es_p6 = ExitStack()
    wo_bf = es_p6.enter_context(nc.sbuf_tensor("wo_bf", [128, 8 * D], BF16))
    rw_bf = es_p6.enter_context(nc.sbuf_tensor("rw_bf", [128, 8 * E], BF16))
    with nc.sbuf_tensor("wstg0b", [128, D], F32) as wstg0b, \
         nc.sbuf_tensor("wstg1b", [128, D], F32) as wstg1b:
        wp2 = []
        for kd in range(8):
            wp2.append((wo_bf, kd * D, D, env.wo[kd * 128:(kd + 1) * 128, :]))
        for kd in range(8):
            wp2.append((rw_bf, kd * E, E, env.rw[kd * 128:(kd + 1) * 128, :]))
        _stream_pieces(env, wp2, [wstg0b, wstg1b])

    _p6(env, wo_bf, rw_bf)
    es_p6.close()

    for th in range(2):
        _moe_pass(env, th)

    if debug:
        sch = Sched(nc)
        sch.op("gpsimd", lambda gp: gp.dma_start(out=env.dbg_cw[:],
                                                 in_=env.cwT[:]),
               inc=env.s_dbg, inck=16)
        sch.drain(env.s_dbg)
        sch.run("dbgcw")

    with nc.Block("fin") as blk:
        def fin(gp):
            for ctr in env.all_dma_ctrs:
                if ctr.n > 0:
                    gp.wait_ge(ctr.h, ctr.n)
        blk.gpsimd(fin)

    ctx.close()
    return nc


def _consts():
    ident = np.eye(128, dtype=np.float32)
    sel16 = np.zeros((128, 128), dtype=ml_dtypes.bfloat16)
    for kd in range(8):
        sel16[0:64, 16 * kd + 2 * kd] = 1
        sel16[64:128, 16 * kd + 2 * kd + 1] = 1
    sel8h = np.zeros((128, 8), dtype=ml_dtypes.bfloat16)
    for p in range(2):
        sel8h[0:64, 4 * p + 2 * p] = 1
        sel8h[64:128, 4 * p + 2 * p + 1] = 1
    sel8e = np.zeros((8, 1024), dtype=ml_dtypes.bfloat16)
    for e in range(8):
        sel8e[e, e * 128:(e + 1) * 128] = 1
    return {"c_ident": ident, "c_sel16": sel16, "c_sel8h": sel8h,
            "c_sel8e": sel8e}


def kernel(L0_attn_x, L0_attn_W_DKV, L0_attn_W_UK, L0_attn_W_UV, L0_attn_W_UQ,
           L0_attn_W_O, L0_ffn_router_w, L0_ffn_w1, L0_ffn_w2):
    x = np.ascontiguousarray(np.asarray(L0_attn_x, dtype=np.float32))
    assert x.shape == (8, 4096, 1024)
    shared = {
        "wdkv": np.ascontiguousarray(np.asarray(L0_attn_W_DKV, np.float32)),
        "wuk": np.ascontiguousarray(np.asarray(L0_attn_W_UK, np.float32)),
        "wuv": np.ascontiguousarray(np.asarray(L0_attn_W_UV, np.float32)),
        "wuq": np.ascontiguousarray(np.asarray(L0_attn_W_UQ, np.float32)),
        "wo": np.ascontiguousarray(np.asarray(L0_attn_W_O, np.float32)),
        "rw": np.ascontiguousarray(np.asarray(L0_ffn_router_w, np.float32)),
        "w1": np.ascontiguousarray(np.asarray(L0_ffn_w1, np.float32)),
        "w2": np.ascontiguousarray(np.asarray(L0_ffn_w2, np.float32)),
    }
    shared.update(_consts())
    nc = build(T=4096, debug=False)
    in_maps = []
    for b in range(8):
        m = dict(shared)
        m["xT"] = np.ascontiguousarray(x[b].T)
        in_maps.append(m)
    res = run_bass_kernel_spmd(nc, in_maps, core_ids=list(range(8)))
    out = np.stack([np.asarray(r["outT"], dtype=np.float32).T
                    for r in res.results], axis=0)
    return out
